# version 10
# baseline (speedup 1.0000x reference)
import numpy as np
import ml_dtypes
import concourse.bass as bass
import concourse.mybir as mybir
from concourse.bass_utils import run_bass_kernel_spmd

F32 = mybir.dt.float32
BF16 = mybir.dt.bfloat16
FP8 = mybir.dt.float8e4
AF = mybir.ActivationFunctionType
ALU = mybir.AluOpType
BF16NP = ml_dtypes.bfloat16
FP8NP = ml_dtypes.float8_e4m3

B, T, D, NH, DH = 4, 4096, 1024, 16, 64
DBG = set()
NCORES = 8
MASKVAL = -30000.0
BSAFE = 1.03


def shapes(T_):
    NT = T_ // 2
    NTT = NT // 128
    NQT = max(1, NT // 512)
    QT = NT // NQT
    NKC = T_ // 128
    KCS = NKC // 2
    HP = NH // 2
    FC = D // 128
    return NT, NTT, NQT, QT, NKC, KCS, HP, FC


def causal_tables(T_):
    NT, NTT, NQT, QT, NKC, KCS, HP, FC = shapes(T_)

    def classify(h, j, g):
        pos_k = NT * (g // KCS) + 128 * (g % KCS)
        d = (NT * h + QT * j) - pos_k
        if d >= 127:
            return ("none", 0)
        if d + (QT - 1) < 0:
            return ("all", 0)
        return ("part", d)

    kc_lists, bias_idx, mask_idx = [], {}, {}
    bias_cls, mask_cls = {}, {}
    for j in range(NQT):
        lst = []
        for g in range(NKC):
            cl = [classify(h, j, g) for h in range(2)]
            if cl[0][0] == "all" and cl[1][0] == "all":
                continue
            lst.append(g)
            bkey = (cl[0][0] == "all", cl[1][0] == "all")
            bias_idx[(j, g)] = bias_cls.setdefault(bkey, len(bias_cls))
            if "part" in (cl[0][0], cl[1][0]):
                mkey = (cl[0][1] if cl[0][0] == "part" else None,
                        cl[1][1] if cl[1][0] == "part" else None)
                mask_idx[(j, g)] = mask_cls.setdefault(mkey, len(mask_cls))
            else:
                mask_idx[(j, g)] = None
        kc_lists.append(lst)
    n_bias = max(1, len(bias_cls))
    n_mask = max(1, len(mask_cls))
    bias_vals = [[0.0] * n_bias for _ in range(2)]
    for (a0, a1), i in bias_cls.items():
        bias_vals[0][i] = MASKVAL if a0 else 0.0
        bias_vals[1][i] = MASKVAL if a1 else 0.0
    mask_keys = [None] * n_mask
    for k, i in mask_cls.items():
        mask_keys[i] = k
    return kc_lists, bias_idx, mask_idx, n_bias, n_mask, bias_vals, mask_keys


def host_masktab(h, mask_keys, QT):
    n = len(mask_keys)
    out = np.zeros((128, n * QT), dtype=np.float32)
    for i, key in enumerate(mask_keys):
        if key is None:
            continue
        d = key[h]
        if d is None:
            continue
        p = np.arange(128)[:, None]
        c = np.arange(QT)[None, :]
        out[:, i * QT:(i + 1) * QT] = np.where(p <= d + c, 0.0, MASKVAL)
    return np.ascontiguousarray(out.astype(BF16NP))


def host_rope():
    HPn = NH // 2
    inv = (np.float32(1.0) / (np.float32(10000.0) **
           (np.arange(0, DH, 2, dtype=np.float32) / np.float32(DH))))
    t = np.arange(2048, dtype=np.float32)
    fr = (t[:, None] * inv[None, :]).astype(np.float32)
    cos = np.cos(fr).astype(np.float32)[:NH]
    sin = np.sin(fr).astype(np.float32)[:NH]
    cosv = np.zeros((128, HPn), np.float32)
    sinv2 = np.zeros((128, HPn), np.float32)
    for hp in range(HPn):
        for p in range(128):
            head = 2 * hp + p // 64
            r = p % 64
            cosv[p, hp] = cos[head][r % 32]
            sinv2[p, hp] = -sin[head][r] if r < 32 else sin[head][r - 32]
    return cosv, sinv2


ENGS = ("sync", "tensor", "scalar", "vector", "gpsimd")
ESEM = {"tensor": "PE", "scalar": "ACT", "vector": "DVE", "gpsimd": "GP"}


class Em:
    def __init__(self, nc):
        self.nc = nc
        self.ops = {e: [] for e in ENGS}
        self.sems = {}
        self.cnt = {}
        for name in ("LD", "SW", "PE", "ACT", "DVE", "GP", "CC",
                     "LW0", "LW1", "LQ", "LK", "LA0", "LA1", "LA2", "LA3"):
            self.sems[name] = nc._es.enter_context(nc.semaphore(name))
            self.cnt[name] = 0

    def wait(self, eng, sem, val):
        if val is None or val <= 0:
            return
        s = self.sems[sem]
        self.ops[eng].append(lambda e: e.wait_ge(s, val))

    def emit(self, eng, fn, inc="auto", amt=1):
        if inc == "auto":
            inc = ESEM.get(eng)
        if inc is None:
            self.ops[eng].append(fn)
            return self.cnt.get(ESEM.get(eng, "PE"), 0)
        s = self.sems[inc]
        self.cnt[inc] += amt

        def run(e, fn=fn, s=s, amt=amt):
            fn(e).then_inc(s, amt)
        self.ops[eng].append(run)
        return self.cnt[inc]

    def mm(self, fn, inc=False):
        return self.emit("tensor", fn, inc="PE" if inc else None)

    def dma(self, eng, out, in_, sem=None):
        sem = sem or ("SW" if eng in ("gpsimd", "scalar") else "LD")
        return self.emit(eng, lambda e: e.dma_start(out=out, in_=in_),
                         inc=sem, amt=16)

    def barrier(self):
        marks = dict(self.cnt)
        for eng in ENGS:
            for sem in marks:
                self.wait(eng, sem, marks[sem])

    def flush(self, block):
        for eng in ENGS:
            ops = self.ops[eng]
            getattr(block, eng)(lambda e, ops=ops: [f(e) for f in ops])


def build_program(T_, n_sub=4, prefix=None):
    NT, NTT, NQT, QT, NKC, KCS, HP, FC = shapes(T_)
    kc_lists, bias_idx, mask_idx, NB, NM, _, _ = causal_tables(T_)
    SUBT = QT // 128
    NOG = D // QT

    import contextlib
    nc = bass.Bass("TRN2", target_bir_lowering=False, debug=False)
    nc._es = contextlib.ExitStack()

    def din(name, shp, dt=F32):
        return nc.dram_tensor(name, shp, dt, kind="ExternalInput").ap()

    x_in = din("x_in", [NT, D])
    w_qkv = {"a": din("w_qkv_a", [D, 3 * D]), "b": din("w_qkv_b", [D, 3 * D])}
    b_qkv = {"a": din("b_qkv_a", [128, 3 * D // 128]),
             "b": din("b_qkv_b", [128, 3 * D // 128])}
    w_o = {"a": din("w_proj_a", [D, D]), "b": din("w_out_b", [D, D])}
    b_o = {"a": din("b_proj_a", [D]), "b": din("b_out_b", [D])}
    w_ffn_t = din("w_ffn_t", [D, D])
    b_ffn = din("b_ffn", [D])
    s_in = din("s_in", [1])
    s_w = din("s_w", [1])
    ident_in = din("ident", [128, 128], BF16)
    ident8_in = din("ident8", [128, 128], FP8)
    otop_in = din("otop", [128, 128], BF16)
    obot_in = din("obot", [128, 128], BF16)
    btab_in = din("btab", [128, NB])
    mtab_in = din("mtab", [128, NM * QT], BF16)
    perm_in = din("permsw", [128, 128])
    rcos_in = din("rcos", [128, HP])
    rsin_in = din("rsin", [128, HP])
    y_out = nc.dram_tensor("y", [NT, D], F32, kind="ExternalOutput").ap()

    def dsc(name, shp, dt=BF16):
        return nc.dram_tensor(name, shp, dt, kind="Internal").ap()

    q_sc = dsc("q_sc", [2, D, NT])
    kt_own = dsc("kt_own", [2, D, NT])
    v_own = dsc("v_own", [2, HP, 128, KCS * 130])
    kmax_own = dsc("kmax_own", [128, HP], F32)
    kt_all = dsc("kt_all", [2, 2, D, NT])
    v_all = dsc("v_all", [2, 2, HP, 128, KCS * 130])
    kmax_all = dsc("kmax_all", [2, 128, HP], F32)
    a_sc = dsc("a_sc", [2, D, NT])

    sbuf = nc.alloc_sbuf_tensor
    rsb = sbuf("rsb", [128, NTT * D], F32)
    AR_COLS = max(2 * FC * NT, 2 * T_ + 2 * (NKC * 130), 2 * FC * D)
    arena = sbuf("arena", [128, AR_COLS], BF16)
    ar = arena.ap()

    def xT(hl):
        return ar[:, hl * FC * NT:(hl + 1) * FC * NT]

    def kbuf(hl):
        return ar[:, hl * T_:(hl + 1) * T_]

    def vbuf(hl):
        base = 2 * T_ + hl * (NKC * 130)
        return ar[:, base:base + NKC * 130]

    def wp(hl):
        return ar[:, hl * FC * D:(hl + 1) * FC * D]

    def xqT():
        return ar[:, 0:FC * NT]

    xd = sbuf("xd", [128, D], F32)
    xn = sbuf("xn", [128, D], F32)
    qe = sbuf("qe", [128, D], F32)
    stat = sbuf("stat", [128, 8], F32)
    wring_f = sbuf("wring_f", [128, 2 * D], F32)
    wring_h = sbuf("wring_h", [128, 2 * D], BF16)
    wring_l = sbuf("wring_l", [128, 2 * D], BF16)
    qtile_h = sbuf("qtile_h", [128, 2 * QT], BF16)
    qtile_l = sbuf("qtile_l", [128, 2 * QT], BF16)
    pf = sbuf("pf", [128, max(2 * QT, D)], F32)
    ph = sbuf("ph", [128, 2 * QT], BF16)
    pl = sbuf("pl", [128, 2 * QT], BF16)
    msk = sbuf("msk", [128, NM * QT], BF16)
    astg_h = sbuf("astg_h", [128, QT], BF16)
    astg_l = sbuf("astg_l", [128, QT], BF16)
    vstg = sbuf("vstg", [128, 2 * QT], BF16)
    wfT = sbuf("wfT", [128, FC * D], BF16)
    ident = sbuf("idents", [128, 128], BF16)
    ident8 = sbuf("ident8s", [128, 128], FP8)
    otop = sbuf("otops", [128, 128], BF16)
    obot = sbuf("obots", [128, 128], BF16)
    btab = sbuf("btabs", [128, NB], F32)
    nbm = sbuf("nbm", [128, 2 * NB], F32)
    negb = sbuf("negbs", [128, 2], F32)
    qmax = sbuf("qmax", [128, HP], F32)
    kmaxs = sbuf("kmaxs", [128, HP], F32)
    kmtmp = sbuf("kmtmp", [128, 2 * HP], F32)
    dmsq = sbuf("dmsq", [128, 2], BF16)
    permS = sbuf("permS", [128, 128], F32)
    rcos = sbuf("rcoss", [128, HP], F32)
    rsin = sbuf("rsins", [128, HP], F32)
    recip = sbuf("recips", [1, 2 * QT], F32)
    ones64 = sbuf("ones64", [1, 64], F32)
    scvec = sbuf("scvec", [128, 4], F32)
    bqv = sbuf("bqv", [128, 3 * D // 128], F32)
    afA = sbuf("afA", [128, QT], F32)
    qsw_t = None

    pt = nc.alloc_psum_tensor
    ps_s = [[pt(f"ps_s{hh}{p}", [128, QT], F32) for p in range(2)]
            for hh in range(2)]
    ps_av = [pt(f"ps_av{hh}", [65, QT], F32) for hh in range(2)]
    ps_rep = pt("ps_rep", [64, QT], F32)
    ps_tr = pt("ps_tr", [128, 1024], BF16)

    em = Em(nc)

    for t in range(NTT):
        em.dma("sync", rsb.ap()[:, D * t:D * (t + 1)],
               x_in[128 * t:128 * (t + 1), :])
    for dst, src in ((ident, ident_in), (ident8, ident8_in), (otop, otop_in),
                     (obot, obot_in), (btab, btab_in), (msk, mtab_in),
                     (rcos, rcos_in), (rsin, rsin_in), (permS, perm_in)):
        em.dma("sync", dst.ap(), src)
    em.dma("sync", scvec.ap()[:, 0:1],
           s_in.unsqueeze(0).broadcast_to([128, 1]))
    em.dma("sync", scvec.ap()[:, 1:2],
           s_w.unsqueeze(0).broadcast_to([128, 1]))
    for ic in range(FC):
        em.wait("sync", "DVE", em.cnt["DVE"])
        ld = em.dma("sync", qe.ap(), w_ffn_t[128 * ic:128 * (ic + 1), :])
        em.wait("vector", "LD", ld)
        em.emit("vector", lambda e, ic=ic: e.tensor_copy(
            out=wfT.ap()[:, D * ic:D * (ic + 1)], in_=qe.ap()))
    ld0 = em.cnt["LD"]
    em.wait("vector", "LD", ld0)
    tp = em.emit("vector", lambda e: e.memset(ar[:, 0:KCS * 130], 0.0))
    em.wait("vector", "DVE", tp)
    tp = em.emit("vector", lambda e: e.memset(
        ar[:, 0:KCS * 130].rearrange("p (g h c) -> p g h c", h=2, c=65)[
            :, :, :, 64:65], 1.0))
    tp0 = em.emit("vector", lambda e: e.memset(
        ar[:, KCS * 130:2 * KCS * 130], 0.0))
    em.wait("sync", "DVE", tp0)
    for hp_ in range(HP):
        em.dma("sync", v_own[0, hp_], ar[:, 0:KCS * 130])
        em.dma("sync", v_own[1, hp_], ar[:, KCS * 130:2 * KCS * 130])
    em.wait("vector", "LD", em.cnt["LD"])
    c = em.emit("vector", lambda e: e.reciprocal(
        out=scvec.ap()[:, 2:3], in_=scvec.ap()[:, 0:1]))
    em.wait("vector", "DVE", c)
    em.emit("vector", lambda e: e.tensor_mul(
        scvec.ap()[:, 3:4], scvec.ap()[:, 0:1], scvec.ap()[:, 1:2]))
    em.emit("vector", lambda e: e.memset(ones64.ap(), 1.0))
    em.barrier()

    def ln_phase(mode):
        for t in range(NTT):
            rt = rsb.ap()[:, D * t:D * (t + 1)]
            em.wait("vector", "ACT", em.cnt["ACT"])
            em.wait("vector", "GP", em.cnt["GP"])
            em.wait("vector", "PE", em.cnt["PE"])
            c0 = em.emit("vector", lambda e, rt=rt: e.reduce_sum(
                stat.ap()[:, 0:1], rt, axis=mybir.AxisListType.X))
            em.wait("vector", "DVE", c0)
            c0b = em.emit("vector", lambda e: e.tensor_scalar_mul(
                stat.ap()[:, 1:2], stat.ap()[:, 0:1], -1.0 / D))
            em.wait("vector", "DVE", c0b)
            c1 = em.emit("vector", lambda e, rt=rt: e.tensor_scalar_add(
                xd.ap(), rt, stat.ap()[:, 1:2]))
            em.wait("scalar", "DVE", c1)
            c2 = em.emit("scalar", lambda e: e.activation(
                out=qe.ap(), in_=xd.ap(), func=AF.Square))
            em.wait("vector", "ACT", c2)
            c3 = em.emit("vector", lambda e: e.reduce_sum(
                stat.ap()[:, 2:3], qe.ap(), axis=mybir.AxisListType.X))
            em.wait("vector", "DVE", c3)
            c3b = em.emit("vector", lambda e: e.tensor_scalar(
                out=stat.ap()[:, 5:6], in0=stat.ap()[:, 2:3],
                scalar1=1.0 / D, scalar2=1e-5, op0=ALU.mult, op1=ALU.add))
            em.wait("scalar", "DVE", c3b)
            c4 = em.emit("scalar", lambda e: e.activation(
                out=stat.ap()[:, 3:4], in_=stat.ap()[:, 5:6], func=AF.Sqrt))
            em.wait("vector", "ACT", c4)
            c4b = em.emit("vector", lambda e: e.reciprocal(
                out=stat.ap()[:, 6:7], in_=stat.ap()[:, 3:4]))
            em.wait("vector", "DVE", c4b)
            c5 = em.emit("vector", lambda e: e.tensor_scalar_mul(
                xn.ap(), xd.ap(), stat.ap()[:, 6:7]))
            if mode == "split":
                em.wait("gpsimd", "DVE", c5)
                hbuf = xd.ap().bitcast(BF16)[:, 0:D]
                lbuf = qe.ap().bitcast(BF16)[:, 0:D]
                g1 = em.emit("gpsimd", lambda e, hbuf=hbuf: e.tensor_copy(
                    out=hbuf, in_=xn.ap()))
                em.wait("vector", "GP", g1)
                c6 = em.emit("vector", lambda e, hbuf=hbuf, lbuf=lbuf:
                             e.tensor_tensor(out=lbuf, in0=xn.ap(), in1=hbuf,
                                             op=ALU.subtract))
                srcs = [(hbuf, g1, "GP", 0), (lbuf, c6, "DVE", 1)]
            else:
                em.wait("vector", "DVE", c5)
                c6a = em.emit("vector", lambda e: e.tensor_scalar(
                    out=xd.ap(), in0=xn.ap(), scalar1=scvec.ap()[:, 2:3],
                    scalar2=448.0, op0=ALU.mult, op1=ALU.min))
                em.wait("vector", "DVE", c6a)
                qbuf = qe.ap().bitcast(FP8)[:, 0:D]
                c6 = em.emit("vector", lambda e, qbuf=qbuf:
                             e.tensor_single_scalar(out=qbuf, in_=xd.ap(),
                                                    scalar=-448.0, op=ALU.max))
                em.wait("gpsimd", "DVE", c6)
                ubuf = xd.ap().bitcast(BF16)[:, 0:D]
                c7 = em.emit("gpsimd", lambda e, qbuf=qbuf, ubuf=ubuf:
                             e.tensor_copy(out=ubuf, in_=qbuf))
                srcs = [(ubuf, c7, "GP", 0)]
            for (src, cnt, sem, hl) in srcs:
                em.wait("tensor", sem, cnt)
                pst, idn = ps_tr.ap(), ident.ap()
                for f in range(FC):
                    em.mm(lambda e, src=src, f=f, pst=pst, idn=idn:
                          e.transpose(pst[:, 128 * f:128 * (f + 1)],
                                      src[:, 128 * f:128 * (f + 1)], idn),
                          inc=(f == FC - 1))
                pe_c = em.cnt["PE"]
                if mode == "split":
                    dst = xT(hl).rearrange("p (f n) -> p f n", f=FC)[
                        :, :, 128 * t:128 * (t + 1)]
                    eng = "scalar" if hl == 0 else "vector"
                else:
                    dst = xqT().rearrange("p (f n) -> p f n", f=FC)[
                        :, :, 128 * t:128 * (t + 1)]
                    eng = "scalar"
                em.wait(eng, "PE", pe_c)
                if eng == "scalar":
                    ev = em.emit(eng, lambda e, dst=dst, pst=pst: e.copy(
                        out=dst, in_=pst.rearrange("p (f n) -> p f n", f=FC)))
                else:
                    ev = em.emit(eng, lambda e, dst=dst, pst=pst: e.tensor_copy(
                        out=dst, in_=pst.rearrange("p (f n) -> p f n", f=FC)))
                em.wait("tensor", ESEM[eng], ev)

    def stream_w_oc(w_ap, oc, ring):
        src = w_ap[:, 128 * oc:128 * (oc + 1)].rearrange(
            "(f p) c -> p f c", p=128)
        dst = wring_f.ap()[:, D * ring:D * (ring + 1)].rearrange(
            "p (f c) -> p f c", f=FC)
        ld = em.dma("sync", dst, src, sem=f"LW{ring}")
        em.wait("gpsimd", f"LW{ring}", ld)
        g = em.emit("gpsimd", lambda e, ring=ring: e.tensor_copy(
            out=wring_h.ap()[:, D * ring:D * (ring + 1)],
            in_=wring_f.ap()[:, D * ring:D * (ring + 1)]))
        em.wait("vector", "GP", g)
        v = em.emit("vector", lambda e, ring=ring: e.tensor_tensor(
            out=wring_l.ap()[:, D * ring:D * (ring + 1)],
            in0=wring_f.ap()[:, D * ring:D * (ring + 1)],
            in1=wring_h.ap()[:, D * ring:D * (ring + 1)], op=ALU.subtract))
        return g, v

    def qkv_phase(bk, oc_sel=None):
        rope = bk == "b"
        NOC = 3 * D // 128
        ldb = em.dma("sync", bqv.ap()[:, 0:NOC], b_qkv[bk])
        em.wait("scalar", "LD", ldb)
        em.wait("vector", "LD", ldb)
        zz = em.emit("vector", lambda e: e.memset(qmax.ap(), 0.0))
        zz = em.emit("vector", lambda e: e.memset(kmaxs.ap(), 0.0))
        em.wait("vector", "DVE", zz)
        prev_mm = {}
        rope_gate = {0: 0, 1: 0}
        oc_list = list(range(NOC)) if oc_sel is None else list(oc_sel)
        for oc in oc_list:
            ring = oc % 2
            if oc - 2 in prev_mm:
                em.wait("gpsimd", "PE", prev_mm[oc - 2])
                em.wait("vector", "PE", prev_mm[oc - 2])
                em.wait("sync", "PE", prev_mm[oc - 2])
            g, v = stream_w_oc(w_qkv[bk], oc, ring)
            em.wait("tensor", "GP", g)
            em.wait("tensor", "DVE", v)
            is_v = oc >= 2 * FC
            if is_v and rope:
                em.wait("tensor", "DVE", max(rope_gate.values()))
            hp = oc % FC
            for tt in range(NT // QT):
                bank = ps_s[1 if is_v else 0][tt % 2]
                for f in range(FC):
                    sth = wring_h.ap()[:, D * ring + 128 * f:
                                       D * ring + 128 * (f + 1)]
                    stl = wring_l.ap()[:, D * ring + 128 * f:
                                       D * ring + 128 * (f + 1)]
                    mvh = xT(0)[:, NT * f + QT * tt:NT * f + QT * (tt + 1)]
                    mvl = xT(1)[:, NT * f + QT * tt:NT * f + QT * (tt + 1)]
                    em.mm(lambda e, b=bank, s=sth, m=mvh, f=f: e.matmul(
                        b.ap(), s, m, start=(f == 0), stop=False))
                    em.mm(lambda e, b=bank, s=sth, m=mvl: e.matmul(
                        b.ap(), s, m, start=False, stop=False))
                    em.mm(lambda e, b=bank, s=stl, m=mvh, f=f: e.matmul(
                        b.ap(), s, m, start=False, stop=(f == FC - 1)),
                        inc=(f == FC - 1))
                pe_mm = em.cnt["PE"]
                prev_mm[oc] = pe_mm
                em.wait("vector", "PE", pe_mm)
                em.wait("vector", "SW", em.cnt["SW"])
                a1 = em.emit("vector", lambda e, b=bank, oc=oc:
                             e.tensor_scalar_add(qe.ap()[:, 0:QT], b.ap(),
                                                 bqv.ap()[:, oc:oc + 1]))
                src_f32 = qe.ap()[:, 0:QT]
                dep_sem, dep_cnt = "DVE", a1
                if rope and not is_v and "norope" not in DBG:
                    em.wait("tensor", "DVE", a1)
                    rb = ps_s[1][tt % 2]
                    if rope_gate[tt % 2]:
                        em.wait("tensor", "DVE", rope_gate[tt % 2])
                    em.mm(lambda e, rb=rb: e.matmul(
                        rb.ap(), permS.ap(), qe.ap()[:, 0:QT],
                        start=True, stop=True), inc=True)
                    rmm = em.cnt["PE"]
                    em.wait("vector", "PE", rmm)
                    r0 = em.emit("vector", lambda e, rb=rb, hp=hp:
                                 e.tensor_scalar_mul(
                                     afA.ap(), rb.ap(),
                                     rsin.ap()[:, hp:hp + 1]))
                    rope_gate[tt % 2] = r0
                    em.wait("vector", "DVE", r0)
                    r1 = em.emit("vector", lambda e, hp=hp:
                                 e.scalar_tensor_tensor(
                                     out=xn.ap()[:, 0:QT],
                                     in0=qe.ap()[:, 0:QT],
                                     scalar=rcos.ap()[:, hp:hp + 1],
                                     in1=afA.ap(), op0=ALU.mult, op1=ALU.add))
                    src_f32 = xn.ap()[:, 0:QT]
                    dep_sem, dep_cnt = "DVE", r1
                em.wait("gpsimd", dep_sem, dep_cnt)
                if not is_v:
                    g2 = em.emit("gpsimd", lambda e, s=src_f32: e.tensor_copy(
                        out=astg_h.ap(), in_=s))
                    em.wait("vector", "GP", g2)
                    em.wait("vector", dep_sem, dep_cnt)
                    v2 = em.emit("vector", lambda e, s=src_f32:
                                 e.tensor_tensor(out=astg_l.ap(), in0=s,
                                                 in1=astg_h.ap(),
                                                 op=ALU.subtract))
                    if "nomax" not in DBG:
                        em.wait("vector", "DVE", v2)
                        m1 = em.emit("vector", lambda e, s=src_f32: e.reduce_max(
                            stat.ap()[:, 4:5], s, axis=mybir.AxisListType.X,
                            apply_absolute_value=True))
                        tgt = qmax if oc < FC else kmaxs
                        em.wait("vector", "DVE", m1)
                        m2 = em.emit("vector", lambda e, tgt=tgt, hp=hp:
                                     e.tensor_tensor(
                                         out=tgt.ap()[:, hp:hp + 1],
                                         in0=tgt.ap()[:, hp:hp + 1],
                                         in1=stat.ap()[:, 4:5], op=ALU.max))
                        em.wait("vector", "DVE", m2)
                    dst_t = q_sc if oc < FC else kt_own
                    r0c = 128 * hp
                    em.wait("scalar", "GP", g2)
                    em.wait("scalar", "DVE", v2)
                    if "nostore" not in DBG:
                        em.dma("scalar",
                               dst_t[0, r0c:r0c + 128, QT * tt:QT * (tt + 1)],
                               astg_h.ap())
                        em.dma("scalar",
                               dst_t[1, r0c:r0c + 128, QT * tt:QT * (tt + 1)],
                               astg_l.ap())
                else:
                    g2 = em.emit("gpsimd", lambda e, s=src_f32: e.tensor_copy(
                        out=astg_h.ap(), in_=s))
                    em.wait("vector", "GP", g2)
                    v2 = em.emit("vector", lambda e, s=src_f32:
                                 e.tensor_tensor(out=astg_l.ap(), in0=s,
                                                 in1=astg_h.ap(),
                                                 op=ALU.subtract))
                    if "novtr" in DBG:
                        continue
                    em.wait("tensor", "GP", g2)
                    em.wait("tensor", "DVE", v2)
                    em.wait("tensor", "ACT", em.cnt["ACT"])
                    for hl, srcb in ((0, astg_h.ap()), (1, astg_l.ap())):
                        for i in range(SUBT):
                            em.mm(lambda e, s=srcb, i=i, hl=hl: e.transpose(
                                ps_tr.ap()[:, 128 * (SUBT * hl + i):
                                           128 * (SUBT * hl + i + 1)],
                                s[:, 128 * i:128 * (i + 1)], ident.ap()),
                                inc=(i == SUBT - 1 and hl == 1))
                    pe_tr = em.cnt["PE"]
                    em.wait("scalar", "PE", pe_tr)
                    ev = em.emit("scalar", lambda e: e.copy(
                        out=vstg.ap()[:, 0:2 * QT],
                        in_=ps_tr.ap()[:, 0:2 * QT]))
                    em.wait("tensor", "ACT", ev)
                    em.wait("scalar", "ACT", ev)
                    hpv = oc - 2 * FC
                    for hl in range(2):
                        for hh in range(2):
                            src2 = vstg.ap()[:, QT * hl:QT * (hl + 1)]\
                                .rearrange("p (i c) -> p i c", i=SUBT)[
                                :, :, 64 * hh:64 * (hh + 1)]
                            dst2 = v_own[hl, hpv].rearrange(
                                "p (g c) -> p g c", c=130)[
                                :, SUBT * tt:SUBT * (tt + 1),
                                65 * hh:65 * hh + 64]
                            em.dma("scalar", dst2, src2)
                sw_now = em.cnt["SW"]
                em.wait("gpsimd", "SW", sw_now)
                em.wait("vector", "SW", sw_now)
                em.wait("scalar", "SW", sw_now)
        em.wait("scalar", "DVE", em.cnt["DVE"])
        em.dma("scalar", kmax_own, kmaxs.ap()[:, 0:HP])

    def collective_phase():
        em.wait("gpsimd", "SW", em.cnt["SW"])
        rg = [[0, 1], [2, 3], [4, 5], [6, 7]]
        for i_, o_ in ((kt_own, kt_all), (v_own, v_all), (kmax_own, kmax_all)):
            cc = em.emit("gpsimd", lambda e, i_=i_, o_=o_: e.collective_compute(
                "AllGather", ALU.bypass, replica_groups=rg,
                ins=[i_.tensor.ap().opt()], outs=[o_.tensor.ap().opt()]),
                inc="CC")
        em.wait("sync", "CC", cc)
        l1 = em.dma("sync", kmtmp.ap()[:, 0:HP], kmax_all[0])
        l1 = em.dma("sync", kmtmp.ap()[:, HP:2 * HP], kmax_all[1])
        em.wait("vector", "LD", l1)
        em.wait("vector", "CC", cc)
        c = em.emit("vector", lambda e: e.tensor_max(
            kmaxs.ap(), kmtmp.ap()[:, 0:HP], kmtmp.ap()[:, HP:2 * HP]))
        em.wait("vector", "DVE", c)
        c = em.emit("vector", lambda e: e.tensor_mul(
            qmax.ap(), qmax.ap(), qmax.ap()))
        em.wait("vector", "DVE", c)
        c = em.emit("vector", lambda e: e.tensor_mul(
            kmaxs.ap(), kmaxs.ap(), kmaxs.ap()))
        em.wait("vector", "DVE", c)

    def attn_phase(bk):
        causal = bk == "a"
        prev_av = 0
        for hp in range(HP):
            em.wait("vector", "ACT", em.cnt["ACT"])
            em.wait("vector", "PE", em.cnt["PE"])
            c = em.emit("vector", lambda e, hp=hp: e.tensor_copy(
                out=dmsq.ap()[:, 0:1], in_=qmax.ap()[:, hp:hp + 1]))
            c = em.emit("vector", lambda e, hp=hp: e.tensor_copy(
                out=dmsq.ap()[:, 1:2], in_=kmaxs.ap()[:, hp:hp + 1]))
            em.wait("tensor", "DVE", c)
            for hh in range(2):
                st = otop if hh == 0 else obot
                em.mm(lambda e, st=st, hh=hh: e.matmul(
                    ps_s[0][0].ap()[:, 2 * hh:2 * hh + 1], st.ap(),
                    dmsq.ap()[:, 0:1], start=True, stop=True))
                em.mm(lambda e, st=st, hh=hh: e.matmul(
                    ps_s[0][0].ap()[:, 2 * hh + 1:2 * hh + 2], st.ap(),
                    dmsq.ap()[:, 1:2], start=True, stop=True), inc=True)
            pe_b = em.cnt["PE"]
            em.wait("scalar", "PE", pe_b)
            aq = em.emit("scalar", lambda e: e.activation(
                out=stat.ap()[:, 4:8], in_=ps_s[0][0].ap()[:, 0:4],
                func=AF.Sqrt))
            em.wait("vector", "ACT", aq)
            nb = None
            for hh in range(2):
                c = em.emit("vector", lambda e, hh=hh: e.tensor_mul(
                    stat.ap()[:, 0:1], stat.ap()[:, 4 + 2 * hh:5 + 2 * hh],
                    stat.ap()[:, 5 + 2 * hh:6 + 2 * hh]))
                em.wait("vector", "DVE", c)
                nb = em.emit("vector", lambda e, hh=hh: e.tensor_scalar_mul(
                    negb.ap()[:, hh:hh + 1], stat.ap()[:, 0:1], -BSAFE / 8.0))
                em.wait("vector", "DVE", nb)
            if causal:
                for hh in range(2):
                    nb = em.emit("vector", lambda e, hh=hh: e.tensor_scalar_add(
                        nbm.ap()[:, hh * NB:(hh + 1) * NB], btab.ap(),
                        negb.ap()[:, hh:hh + 1]))
            em.wait("scalar", "DVE", nb)
            em.wait("tensor", "ACT", em.cnt["ACT"])
            em.wait("sync", "PE", prev_av)
            kw = 0
            for hl in range(2):
                for s in range(2):
                    kw = em.dma("sync", kbuf(hl)[:, NT * s:NT * (s + 1)],
                                kt_all[s, hl, 128 * hp:128 * (hp + 1), :],
                                sem="LK")
            em.wait("tensor", "LK", kw)
            for qt in range(NQT):
                em.wait("sync", "PE", prev_av)
                lq = 0
                for hl, buf in ((0, qtile_h), (1, qtile_l)):
                    lq = em.dma("sync", buf.ap()[:, 0:QT],
                                q_sc[hl, 128 * hp:128 * (hp + 1),
                                     QT * qt:QT * (qt + 1)], sem="LQ")
                for hl in range(2):
                    for s in range(2):
                        dst = vbuf(hl)[:, KCS * 130 * s:KCS * 130 * (s + 1)]
                        src = v_all[s, hl, hp]
                        lq = em.dma("sync", dst, src, sem="LQ")
                em.wait("tensor", "LQ", lq)
                klist = kc_lists[qt] if causal else list(range(NKC))
                exp_at, av_at, pl_at = {}, {}, {}
                for idx, g in enumerate(klist):
                    par = idx % 2
                    if idx >= 2:
                        em.wait("tensor", "ACT", exp_at[idx - 2])
                    for hh in range(2):
                        base = 64 * hh
                        o = ps_s[hh][par]
                        sth = kbuf(0)[base:base + 64, 128 * g:128 * (g + 1)]
                        stl = kbuf(1)[base:base + 64, 128 * g:128 * (g + 1)]
                        mvh = qtile_h.ap()[base:base + 64, 0:QT]
                        mvl = qtile_l.ap()[base:base + 64, 0:QT]
                        em.mm(lambda e, o=o, s=sth, m=mvh, bb=base: e.matmul(
                            o.ap(), s, m, start=True, stop=False,
                            tile_position=(bb, 0)))
                        em.mm(lambda e, o=o, s=sth, m=mvl, bb=base: e.matmul(
                            o.ap(), s, m, start=False, stop=False,
                            tile_position=(bb, 0)))
                        em.mm(lambda e, o=o, s=stl, m=mvh, bb=base: e.matmul(
                            o.ap(), s, m, start=False, stop=True,
                            tile_position=(bb, 0)), inc=True)
                    qk_pe = em.cnt["PE"]
                    mi = mask_idx.get((qt, g)) if causal else None
                    mask_dve = None
                    if mi is not None:
                        em.wait("vector", "PE", qk_pe)
                        for hh in range(2):
                            mask_dve = em.emit(
                                "vector", lambda e, hh=hh, par=par, mi=mi:
                                e.tensor_add(
                                    ps_s[hh][par].ap(), ps_s[hh][par].ap(),
                                    msk.ap()[:, QT * mi:QT * (mi + 1)]))
                    em.wait("scalar", "PE", qk_pe)
                    if mask_dve is not None:
                        em.wait("scalar", "DVE", mask_dve)
                    if idx >= 1:
                        em.wait("scalar", "DVE", pl_at[idx - 1])
                    for hh in range(2):
                        if causal:
                            bi = bias_idx[(qt, g)]
                            bias_ap = nbm.ap()[:, hh * NB + bi:
                                               hh * NB + bi + 1]
                        else:
                            bias_ap = negb.ap()[:, hh:hh + 1]
                        em.emit("scalar",
                                lambda e, hh=hh, par=par, bi_=bias_ap:
                                e.activation(
                                    out=pf.ap()[:, QT * hh:QT * (hh + 1)],
                                    in_=ps_s[hh][par].ap(), func=AF.Exp,
                                    bias=bi_, scale=0.125),
                                inc="ACT" if hh == 1 else None)
                    exp_at[idx] = em.cnt["ACT"]
                    em.wait("gpsimd", "ACT", exp_at[idx])
                    if idx >= 1:
                        em.wait("gpsimd", "PE", av_at[idx - 1])
                    gph = em.emit("gpsimd", lambda e: e.tensor_copy(
                        out=ph.ap(), in_=pf.ap()[:, 0:2 * QT]))
                    em.wait("vector", "GP", gph)
                    plc = em.emit("vector", lambda e: e.tensor_tensor(
                        out=pl.ap(), in0=pf.ap()[:, 0:2 * QT], in1=ph.ap(),
                        op=ALU.subtract))
                    pl_at[idx] = plc
                    em.wait("tensor", "DVE", plc)
                    em.wait("tensor", "GP", gph)
                    first, last = idx == 0, idx == len(klist) - 1
                    for hh in range(2):
                        vb = 130 * g + 65 * hh
                        sth = vbuf(0)[:, vb:vb + 65]
                        stl = vbuf(1)[:, vb:vb + 65]
                        o = ps_av[hh]
                        mh = ph.ap()[:, QT * hh:QT * (hh + 1)]
                        ml = pl.ap()[:, QT * hh:QT * (hh + 1)]
                        em.mm(lambda e, o=o, s=sth, m=mh, fi=first: e.matmul(
                            o.ap(), s, m, start=fi, stop=False))
                        em.mm(lambda e, o=o, s=sth, m=ml: e.matmul(
                            o.ap(), s, m, start=False, stop=False))
                        em.mm(lambda e, o=o, s=stl, m=mh, la=last: e.matmul(
                            o.ap(), s, m, start=False, stop=la), inc=True)
                    av_at[idx] = em.cnt["PE"]
                av_pe = em.cnt["PE"]
                prev_av = av_pe
                em.wait("vector", "PE", av_pe)
                rc = em.emit("vector", lambda e: e.reciprocal(
                    out=recip.ap()[:, 0:QT], in_=ps_av[0].ap()[64:65, :]))
                rc = em.emit("vector", lambda e: e.reciprocal(
                    out=recip.ap()[:, QT:2 * QT], in_=ps_av[1].ap()[64:65, :]))
                em.wait("tensor", "DVE", rc)
                mu = None
                for hh in range(2):
                    rp = em.mm(lambda e, hh=hh: e.matmul(
                        ps_rep.ap(), ones64.ap(),
                        recip.ap()[:, QT * hh:QT * (hh + 1)],
                        start=True, stop=True), inc=True)
                    em.wait("scalar", "PE", rp)
                    gv = em.emit("scalar", lambda e: e.copy(
                        out=pf.ap()[0:64, 0:QT], in_=ps_rep.ap()[0:64, :]))
                    em.wait("vector", "ACT", gv)
                    mu = em.emit("vector", lambda e, hh=hh: e.tensor_tensor(
                        out=afA.ap()[64 * hh:64 * (hh + 1), 0:QT],
                        in0=ps_av[hh].ap()[0:64, :], in1=pf.ap()[0:64, 0:QT],
                        op=ALU.mult))
                    em.wait("tensor", "DVE", mu)
                    em.wait("scalar", "DVE", mu)
                em.wait("gpsimd", "DVE", mu)
                g1 = em.emit("gpsimd", lambda e: e.tensor_copy(
                    out=astg_h.ap(), in_=afA.ap()[:, 0:QT]))
                em.wait("vector", "GP", g1)
                v1 = em.emit("vector", lambda e: e.tensor_tensor(
                    out=astg_l.ap(), in0=afA.ap()[:, 0:QT], in1=astg_h.ap(),
                    op=ALU.subtract))
                em.wait("scalar", "GP", g1)
                em.wait("scalar", "DVE", v1)
                em.dma("scalar", a_sc[0, 128 * hp:128 * (hp + 1),
                                      QT * qt:QT * (qt + 1)], astg_h.ap())
                em.dma("scalar", a_sc[1, 128 * hp:128 * (hp + 1),
                                      QT * qt:QT * (qt + 1)], astg_l.ap())
                sw_now = em.cnt["SW"]
                em.wait("gpsimd", "SW", sw_now)
                em.wait("vector", "SW", sw_now)
                em.wait("tensor", "SW", sw_now)

    def matout_phase(bk):
        last_g = last_v = 0
        for dc in range(FC):
            ld = em.dma("sync", qe.ap(), w_o[bk][128 * dc:128 * (dc + 1), :])
            em.wait("gpsimd", "LD", ld)
            last_g = em.emit("gpsimd", lambda e, dc=dc: e.tensor_copy(
                out=wp(0)[:, D * dc:D * (dc + 1)], in_=qe.ap()))
            em.wait("vector", "GP", last_g)
            last_v = em.emit("vector", lambda e, dc=dc: e.tensor_tensor(
                out=wp(1)[:, D * dc:D * (dc + 1)], in0=qe.ap(),
                in1=wp(0)[:, D * dc:D * (dc + 1)], op=ALU.subtract))
            em.wait("sync", "GP", last_g)
            em.wait("sync", "DVE", last_v)
        ldb = em.dma("sync", pf.ap()[:, 0:D],
                     b_o[bk].unsqueeze(0).broadcast_to([128, D]))
        em.wait("tensor", "GP", last_g)
        em.wait("tensor", "DVE", last_v)
        em.wait("vector", "LD", ldb)
        NRING = 4
        ring_mm = {}
        si = 0
        for tc in range(NTT):
            for og in range(NOG):
                bank = ps_s[0][og % 2]
                for dc in range(FC):
                    slot = si % NRING
                    sname = f"LA{slot}"
                    if si >= NRING:
                        em.wait("sync", "PE", ring_mm[si - NRING])
                    em.dma("sync", qtile_h.ap()[:, 128 * slot:128 * (slot + 1)],
                           a_sc[0, 128 * dc:128 * (dc + 1),
                                128 * tc:128 * (tc + 1)], sem=sname)
                    l2 = em.dma("sync",
                                qtile_l.ap()[:, 128 * slot:128 * (slot + 1)],
                                a_sc[1, 128 * dc:128 * (dc + 1),
                                     128 * tc:128 * (tc + 1)], sem=sname)
                    em.wait("tensor", sname, l2)
                    ah = qtile_h.ap()[:, 128 * slot:128 * (slot + 1)]
                    al = qtile_l.ap()[:, 128 * slot:128 * (slot + 1)]
                    wh = wp(0)[:, D * dc + QT * og:D * dc + QT * (og + 1)]
                    wl = wp(1)[:, D * dc + QT * og:D * dc + QT * (og + 1)]
                    em.mm(lambda e, b=bank, a=ah, w=wh, dc=dc: e.matmul(
                        b.ap(), a, w, start=(dc == 0), stop=False))
                    em.mm(lambda e, b=bank, a=ah, w=wl: e.matmul(
                        b.ap(), a, w, start=False, stop=False))
                    em.mm(lambda e, b=bank, a=al, w=wh, dc=dc: e.matmul(
                        b.ap(), a, w, start=False, stop=(dc == FC - 1)),
                        inc=True)
                    ring_mm[si] = em.cnt["PE"]
                    si += 1
                pe_mm = em.cnt["PE"]
                em.wait("vector", "PE", pe_mm)
                c1 = em.emit("vector", lambda e, b=bank, og=og: e.tensor_add(
                    xd.ap()[:, 0:QT], b.ap(),
                    pf.ap()[:, 0:D][:, QT * og:QT * (og + 1)]))
                em.wait("vector", "DVE", c1)
                ra = rsb.ap()[:, D * tc + QT * og:D * tc + QT * (og + 1)]
                ev = em.emit("vector", lambda e, ra=ra: e.tensor_add(
                    ra, ra, xd.ap()[:, 0:QT]))
                em.wait("tensor", "DVE", ev)

    def ffn_phase():
        ldb = em.dma("sync", pf.ap()[:, 0:D],
                     b_ffn.unsqueeze(0).broadcast_to([128, D]))
        em.wait("vector", "LD", ldb)
        for tc in range(NTT):
            for og in range(NOG):
                bank = ps_s[0][og % 2]
                for ic in range(FC):
                    xa = xqT()[:, NT * ic + 128 * tc:NT * ic + 128 * (tc + 1)]
                    wv = wfT.ap()[:, D * ic + QT * og:D * ic + QT * (og + 1)]
                    em.mm(lambda e, b=bank, x=xa, w=wv, ic=ic: e.matmul(
                        b.ap(), x, w, start=(ic == 0), stop=(ic == FC - 1)),
                        inc=(ic == FC - 1))
                pe_mm = em.cnt["PE"]
                em.wait("vector", "PE", pe_mm)
                c1 = em.emit("vector", lambda e, b=bank, og=og:
                             e.scalar_tensor_tensor(
                                 out=xd.ap()[:, 0:QT], in0=b.ap(),
                                 scalar=scvec.ap()[:, 3:4],
                                 in1=pf.ap()[:, 0:D][:, QT * og:QT * (og + 1)],
                                 op0=ALU.mult, op1=ALU.add))
                em.wait("vector", "DVE", c1)
                ra = rsb.ap()[:, D * tc + QT * og:D * tc + QT * (og + 1)]
                ev = em.emit("vector", lambda e, ra=ra: e.tensor_add(
                    ra, ra, xd.ap()[:, 0:QT]))
                em.wait("tensor", "DVE", ev)

    def attn_sublayer(bk):
        ln_phase("split")
        em.barrier()
        qkv_phase(bk)
        collective_phase()
        em.barrier()
        attn_phase(bk)
        em.barrier()
        matout_phase(bk)
        em.barrier()

    def ffn_sublayer():
        ln_phase("quant")
        em.barrier()
        ffn_phase()
        em.barrier()

    if prefix is not None:
        ln_phase("split")
        em.barrier()
        if prefix == "qkvq":
            qkv_phase("a", range(2 * FC))
            em.barrier()
        if prefix == "qkvv":
            qkv_phase("a", range(2 * FC, 3 * FC))
            em.barrier()
        if prefix == "qkv1":
            qkv_phase("a", [0])
            em.barrier()
        if prefix == "qkvkv":
            qkv_phase("a", range(FC, 3 * FC))
            em.barrier()
        if prefix == "qkvqv":
            qkv_phase("a", list(range(FC)) + list(range(2 * FC, 3 * FC)))
            em.barrier()
        if prefix in ("qkv", "coll", "attn", "matout"):
            qkv_phase("a")
            em.barrier()
        if prefix in ("coll", "attn", "matout"):
            collective_phase()
            em.barrier()
        if prefix in ("attn", "matout"):
            attn_phase("a")
            em.barrier()
        if prefix == "matout":
            matout_phase("a")
            em.barrier()
    else:
        subs = [lambda: attn_sublayer("a"), ffn_sublayer,
                lambda: attn_sublayer("b"), ffn_sublayer]
        for i in range(n_sub):
            subs[i]()
    for t in range(NTT):
        em.dma("scalar", y_out[128 * t:128 * (t + 1), :],
               rsb.ap()[:, D * t:D * (t + 1)])
    em.wait("scalar", "SW", em.cnt["SW"])
    em.wait("gpsimd", "SW", em.cnt["SW"])

    with nc.Block() as block:
        em.flush(block)
    nc._es.close()
    return nc


def _prep_inputs(T_, L0_attn_a_x, L0_attn_a_qkv_w, L0_attn_a_qkv_b,
                 L0_attn_a_proj_w, L0_attn_a_proj_b, L0_attn_b_qkv_w,
                 L0_attn_b_qkv_b, L0_attn_b_out_w, L0_attn_b_out_b,
                 L0_ffn_weight_fp8, L0_ffn_bias, L0_ffn_input_scale,
                 L0_ffn_weight_scale):
    NT, NTT, NQT, QT, NKC, KCS, HP, FC = shapes(T_)
    _, _, _, NB, NM, bias_vals, mask_keys = causal_tables(T_)
    x = np.ascontiguousarray(np.asarray(L0_attn_a_x, dtype=np.float32))
    wb = np.asarray(L0_attn_b_qkv_w, dtype=np.float32)
    bb = np.asarray(L0_attn_b_qkv_b, dtype=np.float32)
    perm = np.zeros(3 * D, dtype=np.int64)
    for h in range(NH):
        for j in range(DH):
            perm[h * DH + j] = h * 3 * DH + j
            perm[D + h * DH + j] = h * 3 * DH + DH + j
            perm[2 * D + h * DH + j] = h * 3 * DH + 2 * DH + j
    wb_g = np.ascontiguousarray(wb[:, perm])
    bb_g = np.ascontiguousarray(bb[perm])
    w_ffn_t = np.ascontiguousarray(
        np.asarray(L0_ffn_weight_fp8, dtype=np.float32).T)
    ident = np.eye(128, dtype=np.float32).astype(BF16NP)
    ident8 = np.eye(128, dtype=np.float32).astype(FP8NP)
    otop = np.zeros((128, 128), np.float32)
    otop[0:64, :] = 1.0
    obot = np.zeros((128, 128), np.float32)
    obot[64:128, :] = 1.0
    rcos, rsin = host_rope()
    permsw = np.zeros((128, 128), np.float32)
    for p in range(128):
        partner = p + 32 if (p % 64) < 32 else p - 32
        permsw[partner, p] = 1.0
    nb = x.shape[0]
    in_maps = []
    for c in range(2 * nb):
        b, h = c // 2, c % 2
        btabv = np.tile(np.asarray(bias_vals[h], np.float32)[None, :],
                        (128, 1))
        in_maps.append(dict(
            x_in=np.ascontiguousarray(x[b, NT * h:NT * (h + 1), :]),
            w_qkv_a=np.asarray(L0_attn_a_qkv_w, dtype=np.float32),
            b_qkv_a=np.ascontiguousarray(np.asarray(
                L0_attn_a_qkv_b, dtype=np.float32).reshape(-1, 128).T),
            w_proj_a=np.asarray(L0_attn_a_proj_w, dtype=np.float32),
            b_proj_a=np.asarray(L0_attn_a_proj_b, dtype=np.float32),
            w_qkv_b=wb_g,
            b_qkv_b=np.ascontiguousarray(bb_g.reshape(-1, 128).T),
            w_out_b=np.asarray(L0_attn_b_out_w, dtype=np.float32),
            b_out_b=np.asarray(L0_attn_b_out_b, dtype=np.float32),
            w_ffn_t=w_ffn_t,
            b_ffn=np.asarray(L0_ffn_bias, dtype=np.float32),
            s_in=np.asarray(L0_ffn_input_scale, dtype=np.float32),
            s_w=np.asarray(L0_ffn_weight_scale, dtype=np.float32),
            ident=ident, ident8=ident8,
            otop=otop.astype(BF16NP), obot=obot.astype(BF16NP),
            btab=btabv, mtab=host_masktab(h, mask_keys, QT),
            rcos=rcos, rsin=rsin, permsw=permsw,
        ))
    return in_maps


def kernel(**inputs):
    NT = shapes(T)[0]
    in_maps = _prep_inputs(T, **inputs)
    nc = build_program(T)
    res = run_bass_kernel_spmd(nc, in_maps, core_ids=list(range(NCORES)))
    out = np.empty((B, T, D), dtype=np.float32)
    for c in range(NCORES):
        b, h = c // 2, c % 2
        out[b, NT * h:NT * (h + 1), :] = res.results[c]["y"]
    return out


# version 11
# speedup vs baseline: 1.2405x; 1.2405x over previous
import numpy as np
import ml_dtypes
import concourse.bass as bass
import concourse.mybir as mybir
from concourse.bass_utils import run_bass_kernel_spmd

F32 = mybir.dt.float32
BF16 = mybir.dt.bfloat16
FP8 = mybir.dt.float8e4
AF = mybir.ActivationFunctionType
ALU = mybir.AluOpType
BF16NP = ml_dtypes.bfloat16
FP8NP = ml_dtypes.float8_e4m3

B, T, D, NH, DH = 4, 4096, 1024, 16, 64
DBG = set()
NCORES = 8
MASKVAL = -30000.0
BSAFE = 1.03


def shapes(T_):
    NT = T_ // 2
    NTT = NT // 128
    NQT = max(1, NT // 512)
    QT = NT // NQT
    NKC = T_ // 128
    KCS = NKC // 2
    HP = NH // 2
    FC = D // 128
    return NT, NTT, NQT, QT, NKC, KCS, HP, FC


def causal_tables(T_):
    NT, NTT, NQT, QT, NKC, KCS, HP, FC = shapes(T_)

    def classify(h, j, g):
        pos_k = NT * (g // KCS) + 128 * (g % KCS)
        d = (NT * h + QT * j) - pos_k
        if d >= 127:
            return ("none", 0)
        if d + (QT - 1) < 0:
            return ("all", 0)
        return ("part", d)

    kc_lists, bias_idx, mask_idx = [], {}, {}
    bias_cls, mask_cls = {}, {}
    for j in range(NQT):
        lst = []
        for g in range(NKC):
            cl = [classify(h, j, g) for h in range(2)]
            if cl[0][0] == "all" and cl[1][0] == "all":
                continue
            lst.append(g)
            bkey = (cl[0][0] == "all", cl[1][0] == "all")
            bias_idx[(j, g)] = bias_cls.setdefault(bkey, len(bias_cls))
            if "part" in (cl[0][0], cl[1][0]):
                mkey = (cl[0][1] if cl[0][0] == "part" else None,
                        cl[1][1] if cl[1][0] == "part" else None)
                mask_idx[(j, g)] = mask_cls.setdefault(mkey, len(mask_cls))
            else:
                mask_idx[(j, g)] = None
        kc_lists.append(lst)
    n_bias = max(1, len(bias_cls))
    n_mask = max(1, len(mask_cls))
    bias_vals = [[0.0] * n_bias for _ in range(2)]
    for (a0, a1), i in bias_cls.items():
        bias_vals[0][i] = MASKVAL if a0 else 0.0
        bias_vals[1][i] = MASKVAL if a1 else 0.0
    mask_keys = [None] * n_mask
    for k, i in mask_cls.items():
        mask_keys[i] = k
    return kc_lists, bias_idx, mask_idx, n_bias, n_mask, bias_vals, mask_keys


def host_masktab(h, mask_keys, QT):
    n = len(mask_keys)
    out = np.zeros((128, n * QT), dtype=np.float32)
    for i, key in enumerate(mask_keys):
        if key is None:
            continue
        d = key[h]
        if d is None:
            continue
        p = np.arange(128)[:, None]
        c = np.arange(QT)[None, :]
        out[:, i * QT:(i + 1) * QT] = np.where(p <= d + c, 0.0, MASKVAL)
    return np.ascontiguousarray(out.astype(BF16NP))


def host_rope():
    HPn = NH // 2
    inv = (np.float32(1.0) / (np.float32(10000.0) **
           (np.arange(0, DH, 2, dtype=np.float32) / np.float32(DH))))
    t = np.arange(2048, dtype=np.float32)
    fr = (t[:, None] * inv[None, :]).astype(np.float32)
    cos = np.cos(fr).astype(np.float32)[:NH]
    sin = np.sin(fr).astype(np.float32)[:NH]
    cosv = np.zeros((128, HPn), np.float32)
    sinv2 = np.zeros((128, HPn), np.float32)
    for hp in range(HPn):
        for p in range(128):
            head = 2 * hp + p // 64
            r = p % 64
            cosv[p, hp] = cos[head][r % 32]
            sinv2[p, hp] = -sin[head][r] if r < 32 else sin[head][r - 32]
    return cosv, sinv2


ENGS = ("sync", "tensor", "scalar", "vector", "gpsimd")
ESEM = {"tensor": "PE", "scalar": "ACT", "vector": "DVE", "gpsimd": "GP"}


class Em:
    def __init__(self, nc):
        self.nc = nc
        self.ops = {e: [] for e in ENGS}
        self.sems = {}
        self.cnt = {}
        for name in ("LD", "SW", "PE", "ACT", "DVE", "GP", "CC",
                     "LW0", "LW1", "LQ", "LK", "LA0", "LA1", "LA2", "LA3"):
            self.sems[name] = nc._es.enter_context(nc.semaphore(name))
            self.cnt[name] = 0

    def wait(self, eng, sem, val):
        if val is None or val <= 0:
            return
        s = self.sems[sem]
        self.ops[eng].append(lambda e: e.wait_ge(s, val))

    def emit(self, eng, fn, inc="auto", amt=1):
        if inc == "auto":
            inc = ESEM.get(eng)
        if inc is None:
            self.ops[eng].append(fn)
            return self.cnt.get(ESEM.get(eng, "PE"), 0)
        s = self.sems[inc]
        self.cnt[inc] += amt

        def run(e, fn=fn, s=s, amt=amt):
            fn(e).then_inc(s, amt)
        self.ops[eng].append(run)
        return self.cnt[inc]

    def mm(self, fn, inc=False):
        return self.emit("tensor", fn, inc="PE" if inc else None)

    def dma(self, eng, out, in_, sem=None):
        sem = sem or ("SW" if eng in ("gpsimd", "scalar") else "LD")
        return self.emit(eng, lambda e: e.dma_start(out=out, in_=in_),
                         inc=sem, amt=16)

    def barrier(self):
        marks = dict(self.cnt)
        for eng in ENGS:
            for sem in marks:
                self.wait(eng, sem, marks[sem])

    def flush(self, block):
        for eng in ENGS:
            ops = self.ops[eng]
            getattr(block, eng)(lambda e, ops=ops: [f(e) for f in ops])


def build_program(T_, n_sub=4, prefix=None):
    NT, NTT, NQT, QT, NKC, KCS, HP, FC = shapes(T_)
    kc_lists, bias_idx, mask_idx, NB, NM, _, _ = causal_tables(T_)
    SUBT = QT // 128
    NOG = D // QT

    import contextlib
    nc = bass.Bass("TRN2", target_bir_lowering=False, debug=False)
    nc._es = contextlib.ExitStack()

    def din(name, shp, dt=F32):
        return nc.dram_tensor(name, shp, dt, kind="ExternalInput").ap()

    x_in = din("x_in", [NT, D])
    w_qkv = {"a": din("w_qkv_a", [D, 3 * D]), "b": din("w_qkv_b", [D, 3 * D])}
    b_qkv = {"a": din("b_qkv_a", [128, 3 * D // 128]),
             "b": din("b_qkv_b", [128, 3 * D // 128])}
    w_o = {"a": din("w_proj_a", [D, D]), "b": din("w_out_b", [D, D])}
    b_o = {"a": din("b_proj_a", [D]), "b": din("b_out_b", [D])}
    w_ffn_t = din("w_ffn_t", [D, D])
    b_ffn = din("b_ffn", [D])
    s_in = din("s_in", [1])
    s_w = din("s_w", [1])
    ident_in = din("ident", [128, 128], BF16)
    ident8_in = din("ident8", [128, 128], FP8)
    otop_in = din("otop", [128, 128], BF16)
    obot_in = din("obot", [128, 128], BF16)
    btab_in = din("btab", [128, NB])
    mtab_in = din("mtab", [128, NM * QT], BF16)
    perm_in = din("permsw", [128, 128])
    rcos_in = din("rcos", [128, HP])
    rsin_in = din("rsin", [128, HP])
    y_out = nc.dram_tensor("y", [NT, D], F32, kind="ExternalOutput").ap()

    def dsc(name, shp, dt=BF16):
        return nc.dram_tensor(name, shp, dt, kind="Internal").ap()

    q_sc = dsc("q_sc", [2, D, NT])
    kt_own = dsc("kt_own", [2, D, NT])
    v_own = dsc("v_own", [2, HP, 128, KCS * 130])
    kmax_own = dsc("kmax_own", [128, HP], F32)
    kt_all = dsc("kt_all", [2, 2, D, NT])
    v_all = dsc("v_all", [2, 2, HP, 128, KCS * 130])
    kmax_all = dsc("kmax_all", [2, 128, HP], F32)
    a_sc = dsc("a_sc", [2, D, NT])

    sbuf = nc.alloc_sbuf_tensor
    rsb = sbuf("rsb", [128, NTT * D], F32)
    AR_COLS = max(2 * FC * NT, 2 * T_ + 2 * (NKC * 130), 2 * FC * D)
    arena = sbuf("arena", [128, AR_COLS], BF16)
    ar = arena.ap()

    def xT(hl):
        return ar[:, hl * FC * NT:(hl + 1) * FC * NT]

    def kbuf(hl):
        return ar[:, hl * T_:(hl + 1) * T_]

    def vbuf(hl):
        base = 2 * T_ + hl * (NKC * 130)
        return ar[:, base:base + NKC * 130]

    def wp(hl):
        return ar[:, hl * FC * D:(hl + 1) * FC * D]

    def xqT():
        return ar[:, 0:FC * NT]

    xd = sbuf("xd", [128, D], F32)
    xn = sbuf("xn", [128, D], F32)
    qe = sbuf("qe", [128, D], F32)
    stat = sbuf("stat", [128, 8], F32)
    wring_f = sbuf("wring_f", [128, 2 * D], F32)
    wring_h = sbuf("wring_h", [128, 2 * D], BF16)
    wring_l = sbuf("wring_l", [128, 2 * D], BF16)
    qtile_h = sbuf("qtile_h", [128, 2 * QT], BF16)
    qtile_l = sbuf("qtile_l", [128, 2 * QT], BF16)
    pf = sbuf("pf", [128, max(2 * QT, D)], F32)
    ph = sbuf("ph", [128, 2 * QT], BF16)
    pl = sbuf("pl", [128, 2 * QT], BF16)
    msk = sbuf("msk", [128, NM * QT], BF16)
    astg_h = sbuf("astg_h", [128, QT], BF16)
    astg_l = sbuf("astg_l", [128, QT], BF16)
    vstg = sbuf("vstg", [128, 2 * QT], BF16)
    wfT = sbuf("wfT", [128, FC * D], BF16)
    ident = sbuf("idents", [128, 128], BF16)
    ident8 = sbuf("ident8s", [128, 128], FP8)
    otop = sbuf("otops", [128, 128], BF16)
    obot = sbuf("obots", [128, 128], BF16)
    btab = sbuf("btabs", [128, NB], F32)
    nbm = sbuf("nbm", [128, 2 * NB], F32)
    negb = sbuf("negbs", [128, 2], F32)
    qmax = sbuf("qmax", [128, HP], F32)
    kmaxs = sbuf("kmaxs", [128, HP], F32)
    kmtmp = sbuf("kmtmp", [128, 2 * HP], F32)
    dmsq = sbuf("dmsq", [128, 2], BF16)
    permS = sbuf("permS", [128, 128], F32)
    rcos = sbuf("rcoss", [128, HP], F32)
    rsin = sbuf("rsins", [128, HP], F32)
    recip = sbuf("recips", [1, 2 * QT], F32)
    ones64 = sbuf("ones64", [1, 64], F32)
    scvec = sbuf("scvec", [128, 4], F32)
    bqv = sbuf("bqv", [128, 3 * D // 128], F32)
    afA = sbuf("afA", [128, QT], F32)
    qsw_t = None

    pt = nc.alloc_psum_tensor
    ps_s = [[pt(f"ps_s{hh}{p}", [128, QT], F32) for p in range(2)]
            for hh in range(2)]
    ps_av = [pt(f"ps_av{hh}", [65, QT], F32) for hh in range(2)]
    ps_rep = pt("ps_rep", [64, QT], F32)
    ps_tr = pt("ps_tr", [128, 1024], BF16)

    em = Em(nc)

    for t in range(NTT):
        em.dma("sync", rsb.ap()[:, D * t:D * (t + 1)],
               x_in[128 * t:128 * (t + 1), :])
    for dst, src in ((ident, ident_in), (ident8, ident8_in), (otop, otop_in),
                     (obot, obot_in), (btab, btab_in), (msk, mtab_in),
                     (rcos, rcos_in), (rsin, rsin_in), (permS, perm_in)):
        em.dma("sync", dst.ap(), src)
    em.dma("sync", scvec.ap()[:, 0:1],
           s_in.unsqueeze(0).broadcast_to([128, 1]))
    em.dma("sync", scvec.ap()[:, 1:2],
           s_w.unsqueeze(0).broadcast_to([128, 1]))
    for ic in range(FC):
        em.wait("sync", "DVE", em.cnt["DVE"])
        ld = em.dma("sync", qe.ap(), w_ffn_t[128 * ic:128 * (ic + 1), :])
        em.wait("vector", "LD", ld)
        em.emit("vector", lambda e, ic=ic: e.tensor_copy(
            out=wfT.ap()[:, D * ic:D * (ic + 1)], in_=qe.ap()))
    ld0 = em.cnt["LD"]
    em.wait("vector", "LD", ld0)
    tp = em.emit("vector", lambda e: e.memset(ar[:, 0:KCS * 130], 0.0))
    em.wait("vector", "DVE", tp)
    tp = em.emit("vector", lambda e: e.memset(
        ar[:, 0:KCS * 130].rearrange("p (g h c) -> p g h c", h=2, c=65)[
            :, :, :, 64:65], 1.0))
    tp0 = em.emit("vector", lambda e: e.memset(
        ar[:, KCS * 130:2 * KCS * 130], 0.0))
    em.wait("sync", "DVE", tp0)
    for hp_ in range(HP):
        em.dma("sync", v_own[0, hp_], ar[:, 0:KCS * 130])
        em.dma("sync", v_own[1, hp_], ar[:, KCS * 130:2 * KCS * 130])
    em.wait("vector", "LD", em.cnt["LD"])
    c = em.emit("vector", lambda e: e.reciprocal(
        out=scvec.ap()[:, 2:3], in_=scvec.ap()[:, 0:1]))
    em.wait("vector", "DVE", c)
    em.emit("vector", lambda e: e.tensor_mul(
        scvec.ap()[:, 3:4], scvec.ap()[:, 0:1], scvec.ap()[:, 1:2]))
    em.emit("vector", lambda e: e.memset(ones64.ap(), 1.0))
    em.barrier()

    def ln_phase(mode):
        for t in range(NTT):
            rt = rsb.ap()[:, D * t:D * (t + 1)]
            em.wait("vector", "ACT", em.cnt["ACT"])
            em.wait("vector", "GP", em.cnt["GP"])
            em.wait("vector", "PE", em.cnt["PE"])
            c0 = em.emit("vector", lambda e, rt=rt: e.reduce_sum(
                stat.ap()[:, 0:1], rt, axis=mybir.AxisListType.X))
            em.wait("vector", "DVE", c0)
            c0b = em.emit("vector", lambda e: e.tensor_scalar_mul(
                stat.ap()[:, 1:2], stat.ap()[:, 0:1], -1.0 / D))
            em.wait("vector", "DVE", c0b)
            c1 = em.emit("vector", lambda e, rt=rt: e.tensor_scalar_add(
                xd.ap(), rt, stat.ap()[:, 1:2]))
            em.wait("scalar", "DVE", c1)
            c2 = em.emit("scalar", lambda e: e.activation(
                out=qe.ap(), in_=xd.ap(), func=AF.Square))
            em.wait("vector", "ACT", c2)
            c3 = em.emit("vector", lambda e: e.reduce_sum(
                stat.ap()[:, 2:3], qe.ap(), axis=mybir.AxisListType.X))
            em.wait("vector", "DVE", c3)
            c3b = em.emit("vector", lambda e: e.tensor_scalar(
                out=stat.ap()[:, 5:6], in0=stat.ap()[:, 2:3],
                scalar1=1.0 / D, scalar2=1e-5, op0=ALU.mult, op1=ALU.add))
            em.wait("scalar", "DVE", c3b)
            c4 = em.emit("scalar", lambda e: e.activation(
                out=stat.ap()[:, 3:4], in_=stat.ap()[:, 5:6], func=AF.Sqrt))
            em.wait("vector", "ACT", c4)
            c4b = em.emit("vector", lambda e: e.reciprocal(
                out=stat.ap()[:, 6:7], in_=stat.ap()[:, 3:4]))
            em.wait("vector", "DVE", c4b)
            c5 = em.emit("vector", lambda e: e.tensor_scalar_mul(
                xn.ap(), xd.ap(), stat.ap()[:, 6:7]))
            if mode == "split":
                em.wait("gpsimd", "DVE", c5)
                hbuf = xd.ap().bitcast(BF16)[:, 0:D]
                lbuf = qe.ap().bitcast(BF16)[:, 0:D]
                g1 = em.emit("gpsimd", lambda e, hbuf=hbuf: e.tensor_copy(
                    out=hbuf, in_=xn.ap()))
                em.wait("vector", "GP", g1)
                c6 = em.emit("vector", lambda e, hbuf=hbuf, lbuf=lbuf:
                             e.tensor_tensor(out=lbuf, in0=xn.ap(), in1=hbuf,
                                             op=ALU.subtract))
                srcs = [(hbuf, g1, "GP", 0), (lbuf, c6, "DVE", 1)]
            else:
                em.wait("vector", "DVE", c5)
                c6a = em.emit("vector", lambda e: e.tensor_scalar(
                    out=xd.ap(), in0=xn.ap(), scalar1=scvec.ap()[:, 2:3],
                    scalar2=448.0, op0=ALU.mult, op1=ALU.min))
                em.wait("vector", "DVE", c6a)
                qbuf = qe.ap().bitcast(FP8)[:, 0:D]
                c6 = em.emit("vector", lambda e, qbuf=qbuf:
                             e.tensor_single_scalar(out=qbuf, in_=xd.ap(),
                                                    scalar=-448.0, op=ALU.max))
                em.wait("gpsimd", "DVE", c6)
                ubuf = xd.ap().bitcast(BF16)[:, 0:D]
                c7 = em.emit("gpsimd", lambda e, qbuf=qbuf, ubuf=ubuf:
                             e.tensor_copy(out=ubuf, in_=qbuf))
                srcs = [(ubuf, c7, "GP", 0)]
            for (src, cnt, sem, hl) in srcs:
                em.wait("tensor", sem, cnt)
                pst, idn = ps_tr.ap(), ident.ap()
                for f in range(FC):
                    em.mm(lambda e, src=src, f=f, pst=pst, idn=idn:
                          e.transpose(pst[:, 128 * f:128 * (f + 1)],
                                      src[:, 128 * f:128 * (f + 1)], idn),
                          inc=(f == FC - 1))
                pe_c = em.cnt["PE"]
                if mode == "split":
                    dst = xT(hl).rearrange("p (f n) -> p f n", f=FC)[
                        :, :, 128 * t:128 * (t + 1)]
                    eng = "scalar" if hl == 0 else "vector"
                else:
                    dst = xqT().rearrange("p (f n) -> p f n", f=FC)[
                        :, :, 128 * t:128 * (t + 1)]
                    eng = "scalar"
                em.wait(eng, "PE", pe_c)
                if eng == "scalar":
                    ev = em.emit(eng, lambda e, dst=dst, pst=pst: e.copy(
                        out=dst, in_=pst.rearrange("p (f n) -> p f n", f=FC)))
                else:
                    ev = em.emit(eng, lambda e, dst=dst, pst=pst: e.tensor_copy(
                        out=dst, in_=pst.rearrange("p (f n) -> p f n", f=FC)))
                em.wait("tensor", ESEM[eng], ev)

    def stream_w_oc(w_ap, oc, ring):
        src = w_ap[:, 128 * oc:128 * (oc + 1)].rearrange(
            "(f p) c -> p f c", p=128)
        dst = wring_f.ap()[:, D * ring:D * (ring + 1)].rearrange(
            "p (f c) -> p f c", f=FC)
        ld = em.dma("sync", dst, src, sem=f"LW{ring}")
        em.wait("gpsimd", f"LW{ring}", ld)
        g = em.emit("gpsimd", lambda e, ring=ring: e.tensor_copy(
            out=wring_h.ap()[:, D * ring:D * (ring + 1)],
            in_=wring_f.ap()[:, D * ring:D * (ring + 1)]))
        em.wait("vector", "GP", g)
        v = em.emit("vector", lambda e, ring=ring: e.tensor_tensor(
            out=wring_l.ap()[:, D * ring:D * (ring + 1)],
            in0=wring_f.ap()[:, D * ring:D * (ring + 1)],
            in1=wring_h.ap()[:, D * ring:D * (ring + 1)], op=ALU.subtract))
        return g, v

    def qkv_phase(bk, oc_sel=None):
        rope = bk == "b"
        NOC = 3 * D // 128
        ldb = em.dma("sync", bqv.ap()[:, 0:NOC], b_qkv[bk])
        em.wait("scalar", "LD", ldb)
        em.wait("vector", "LD", ldb)
        zz = em.emit("vector", lambda e: e.memset(qmax.ap(), 0.0))
        zz = em.emit("vector", lambda e: e.memset(kmaxs.ap(), 0.0))
        em.wait("vector", "DVE", zz)
        prev_mm = {}
        rope_gate = {0: 0, 1: 0}
        oc_list = list(range(NOC)) if oc_sel is None else list(oc_sel)
        for oc in oc_list:
            ring = oc % 2
            if oc - 2 in prev_mm:
                em.wait("gpsimd", "PE", prev_mm[oc - 2])
                em.wait("vector", "PE", prev_mm[oc - 2])
                em.wait("sync", "PE", prev_mm[oc - 2])
            g, v = stream_w_oc(w_qkv[bk], oc, ring)
            em.wait("tensor", "GP", g)
            em.wait("tensor", "DVE", v)
            is_v = oc >= 2 * FC
            if is_v and rope:
                em.wait("tensor", "DVE", max(rope_gate.values()))
            hp = oc % FC
            for tt in range(NT // QT):
                bank = ps_s[1 if is_v else 0][tt % 2]
                for f in range(FC):
                    sth = wring_h.ap()[:, D * ring + 128 * f:
                                       D * ring + 128 * (f + 1)]
                    stl = wring_l.ap()[:, D * ring + 128 * f:
                                       D * ring + 128 * (f + 1)]
                    mvh = xT(0)[:, NT * f + QT * tt:NT * f + QT * (tt + 1)]
                    mvl = xT(1)[:, NT * f + QT * tt:NT * f + QT * (tt + 1)]
                    em.mm(lambda e, b=bank, s=sth, m=mvh, f=f: e.matmul(
                        b.ap(), s, m, start=(f == 0), stop=False))
                    em.mm(lambda e, b=bank, s=sth, m=mvl: e.matmul(
                        b.ap(), s, m, start=False, stop=False))
                    em.mm(lambda e, b=bank, s=stl, m=mvh, f=f: e.matmul(
                        b.ap(), s, m, start=False, stop=(f == FC - 1)),
                        inc=(f == FC - 1))
                pe_mm = em.cnt["PE"]
                prev_mm[oc] = pe_mm
                em.wait("vector", "PE", pe_mm)
                em.wait("vector", "SW", em.cnt["SW"])
                a1 = em.emit("vector", lambda e, b=bank, oc=oc:
                             e.tensor_scalar_add(qe.ap()[:, 0:QT], b.ap(),
                                                 bqv.ap()[:, oc:oc + 1]))
                src_f32 = qe.ap()[:, 0:QT]
                dep_sem, dep_cnt = "DVE", a1
                if rope and not is_v and "norope" not in DBG:
                    em.wait("tensor", "DVE", a1)
                    rb = ps_s[1][tt % 2]
                    if rope_gate[tt % 2]:
                        em.wait("tensor", "DVE", rope_gate[tt % 2])
                    em.mm(lambda e, rb=rb: e.matmul(
                        rb.ap(), permS.ap(), qe.ap()[:, 0:QT],
                        start=True, stop=True), inc=True)
                    rmm = em.cnt["PE"]
                    em.wait("vector", "PE", rmm)
                    r0 = em.emit("vector", lambda e, rb=rb, hp=hp:
                                 e.tensor_scalar_mul(
                                     afA.ap(), rb.ap(),
                                     rsin.ap()[:, hp:hp + 1]))
                    rope_gate[tt % 2] = r0
                    em.wait("vector", "DVE", r0)
                    r1 = em.emit("vector", lambda e, hp=hp:
                                 e.scalar_tensor_tensor(
                                     out=xn.ap()[:, 0:QT],
                                     in0=qe.ap()[:, 0:QT],
                                     scalar=rcos.ap()[:, hp:hp + 1],
                                     in1=afA.ap(), op0=ALU.mult, op1=ALU.add))
                    src_f32 = xn.ap()[:, 0:QT]
                    dep_sem, dep_cnt = "DVE", r1
                em.wait("gpsimd", dep_sem, dep_cnt)
                if not is_v:
                    g2 = em.emit("gpsimd", lambda e, s=src_f32: e.tensor_copy(
                        out=astg_h.ap(), in_=s))
                    em.wait("vector", "GP", g2)
                    em.wait("vector", dep_sem, dep_cnt)
                    v2 = em.emit("vector", lambda e, s=src_f32:
                                 e.tensor_tensor(out=astg_l.ap(), in0=s,
                                                 in1=astg_h.ap(),
                                                 op=ALU.subtract))
                    if "nomax" not in DBG:
                        em.wait("vector", "DVE", v2)
                        m1 = em.emit("vector", lambda e, s=src_f32: e.reduce_max(
                            stat.ap()[:, 4:5], s, axis=mybir.AxisListType.X,
                            apply_absolute_value=True))
                        tgt = qmax if oc < FC else kmaxs
                        em.wait("vector", "DVE", m1)
                        m2 = em.emit("vector", lambda e, tgt=tgt, hp=hp:
                                     e.tensor_tensor(
                                         out=tgt.ap()[:, hp:hp + 1],
                                         in0=tgt.ap()[:, hp:hp + 1],
                                         in1=stat.ap()[:, 4:5], op=ALU.max))
                        em.wait("vector", "DVE", m2)
                    dst_t = q_sc if oc < FC else kt_own
                    r0c = 128 * hp
                    em.wait("scalar", "GP", g2)
                    em.wait("scalar", "DVE", v2)
                    if "nostore" not in DBG:
                        em.dma("scalar",
                               dst_t[0, r0c:r0c + 128, QT * tt:QT * (tt + 1)],
                               astg_h.ap())
                        em.dma("scalar",
                               dst_t[1, r0c:r0c + 128, QT * tt:QT * (tt + 1)],
                               astg_l.ap())
                else:
                    g2 = em.emit("gpsimd", lambda e, s=src_f32: e.tensor_copy(
                        out=astg_h.ap(), in_=s))
                    em.wait("vector", "GP", g2)
                    v2 = em.emit("vector", lambda e, s=src_f32:
                                 e.tensor_tensor(out=astg_l.ap(), in0=s,
                                                 in1=astg_h.ap(),
                                                 op=ALU.subtract))
                    if "novtr" in DBG:
                        continue
                    em.wait("tensor", "GP", g2)
                    em.wait("tensor", "DVE", v2)
                    em.wait("tensor", "ACT", em.cnt["ACT"])
                    for hl, srcb in ((0, astg_h.ap()), (1, astg_l.ap())):
                        for i in range(SUBT):
                            em.mm(lambda e, s=srcb, i=i, hl=hl: e.transpose(
                                ps_tr.ap()[:, 128 * (SUBT * hl + i):
                                           128 * (SUBT * hl + i + 1)],
                                s[:, 128 * i:128 * (i + 1)], ident.ap()),
                                inc=(i == SUBT - 1 and hl == 1))
                    pe_tr = em.cnt["PE"]
                    em.wait("scalar", "PE", pe_tr)
                    ev = em.emit("scalar", lambda e: e.copy(
                        out=vstg.ap()[:, 0:2 * QT],
                        in_=ps_tr.ap()[:, 0:2 * QT]))
                    em.wait("tensor", "ACT", ev)
                    em.wait("scalar", "ACT", ev)
                    hpv = oc - 2 * FC
                    for hl in range(2):
                        for hh in range(2):
                            src2 = vstg.ap()[:, QT * hl:QT * (hl + 1)]\
                                .rearrange("p (i c) -> p i c", i=SUBT)[
                                :, :, 64 * hh:64 * (hh + 1)]
                            dst2 = v_own[hl, hpv].rearrange(
                                "p (g c) -> p g c", c=130)[
                                :, SUBT * tt:SUBT * (tt + 1),
                                65 * hh:65 * hh + 64]
                            em.dma("scalar", dst2, src2)
                sw_now = em.cnt["SW"]
                em.wait("gpsimd", "SW", sw_now)
                em.wait("vector", "SW", sw_now)
                em.wait("scalar", "SW", sw_now)
        em.wait("scalar", "DVE", em.cnt["DVE"])
        em.dma("scalar", kmax_own, kmaxs.ap()[:, 0:HP])

    def collective_phase():
        em.wait("gpsimd", "SW", em.cnt["SW"])
        rg = [[0, 1], [2, 3], [4, 5], [6, 7]]
        for i_, o_ in ((kt_own, kt_all), (v_own, v_all), (kmax_own, kmax_all)):
            cc = em.emit("gpsimd", lambda e, i_=i_, o_=o_: e.collective_compute(
                "AllGather", ALU.bypass, replica_groups=rg,
                ins=[i_.tensor.ap().opt()], outs=[o_.tensor.ap().opt()]),
                inc="CC")
        em.wait("sync", "CC", cc)
        l1 = em.dma("sync", kmtmp.ap()[:, 0:HP], kmax_all[0])
        l1 = em.dma("sync", kmtmp.ap()[:, HP:2 * HP], kmax_all[1])
        em.wait("vector", "LD", l1)
        em.wait("vector", "CC", cc)
        c = em.emit("vector", lambda e: e.tensor_max(
            kmaxs.ap(), kmtmp.ap()[:, 0:HP], kmtmp.ap()[:, HP:2 * HP]))
        em.wait("vector", "DVE", c)
        c = em.emit("vector", lambda e: e.tensor_mul(
            qmax.ap(), qmax.ap(), qmax.ap()))
        em.wait("vector", "DVE", c)
        c = em.emit("vector", lambda e: e.tensor_mul(
            kmaxs.ap(), kmaxs.ap(), kmaxs.ap()))
        em.wait("vector", "DVE", c)

    def attn_phase(bk):
        causal = bk == "a"
        prev_av = 0
        for hp in range(HP):
            em.wait("vector", "ACT", em.cnt["ACT"])
            em.wait("vector", "PE", em.cnt["PE"])
            c = em.emit("vector", lambda e, hp=hp: e.tensor_copy(
                out=dmsq.ap()[:, 0:1], in_=qmax.ap()[:, hp:hp + 1]))
            c = em.emit("vector", lambda e, hp=hp: e.tensor_copy(
                out=dmsq.ap()[:, 1:2], in_=kmaxs.ap()[:, hp:hp + 1]))
            em.wait("tensor", "DVE", c)
            for hh in range(2):
                st = otop if hh == 0 else obot
                em.mm(lambda e, st=st, hh=hh: e.matmul(
                    ps_s[0][0].ap()[:, 2 * hh:2 * hh + 1], st.ap(),
                    dmsq.ap()[:, 0:1], start=True, stop=True))
                em.mm(lambda e, st=st, hh=hh: e.matmul(
                    ps_s[0][0].ap()[:, 2 * hh + 1:2 * hh + 2], st.ap(),
                    dmsq.ap()[:, 1:2], start=True, stop=True), inc=True)
            pe_b = em.cnt["PE"]
            em.wait("scalar", "PE", pe_b)
            aq = em.emit("scalar", lambda e: e.activation(
                out=stat.ap()[:, 4:8], in_=ps_s[0][0].ap()[:, 0:4],
                func=AF.Sqrt))
            em.wait("vector", "ACT", aq)
            nb = None
            for hh in range(2):
                c = em.emit("vector", lambda e, hh=hh: e.tensor_mul(
                    stat.ap()[:, 0:1], stat.ap()[:, 4 + 2 * hh:5 + 2 * hh],
                    stat.ap()[:, 5 + 2 * hh:6 + 2 * hh]))
                em.wait("vector", "DVE", c)
                nb = em.emit("vector", lambda e, hh=hh: e.tensor_scalar_mul(
                    negb.ap()[:, hh:hh + 1], stat.ap()[:, 0:1], -BSAFE / 8.0))
                em.wait("vector", "DVE", nb)
            if causal:
                for hh in range(2):
                    nb = em.emit("vector", lambda e, hh=hh: e.tensor_scalar_add(
                        nbm.ap()[:, hh * NB:(hh + 1) * NB], btab.ap(),
                        negb.ap()[:, hh:hh + 1]))
            em.wait("scalar", "DVE", nb)
            em.wait("tensor", "ACT", em.cnt["ACT"])
            em.wait("sync", "PE", prev_av)
            kw = 0
            for hl in range(2):
                for s in range(2):
                    kw = em.dma("sync", kbuf(hl)[:, NT * s:NT * (s + 1)],
                                kt_all[s, hl, 128 * hp:128 * (hp + 1), :],
                                sem="LK")
            em.wait("tensor", "LK", kw)
            for qt in range(NQT):
                em.wait("sync", "PE", prev_av)
                lq = 0
                for hl, buf in ((0, qtile_h), (1, qtile_l)):
                    lq = em.dma("sync", buf.ap()[:, 0:QT],
                                q_sc[hl, 128 * hp:128 * (hp + 1),
                                     QT * qt:QT * (qt + 1)], sem="LQ")
                for hl in range(2):
                    for s in range(2):
                        dst = vbuf(hl)[:, KCS * 130 * s:KCS * 130 * (s + 1)]
                        src = v_all[s, hl, hp]
                        lq = em.dma("sync", dst, src, sem="LQ")
                em.wait("tensor", "LQ", lq)
                klist = kc_lists[qt] if causal else list(range(NKC))
                exp_at, av_at, pl_at = {}, {}, {}
                for idx, g in enumerate(klist):
                    par = idx % 2
                    if idx >= 2:
                        em.wait("tensor", "ACT", exp_at[idx - 2])
                    for hh in range(2):
                        base = 64 * hh
                        o = ps_s[hh][par]
                        sth = kbuf(0)[base:base + 64, 128 * g:128 * (g + 1)]
                        stl = kbuf(1)[base:base + 64, 128 * g:128 * (g + 1)]
                        mvh = qtile_h.ap()[base:base + 64, 0:QT]
                        mvl = qtile_l.ap()[base:base + 64, 0:QT]
                        em.mm(lambda e, o=o, s=sth, m=mvh, bb=base: e.matmul(
                            o.ap(), s, m, start=True, stop=False,
                            tile_position=(bb, 0)))
                        em.mm(lambda e, o=o, s=sth, m=mvl, bb=base: e.matmul(
                            o.ap(), s, m, start=False, stop=False,
                            tile_position=(bb, 0)))
                        em.mm(lambda e, o=o, s=stl, m=mvh, bb=base: e.matmul(
                            o.ap(), s, m, start=False, stop=True,
                            tile_position=(bb, 0)), inc=True)
                    qk_pe = em.cnt["PE"]
                    mi = mask_idx.get((qt, g)) if causal else None
                    mask_dve = None
                    if mi is not None:
                        em.wait("vector", "PE", qk_pe)
                        for hh in range(2):
                            mask_dve = em.emit(
                                "vector", lambda e, hh=hh, par=par, mi=mi:
                                e.tensor_add(
                                    ps_s[hh][par].ap(), ps_s[hh][par].ap(),
                                    msk.ap()[:, QT * mi:QT * (mi + 1)]))
                    em.wait("scalar", "PE", qk_pe)
                    if mask_dve is not None:
                        em.wait("scalar", "DVE", mask_dve)
                    if idx >= 1:
                        em.wait("scalar", "DVE", pl_at[idx - 1])
                    for hh in range(2):
                        if causal:
                            bi = bias_idx[(qt, g)]
                            bias_ap = nbm.ap()[:, hh * NB + bi:
                                               hh * NB + bi + 1]
                        else:
                            bias_ap = negb.ap()[:, hh:hh + 1]
                        em.emit("scalar",
                                lambda e, hh=hh, par=par, bi_=bias_ap:
                                e.activation(
                                    out=pf.ap()[:, QT * hh:QT * (hh + 1)],
                                    in_=ps_s[hh][par].ap(), func=AF.Exp,
                                    bias=bi_, scale=0.125),
                                inc="ACT" if hh == 1 else None)
                    exp_at[idx] = em.cnt["ACT"]
                    if idx >= 1:
                        em.wait("scalar", "PE", av_at[idx - 1])
                        em.wait("vector", "PE", av_at[idx - 1])
                    em.wait("scalar", "ACT", exp_at[idx])
                    gph = em.emit("scalar", lambda e: e.copy(
                        out=ph.ap()[:, 0:QT], in_=pf.ap()[:, 0:QT]))
                    em.wait("vector", "ACT", exp_at[idx])
                    h1 = em.emit("vector", lambda e: e.tensor_copy(
                        out=ph.ap()[:, QT:2 * QT], in_=pf.ap()[:, QT:2 * QT]))
                    em.wait("vector", "ACT", gph)
                    em.wait("vector", "DVE", h1)
                    plc = em.emit("vector", lambda e: e.tensor_tensor(
                        out=pl.ap(), in0=pf.ap()[:, 0:2 * QT], in1=ph.ap(),
                        op=ALU.subtract))
                    pl_at[idx] = plc
                    em.wait("tensor", "DVE", plc)
                    em.wait("tensor", "ACT", gph)
                    first, last = idx == 0, idx == len(klist) - 1
                    for hh in range(2):
                        vb = 130 * g + 65 * hh
                        sth = vbuf(0)[:, vb:vb + 65]
                        stl = vbuf(1)[:, vb:vb + 65]
                        o = ps_av[hh]
                        mh = ph.ap()[:, QT * hh:QT * (hh + 1)]
                        ml = pl.ap()[:, QT * hh:QT * (hh + 1)]
                        em.mm(lambda e, o=o, s=sth, m=mh, fi=first: e.matmul(
                            o.ap(), s, m, start=fi, stop=False))
                        em.mm(lambda e, o=o, s=sth, m=ml: e.matmul(
                            o.ap(), s, m, start=False, stop=False))
                        em.mm(lambda e, o=o, s=stl, m=mh, la=last: e.matmul(
                            o.ap(), s, m, start=False, stop=la), inc=True)
                    av_at[idx] = em.cnt["PE"]
                av_pe = em.cnt["PE"]
                prev_av = av_pe
                em.wait("vector", "PE", av_pe)
                rc = em.emit("vector", lambda e: e.reciprocal(
                    out=recip.ap()[:, 0:QT], in_=ps_av[0].ap()[64:65, :]))
                rc = em.emit("vector", lambda e: e.reciprocal(
                    out=recip.ap()[:, QT:2 * QT], in_=ps_av[1].ap()[64:65, :]))
                em.wait("tensor", "DVE", rc)
                mu = None
                for hh in range(2):
                    rp = em.mm(lambda e, hh=hh: e.matmul(
                        ps_rep.ap(), ones64.ap(),
                        recip.ap()[:, QT * hh:QT * (hh + 1)],
                        start=True, stop=True), inc=True)
                    em.wait("scalar", "PE", rp)
                    gv = em.emit("scalar", lambda e: e.copy(
                        out=pf.ap()[0:64, 0:QT], in_=ps_rep.ap()[0:64, :]))
                    em.wait("vector", "ACT", gv)
                    mu = em.emit("vector", lambda e, hh=hh: e.tensor_tensor(
                        out=afA.ap()[64 * hh:64 * (hh + 1), 0:QT],
                        in0=ps_av[hh].ap()[0:64, :], in1=pf.ap()[0:64, 0:QT],
                        op=ALU.mult))
                    em.wait("tensor", "DVE", mu)
                    em.wait("scalar", "DVE", mu)
                em.wait("gpsimd", "DVE", mu)
                g1 = em.emit("gpsimd", lambda e: e.tensor_copy(
                    out=astg_h.ap(), in_=afA.ap()[:, 0:QT]))
                em.wait("vector", "GP", g1)
                v1 = em.emit("vector", lambda e: e.tensor_tensor(
                    out=astg_l.ap(), in0=afA.ap()[:, 0:QT], in1=astg_h.ap(),
                    op=ALU.subtract))
                em.wait("scalar", "GP", g1)
                em.wait("scalar", "DVE", v1)
                em.dma("scalar", a_sc[0, 128 * hp:128 * (hp + 1),
                                      QT * qt:QT * (qt + 1)], astg_h.ap())
                em.dma("scalar", a_sc[1, 128 * hp:128 * (hp + 1),
                                      QT * qt:QT * (qt + 1)], astg_l.ap())
                sw_now = em.cnt["SW"]
                em.wait("gpsimd", "SW", sw_now)
                em.wait("vector", "SW", sw_now)
                em.wait("tensor", "SW", sw_now)

    def matout_phase(bk):
        last_g = last_v = 0
        for dc in range(FC):
            ld = em.dma("sync", qe.ap(), w_o[bk][128 * dc:128 * (dc + 1), :])
            em.wait("gpsimd", "LD", ld)
            last_g = em.emit("gpsimd", lambda e, dc=dc: e.tensor_copy(
                out=wp(0)[:, D * dc:D * (dc + 1)], in_=qe.ap()))
            em.wait("vector", "GP", last_g)
            last_v = em.emit("vector", lambda e, dc=dc: e.tensor_tensor(
                out=wp(1)[:, D * dc:D * (dc + 1)], in0=qe.ap(),
                in1=wp(0)[:, D * dc:D * (dc + 1)], op=ALU.subtract))
            em.wait("sync", "GP", last_g)
            em.wait("sync", "DVE", last_v)
        ldb = em.dma("sync", pf.ap()[:, 0:D],
                     b_o[bk].unsqueeze(0).broadcast_to([128, D]))
        em.wait("tensor", "GP", last_g)
        em.wait("tensor", "DVE", last_v)
        em.wait("vector", "LD", ldb)
        NRING = 4
        ring_mm = {}
        si = 0
        for tc in range(NTT):
            for og in range(NOG):
                bank = ps_s[0][og % 2]
                for dc in range(FC):
                    slot = si % NRING
                    sname = f"LA{slot}"
                    if si >= NRING:
                        em.wait("sync", "PE", ring_mm[si - NRING])
                    em.dma("sync", qtile_h.ap()[:, 128 * slot:128 * (slot + 1)],
                           a_sc[0, 128 * dc:128 * (dc + 1),
                                128 * tc:128 * (tc + 1)], sem=sname)
                    l2 = em.dma("sync",
                                qtile_l.ap()[:, 128 * slot:128 * (slot + 1)],
                                a_sc[1, 128 * dc:128 * (dc + 1),
                                     128 * tc:128 * (tc + 1)], sem=sname)
                    em.wait("tensor", sname, l2)
                    ah = qtile_h.ap()[:, 128 * slot:128 * (slot + 1)]
                    al = qtile_l.ap()[:, 128 * slot:128 * (slot + 1)]
                    wh = wp(0)[:, D * dc + QT * og:D * dc + QT * (og + 1)]
                    wl = wp(1)[:, D * dc + QT * og:D * dc + QT * (og + 1)]
                    em.mm(lambda e, b=bank, a=ah, w=wh, dc=dc: e.matmul(
                        b.ap(), a, w, start=(dc == 0), stop=False))
                    em.mm(lambda e, b=bank, a=ah, w=wl: e.matmul(
                        b.ap(), a, w, start=False, stop=False))
                    em.mm(lambda e, b=bank, a=al, w=wh, dc=dc: e.matmul(
                        b.ap(), a, w, start=False, stop=(dc == FC - 1)),
                        inc=True)
                    ring_mm[si] = em.cnt["PE"]
                    si += 1
                pe_mm = em.cnt["PE"]
                em.wait("vector", "PE", pe_mm)
                c1 = em.emit("vector", lambda e, b=bank, og=og: e.tensor_add(
                    xd.ap()[:, 0:QT], b.ap(),
                    pf.ap()[:, 0:D][:, QT * og:QT * (og + 1)]))
                em.wait("vector", "DVE", c1)
                ra = rsb.ap()[:, D * tc + QT * og:D * tc + QT * (og + 1)]
                ev = em.emit("vector", lambda e, ra=ra: e.tensor_add(
                    ra, ra, xd.ap()[:, 0:QT]))
                em.wait("tensor", "DVE", ev)

    def ffn_phase():
        ldb = em.dma("sync", pf.ap()[:, 0:D],
                     b_ffn.unsqueeze(0).broadcast_to([128, D]))
        em.wait("vector", "LD", ldb)
        for tc in range(NTT):
            for og in range(NOG):
                bank = ps_s[0][og % 2]
                for ic in range(FC):
                    xa = xqT()[:, NT * ic + 128 * tc:NT * ic + 128 * (tc + 1)]
                    wv = wfT.ap()[:, D * ic + QT * og:D * ic + QT * (og + 1)]
                    em.mm(lambda e, b=bank, x=xa, w=wv, ic=ic: e.matmul(
                        b.ap(), x, w, start=(ic == 0), stop=(ic == FC - 1)),
                        inc=(ic == FC - 1))
                pe_mm = em.cnt["PE"]
                em.wait("vector", "PE", pe_mm)
                c1 = em.emit("vector", lambda e, b=bank, og=og:
                             e.scalar_tensor_tensor(
                                 out=xd.ap()[:, 0:QT], in0=b.ap(),
                                 scalar=scvec.ap()[:, 3:4],
                                 in1=pf.ap()[:, 0:D][:, QT * og:QT * (og + 1)],
                                 op0=ALU.mult, op1=ALU.add))
                em.wait("vector", "DVE", c1)
                ra = rsb.ap()[:, D * tc + QT * og:D * tc + QT * (og + 1)]
                ev = em.emit("vector", lambda e, ra=ra: e.tensor_add(
                    ra, ra, xd.ap()[:, 0:QT]))
                em.wait("tensor", "DVE", ev)

    def attn_sublayer(bk):
        ln_phase("split")
        em.barrier()
        qkv_phase(bk)
        collective_phase()
        em.barrier()
        attn_phase(bk)
        em.barrier()
        matout_phase(bk)
        em.barrier()

    def ffn_sublayer():
        ln_phase("quant")
        em.barrier()
        ffn_phase()
        em.barrier()

    if prefix is not None:
        ln_phase("split")
        em.barrier()
        if prefix == "qkvq":
            qkv_phase("a", range(2 * FC))
            em.barrier()
        if prefix == "qkvv":
            qkv_phase("a", range(2 * FC, 3 * FC))
            em.barrier()
        if prefix == "qkv1":
            qkv_phase("a", [0])
            em.barrier()
        if prefix == "qkvkv":
            qkv_phase("a", range(FC, 3 * FC))
            em.barrier()
        if prefix == "qkvqv":
            qkv_phase("a", list(range(FC)) + list(range(2 * FC, 3 * FC)))
            em.barrier()
        if prefix in ("qkv", "coll", "attn", "matout"):
            qkv_phase("a")
            em.barrier()
        if prefix in ("coll", "attn", "matout"):
            collective_phase()
            em.barrier()
        if prefix in ("attn", "matout"):
            attn_phase("a")
            em.barrier()
        if prefix == "matout":
            matout_phase("a")
            em.barrier()
    else:
        subs = [lambda: attn_sublayer("a"), ffn_sublayer,
                lambda: attn_sublayer("b"), ffn_sublayer]
        for i in range(n_sub):
            subs[i]()
    for t in range(NTT):
        em.dma("scalar", y_out[128 * t:128 * (t + 1), :],
               rsb.ap()[:, D * t:D * (t + 1)])
    em.wait("scalar", "SW", em.cnt["SW"])
    em.wait("gpsimd", "SW", em.cnt["SW"])

    with nc.Block() as block:
        em.flush(block)
    nc._es.close()
    return nc


def _prep_inputs(T_, L0_attn_a_x, L0_attn_a_qkv_w, L0_attn_a_qkv_b,
                 L0_attn_a_proj_w, L0_attn_a_proj_b, L0_attn_b_qkv_w,
                 L0_attn_b_qkv_b, L0_attn_b_out_w, L0_attn_b_out_b,
                 L0_ffn_weight_fp8, L0_ffn_bias, L0_ffn_input_scale,
                 L0_ffn_weight_scale):
    NT, NTT, NQT, QT, NKC, KCS, HP, FC = shapes(T_)
    _, _, _, NB, NM, bias_vals, mask_keys = causal_tables(T_)
    x = np.ascontiguousarray(np.asarray(L0_attn_a_x, dtype=np.float32))
    wb = np.asarray(L0_attn_b_qkv_w, dtype=np.float32)
    bb = np.asarray(L0_attn_b_qkv_b, dtype=np.float32)
    perm = np.zeros(3 * D, dtype=np.int64)
    for h in range(NH):
        for j in range(DH):
            perm[h * DH + j] = h * 3 * DH + j
            perm[D + h * DH + j] = h * 3 * DH + DH + j
            perm[2 * D + h * DH + j] = h * 3 * DH + 2 * DH + j
    wb_g = np.ascontiguousarray(wb[:, perm])
    bb_g = np.ascontiguousarray(bb[perm])
    w_ffn_t = np.ascontiguousarray(
        np.asarray(L0_ffn_weight_fp8, dtype=np.float32).T)
    ident = np.eye(128, dtype=np.float32).astype(BF16NP)
    ident8 = np.eye(128, dtype=np.float32).astype(FP8NP)
    otop = np.zeros((128, 128), np.float32)
    otop[0:64, :] = 1.0
    obot = np.zeros((128, 128), np.float32)
    obot[64:128, :] = 1.0
    rcos, rsin = host_rope()
    permsw = np.zeros((128, 128), np.float32)
    for p in range(128):
        partner = p + 32 if (p % 64) < 32 else p - 32
        permsw[partner, p] = 1.0
    nb = x.shape[0]
    in_maps = []
    for c in range(2 * nb):
        b, h = c // 2, c % 2
        btabv = np.tile(np.asarray(bias_vals[h], np.float32)[None, :],
                        (128, 1))
        in_maps.append(dict(
            x_in=np.ascontiguousarray(x[b, NT * h:NT * (h + 1), :]),
            w_qkv_a=np.asarray(L0_attn_a_qkv_w, dtype=np.float32),
            b_qkv_a=np.ascontiguousarray(np.asarray(
                L0_attn_a_qkv_b, dtype=np.float32).reshape(-1, 128).T),
            w_proj_a=np.asarray(L0_attn_a_proj_w, dtype=np.float32),
            b_proj_a=np.asarray(L0_attn_a_proj_b, dtype=np.float32),
            w_qkv_b=wb_g,
            b_qkv_b=np.ascontiguousarray(bb_g.reshape(-1, 128).T),
            w_out_b=np.asarray(L0_attn_b_out_w, dtype=np.float32),
            b_out_b=np.asarray(L0_attn_b_out_b, dtype=np.float32),
            w_ffn_t=w_ffn_t,
            b_ffn=np.asarray(L0_ffn_bias, dtype=np.float32),
            s_in=np.asarray(L0_ffn_input_scale, dtype=np.float32),
            s_w=np.asarray(L0_ffn_weight_scale, dtype=np.float32),
            ident=ident, ident8=ident8,
            otop=otop.astype(BF16NP), obot=obot.astype(BF16NP),
            btab=btabv, mtab=host_masktab(h, mask_keys, QT),
            rcos=rcos, rsin=rsin, permsw=permsw,
        ))
    return in_maps


def kernel(**inputs):
    NT = shapes(T)[0]
    in_maps = _prep_inputs(T, **inputs)
    nc = build_program(T)
    res = run_bass_kernel_spmd(nc, in_maps, core_ids=list(range(NCORES)))
    out = np.empty((B, T, D), dtype=np.float32)
    for c in range(NCORES):
        b, h = c // 2, c % 2
        out[b, NT * h:NT * (h + 1), :] = res.results[c]["y"]
    return out


# version 12
# speedup vs baseline: 2.1313x; 1.7181x over previous
import numpy as np
import ml_dtypes
import concourse.bass as bass
import concourse.mybir as mybir
from concourse.bass_utils import run_bass_kernel_spmd

F32 = mybir.dt.float32
BF16 = mybir.dt.bfloat16
FP8 = mybir.dt.float8e4
AF = mybir.ActivationFunctionType
ALU = mybir.AluOpType
BF16NP = ml_dtypes.bfloat16

B, T, D, NH, DH = 4, 4096, 1024, 16, 64
DBG = set()
NCORES = 8
MASKVAL = -30000.0
BSAFE = 1.03


def shapes(T_):
    NT = T_ // 2
    NTT = NT // 128
    NQT = max(1, NT // 512)
    QT = NT // NQT
    NKC = T_ // 128
    KCS = NKC // 2
    HP = NH // 2
    FC = D // 128
    return NT, NTT, NQT, QT, NKC, KCS, HP, FC


def causal_tables(T_):
    NT, NTT, NQT, QT, NKC, KCS, HP, FC = shapes(T_)

    def classify(h, j, g):
        pos_k = NT * (g // KCS) + 128 * (g % KCS)
        d = (NT * h + QT * j) - pos_k
        if d >= 127:
            return ("none", 0)
        if d + (QT - 1) < 0:
            return ("all", 0)
        return ("part", d)

    kc_lists, bias_idx, mask_idx = [], {}, {}
    bias_cls, mask_cls = {}, {}
    for j in range(NQT):
        lst = []
        for g in range(NKC):
            cl = [classify(h, j, g) for h in range(2)]
            if cl[0][0] == "all" and cl[1][0] == "all":
                continue
            lst.append(g)
            bkey = (cl[0][0] == "all", cl[1][0] == "all")
            bias_idx[(j, g)] = bias_cls.setdefault(bkey, len(bias_cls))
            if "part" in (cl[0][0], cl[1][0]):
                mkey = (cl[0][1] if cl[0][0] == "part" else None,
                        cl[1][1] if cl[1][0] == "part" else None)
                mask_idx[(j, g)] = mask_cls.setdefault(mkey, len(mask_cls))
            else:
                mask_idx[(j, g)] = None
        kc_lists.append(lst)
    n_bias = max(1, len(bias_cls))
    n_mask = max(1, len(mask_cls))
    bias_vals = [[0.0] * n_bias for _ in range(2)]
    for (a0, a1), i in bias_cls.items():
        bias_vals[0][i] = MASKVAL if a0 else 0.0
        bias_vals[1][i] = MASKVAL if a1 else 0.0
    mask_keys = [None] * n_mask
    for k, i in mask_cls.items():
        mask_keys[i] = k
    return kc_lists, bias_idx, mask_idx, n_bias, n_mask, bias_vals, mask_keys


def host_masktab(h, mask_keys, QT):
    n = len(mask_keys)
    out = np.zeros((128, n * QT), dtype=np.float32)
    for i, key in enumerate(mask_keys):
        if key is None:
            continue
        d = key[h]
        if d is None:
            continue
        p = np.arange(128)[:, None]
        c = np.arange(QT)[None, :]
        out[:, i * QT:(i + 1) * QT] = np.where(p <= d + c, 0.0, MASKVAL)
    return np.ascontiguousarray(out.astype(BF16NP))


def host_rope():
    HPn = NH // 2
    inv = (np.float32(1.0) / (np.float32(10000.0) **
           (np.arange(0, DH, 2, dtype=np.float32) / np.float32(DH))))
    t = np.arange(2048, dtype=np.float32)
    fr = (t[:, None] * inv[None, :]).astype(np.float32)
    cos = np.cos(fr).astype(np.float32)[:NH]
    sin = np.sin(fr).astype(np.float32)[:NH]
    cosv = np.zeros((128, HPn), np.float32)
    sinv2 = np.zeros((128, HPn), np.float32)
    for hp in range(HPn):
        for p in range(128):
            head = 2 * hp + p // 64
            r = p % 64
            cosv[p, hp] = cos[head][r % 32]
            sinv2[p, hp] = -sin[head][r] if r < 32 else sin[head][r - 32]
    return cosv, sinv2


ENGS = ("sync", "tensor", "scalar", "vector", "gpsimd")
ESEM = {"tensor": "PE", "scalar": "ACT", "vector": "DVE", "gpsimd": "GP"}


class Em:
    def __init__(self, nc):
        self.nc = nc
        self.ops = {e: [] for e in ENGS}
        self.sems = {}
        self.cnt = {}
        for name in ("LD", "SW", "PE", "ACT", "DVE", "GP", "CC",
                     "LW0", "LW1", "LQ", "LK", "LA0", "LA1", "LA2", "LA3"):
            self.sems[name] = nc._es.enter_context(nc.semaphore(name))
            self.cnt[name] = 0

    def wait(self, eng, sem, val):
        if val is None or val <= 0:
            return
        s = self.sems[sem]
        self.ops[eng].append(lambda e: e.wait_ge(s, val))

    def emit(self, eng, fn, inc="auto", amt=1):
        if inc == "auto":
            inc = ESEM.get(eng)
        if inc is None:
            self.ops[eng].append(fn)
            return self.cnt.get(ESEM.get(eng, "PE"), 0)
        s = self.sems[inc]
        self.cnt[inc] += amt

        def run(e, fn=fn, s=s, amt=amt):
            fn(e).then_inc(s, amt)
        self.ops[eng].append(run)
        return self.cnt[inc]

    def mm(self, fn, inc=False):
        return self.emit("tensor", fn, inc="PE" if inc else None)

    def dma(self, eng, out, in_, sem=None):
        sem = sem or ("SW" if eng in ("gpsimd", "scalar") else "LD")
        return self.emit(eng, lambda e: e.dma_start(out=out, in_=in_),
                         inc=sem, amt=16)

    def barrier(self):
        marks = dict(self.cnt)
        for eng in ENGS:
            for sem in marks:
                self.wait(eng, sem, marks[sem])

    def flush(self, block):
        for eng in ENGS:
            ops = self.ops[eng]
            getattr(block, eng)(lambda e, ops=ops: [f(e) for f in ops])


def build_program(T_, n_sub=4, prefix=None):
    NT, NTT, NQT, QT, NKC, KCS, HP, FC = shapes(T_)
    kc_lists, bias_idx, mask_idx, NB, NM, _, _ = causal_tables(T_)
    NOC = 3 * FC
    NOG = D // QT

    import contextlib
    nc = bass.Bass("TRN2", target_bir_lowering=False, debug=False)
    nc._es = contextlib.ExitStack()

    def din(name, shp, dt=F32):
        return nc.dram_tensor(name, shp, dt, kind="ExternalInput").ap()

    x_in = din("x_in", [NT, D])
    w_qkv = {"a": din("w_qkv_a", [NOC, 128, D]),
             "b": din("w_qkv_b", [NOC, 128, D])}
    b_qkv = {"a": din("b_qkv_a", [128, NOC]), "b": din("b_qkv_b", [128, NOC])}
    w_o = {"a": din("w_proj_a", [FC, 128, D]), "b": din("w_out_b", [FC, 128, D])}
    b_o = {"a": din("b_proj_a", [D]), "b": din("b_out_b", [D])}
    w_ffn_t = din("w_ffn_t", [D, D])
    b_ffn = din("b_ffn", [D])
    s_in = din("s_in", [1])
    s_w = din("s_w", [1])
    identF_in = din("identf", [128, 128])
    identB_in = din("identb", [128, 128], BF16)
    otop_in = din("otop", [128, 128], BF16)
    obot_in = din("obot", [128, 128], BF16)
    btab_in = din("btab", [128, NB])
    mtab_in = din("mtab", [128, NM * QT], BF16)
    perm_in = din("permsw", [128, 128])
    rcos_in = din("rcos", [128, HP])
    rsin_in = din("rsin", [128, HP])
    y_out = nc.dram_tensor("y", [NT, D], F32, kind="ExternalOutput").ap()

    def dsc(name, shp, dt=F32):
        return nc.dram_tensor(name, shp, dt, kind="Internal").ap()

    q_sc = dsc("q_sc", [D, NT])
    kt_own = dsc("kt_own", [D, NT])
    v_own = dsc("v_own", [HP, 128, KCS * 128])
    kmax_own = dsc("kmax_own", [128, HP])
    kt_all = dsc("kt_all", [2, D, NT])
    v_all = dsc("v_all", [2, HP, 128, KCS * 128])
    kmax_all = dsc("kmax_all", [2, 128, HP])

    sbuf = nc.alloc_sbuf_tensor
    rsb = sbuf("rsb", [128, NTT * D], F32)
    arena = sbuf("arena", [128, FC * NT], F32)
    ar = arena.ap()
    stream = sbuf("stream", [128, 8192], F32)
    st_ = stream.ap()
    st8 = stream.ap().bitcast(FP8)

    def wring(r):
        return st_[:, r * D:(r + 1) * D]

    def kb(s):
        return st_[:, s * NT:(s + 1) * NT]

    def vb(s):
        return st_[:, 2 * NT + s * NT:2 * NT + (s + 1) * NT]

    def wp(dc):
        return st_[:, dc * D:(dc + 1) * D]

    def xq8c(ic):
        return st8[:, ic * NT:(ic + 1) * NT]

    def wf8(ic):
        return st8[:, FC * NT + ic * D:FC * NT + (ic + 1) * D]

    xd = sbuf("xd", [128, D], F32)
    xn = sbuf("xn", [128, D], F32)
    scr = sbuf("scr", [128, 3 * D // 2], F32)
    sc = scr.ap()
    scr8 = scr.ap().bitcast(FP8)
    scrB = scr.ap().bitcast(BF16)
    pexp = sbuf("pexp", [128, 4 * QT], F32)

    def pex(hh, par):
        return pexp.ap()[:, (2 * hh + par) * QT:(2 * hh + par + 1) * QT]

    dsumb = sbuf("dsumb", [128, 2 * QT], F32)

    def dsum(hh):
        return dsumb.ap()[:, hh * QT:(hh + 1) * QT]

    pf = sbuf("pf", [128, D], F32)

    def qti(par):
        return pf.ap()[:, par * QT:(par + 1) * QT]
    stat = sbuf("stat", [128, 16], F32)
    sq = sbuf("sq", [128, 4 * HP], F32)
    recip = sbuf("recips", [1, 2 * QT], F32)
    repS = sbuf("repS", [128, QT], F32)
    msk = sbuf("msk", [128, NM * QT], BF16)
    btab = sbuf("btabs", [128, NB], F32)
    nbmA = sbuf("nbmA", [128, 2 * HP * NB], F32)
    negbA = sbuf("negbA", [128, 2 * HP], F32)
    qmax = sbuf("qmax", [128, HP], F32)
    kmaxs = sbuf("kmaxs", [128, HP], F32)
    kmtmp = sbuf("kmtmp", [128, 2 * HP], F32)
    dmsqA = sbuf("dmsqA", [128, 2 * HP], BF16)
    otop = sbuf("otops", [128, 128], BF16)
    obot = sbuf("obots", [128, 128], BF16)
    identF = sbuf("identFs", [128, 128], F32)
    identB = sbuf("identBs", [128, 128], BF16)
    permS = sbuf("permS", [128, 128], F32)
    rcos = sbuf("rcoss", [128, HP], F32)
    rsin = sbuf("rsins", [128, HP], F32)
    ones128 = sbuf("ones128", [128, 1], F32)
    ones64 = sbuf("ones64", [1, 64], F32)
    scvec = sbuf("scvec", [128, 4], F32)
    bqv = sbuf("bqv", [128, NOC], F32)

    pt = nc.alloc_psum_tensor
    ps_s = [[pt(f"ps_s{hh}{p}", [128, QT], F32) for p in range(2)]
            for hh in range(2)]
    ps_av = [pt(f"ps_av{p}", [128, QT], F32) for p in range(2)]
    ps_rep = pt("ps_rep", [128, QT], F32)
    ps_tr = pt("ps_tr", [128, 1024], BF16)

    em = Em(nc)

    for t in range(NTT):
        em.dma("sync", rsb.ap()[:, D * t:D * (t + 1)],
               x_in[128 * t:128 * (t + 1), :])
    for dst, src in ((identF, identF_in), (identB, identB_in),
                     (otop, otop_in), (obot, obot_in), (btab, btab_in),
                     (msk, mtab_in), (rcos, rcos_in), (rsin, rsin_in),
                     (permS, perm_in)):
        em.dma("sync", dst.ap(), src)
    em.dma("sync", scvec.ap()[:, 0:1],
           s_in.unsqueeze(0).broadcast_to([128, 1]))
    em.dma("sync", scvec.ap()[:, 1:2],
           s_w.unsqueeze(0).broadcast_to([128, 1]))
    ld0 = em.cnt["LD"]
    em.wait("vector", "LD", ld0)
    c = em.emit("vector", lambda e: e.reciprocal(
        out=scvec.ap()[:, 2:3], in_=scvec.ap()[:, 0:1]))
    em.wait("vector", "DVE", c)
    em.emit("vector", lambda e: e.tensor_mul(
        scvec.ap()[:, 3:4], scvec.ap()[:, 0:1], scvec.ap()[:, 1:2]))
    em.emit("vector", lambda e: e.memset(ones128.ap(), 1.0))
    em.emit("vector", lambda e: e.memset(ones64.ap(), 1.0))
    em.barrier()

    def ln_phase(mode):
        for t in range(NTT):
            rt = rsb.ap()[:, D * t:D * (t + 1)]
            em.wait("vector", "ACT", em.cnt["ACT"])
            em.wait("vector", "GP", em.cnt["GP"])
            em.wait("vector", "PE", em.cnt["PE"])
            c0 = em.emit("vector", lambda e, rt=rt: e.reduce_sum(
                stat.ap()[:, 0:1], rt, axis=mybir.AxisListType.X))
            em.wait("vector", "DVE", c0)
            c0b = em.emit("vector", lambda e: e.tensor_scalar_mul(
                stat.ap()[:, 1:2], stat.ap()[:, 0:1], -1.0 / D))
            em.wait("vector", "DVE", c0b)
            c1 = em.emit("vector", lambda e, rt=rt: e.tensor_scalar_add(
                xd.ap(), rt, stat.ap()[:, 1:2]))
            em.wait("scalar", "DVE", c1)
            c2 = em.emit("scalar", lambda e: e.activation(
                out=sc[:, 0:D], in_=xd.ap(), func=AF.Square))
            em.wait("vector", "ACT", c2)
            c3 = em.emit("vector", lambda e: e.reduce_sum(
                stat.ap()[:, 2:3], sc[:, 0:D], axis=mybir.AxisListType.X))
            em.wait("vector", "DVE", c3)
            c3b = em.emit("vector", lambda e: e.tensor_scalar(
                out=stat.ap()[:, 5:6], in0=stat.ap()[:, 2:3],
                scalar1=1.0 / D, scalar2=1e-5, op0=ALU.mult, op1=ALU.add))
            em.wait("scalar", "DVE", c3b)
            c4 = em.emit("scalar", lambda e: e.activation(
                out=stat.ap()[:, 3:4], in_=stat.ap()[:, 5:6], func=AF.Sqrt))
            em.wait("vector", "ACT", c4)
            c4b = em.emit("vector", lambda e: e.reciprocal(
                out=stat.ap()[:, 6:7], in_=stat.ap()[:, 3:4]))
            em.wait("vector", "DVE", c4b)
            c5 = em.emit("vector", lambda e: e.tensor_scalar_mul(
                xn.ap(), xd.ap(), stat.ap()[:, 6:7]))
            if mode == "attn":
                em.wait("tensor", "DVE", c5)
                for half in range(2):
                    em.wait("tensor", "ACT", em.cnt["ACT"])
                    for i in range(4):
                        f = 4 * half + i
                        em.mm(lambda e, f=f, i=i: e.transpose(
                            ps_rep.ap()[:, 128 * i:128 * (i + 1)],
                            xn.ap()[:, 128 * f:128 * (f + 1)], identF.ap()),
                            inc=(i == 3))
                    pe_c = em.cnt["PE"]
                    em.wait("scalar", "PE", pe_c)
                    dst = ar.rearrange("p (f n) -> p f n", f=FC)[
                        :, 4 * half:4 * half + 4, 128 * t:128 * (t + 1)]
                    em.emit("scalar", lambda e, dst=dst: e.copy(
                        out=dst, in_=ps_rep.ap().rearrange(
                            "p (f n) -> p f n", f=4)))
            else:
                em.wait("vector", "DVE", c5)
                c6a = em.emit("vector", lambda e: e.tensor_scalar(
                    out=xd.ap(), in0=xn.ap(), scalar1=scvec.ap()[:, 2:3],
                    scalar2=448.0, op0=ALU.mult, op1=ALU.min))
                em.wait("vector", "DVE", c6a)
                q8 = scr8[:, 0:D]
                c6 = em.emit("vector", lambda e, q8=q8:
                             e.tensor_single_scalar(out=q8, in_=xd.ap(),
                                                    scalar=-448.0, op=ALU.max))
                em.wait("gpsimd", "DVE", c6)
                em.wait("gpsimd", "PE", em.cnt["PE"])
                ub = scrB[:, 2 * D:3 * D]
                c7 = em.emit("gpsimd", lambda e, q8=q8, ub=ub:
                             e.tensor_copy(out=ub, in_=q8))
                em.wait("tensor", "GP", c7)
                em.wait("tensor", "ACT", em.cnt["ACT"])
                for f in range(FC):
                    em.mm(lambda e, ub=ub, f=f: e.transpose(
                        ps_tr.ap()[:, 128 * f:128 * (f + 1)],
                        ub[:, 128 * f:128 * (f + 1)], identB.ap()),
                        inc=(f == FC - 1))
                pe_c = em.cnt["PE"]
                em.wait("scalar", "PE", pe_c)
                dst = st8[:, 0:FC * NT].rearrange("p (f n) -> p f n", f=FC)[
                    :, :, 128 * t:128 * (t + 1)]
                em.emit("scalar", lambda e, dst=dst: e.copy(
                    out=dst, in_=ps_tr.ap().rearrange(
                        "p (f n) -> p f n", f=FC)))

    def qkv_phase(bk, oc_sel=None):
        rope = bk == "b"
        ldb = em.dma("sync", bqv.ap()[:, 0:NOC], b_qkv[bk])
        em.wait("vector", "LD", ldb)
        zz = em.emit("vector", lambda e: e.memset(qmax.ap(), 0.0))
        zz = em.emit("vector", lambda e: e.memset(kmaxs.ap(), 0.0))
        em.wait("vector", "DVE", zz)
        prev_mm = {}
        evac_at = {}
        rope_done = {}
        oc_list = list(range(NOC)) if oc_sel is None else list(oc_sel)
        for oc in oc_list:
            ring = oc % 2
            if oc - 2 in prev_mm:
                em.wait("sync", "PE", prev_mm[oc - 2])
            lw = em.dma("sync", wring(ring), w_qkv[bk][oc], sem=f"LW{ring}")
            em.wait("tensor", f"LW{ring}", lw)
            is_v = oc >= 2 * FC
            hp = oc % FC
            for tt in range(NT // QT):
                par = tt % 2
                bank = ps_s[0][par]
                em.wait("tensor", "DVE", evac_at.get(("m", par), 0))
                for f in range(FC):
                    mv = ar[:, NT * f + QT * tt:NT * f + QT * (tt + 1)]
                    em.mm(lambda e, b=bank, f=f, mv=mv, ring=ring: e.matmul(
                        b.ap(), wring(ring)[:, 128 * f:128 * (f + 1)], mv,
                        start=(f == 0), stop=(f == FC - 1)),
                        inc=(f == FC - 1))
                pe_mm = em.cnt["PE"]
                prev_mm[oc] = pe_mm
                em.wait("vector", "PE", pe_mm)
                em.wait("vector", "SW", em.cnt["SW"])
                a1 = em.emit("vector", lambda e, b=bank, oc=oc, par=par:
                             e.tensor_scalar_add(sc[:, QT * par:QT * (par + 1)],
                                                 b.ap(),
                                                 bqv.ap()[:, oc:oc + 1]))
                evac_at[("m", par)] = a1
                src_f32 = sc[:, QT * par:QT * (par + 1)]
                dep = a1
                if rope and not is_v:
                    rb = ps_s[1][par]
                    em.wait("tensor", "DVE", dep)
                    em.wait("tensor", "DVE", rope_done.get(par, 0))
                    em.mm(lambda e, rb=rb, s=src_f32: e.matmul(
                        rb.ap(), permS.ap(), s, start=True, stop=True),
                        inc=True)
                    rmm = em.cnt["PE"]
                    em.wait("vector", "PE", rmm)
                    r0 = em.emit("vector", lambda e, rb=rb, hp=hp:
                                 e.tensor_scalar_mul(
                                     xd.ap()[:, 0:QT], rb.ap(),
                                     rsin.ap()[:, hp:hp + 1]))
                    rope_done[par] = r0
                    em.wait("vector", "DVE", r0)
                    r1 = em.emit("vector", lambda e, s=src_f32, hp=hp:
                                 e.scalar_tensor_tensor(
                                     out=xn.ap()[:, 0:QT], in0=s,
                                     scalar=rcos.ap()[:, hp:hp + 1],
                                     in1=xd.ap()[:, 0:QT],
                                     op0=ALU.mult, op1=ALU.add))
                    src_f32 = xn.ap()[:, 0:QT]
                    dep = r1
                if not is_v:
                    em.wait("vector", "DVE", dep)
                    m1 = em.emit("vector", lambda e, s=src_f32: e.reduce_max(
                        stat.ap()[:, 8:9], s, axis=mybir.AxisListType.X,
                        apply_absolute_value=True))
                    tgt = qmax if oc < FC else kmaxs
                    em.wait("vector", "DVE", m1)
                    m2 = em.emit("vector", lambda e, tgt=tgt, hp=hp:
                                 e.tensor_tensor(
                                     out=tgt.ap()[:, hp:hp + 1],
                                     in0=tgt.ap()[:, hp:hp + 1],
                                     in1=stat.ap()[:, 8:9], op=ALU.max))
                    dst_t = q_sc if oc < FC else kt_own
                    r0c = 128 * hp
                    em.wait("scalar", "DVE", dep)
                    em.dma("scalar",
                           dst_t[r0c:r0c + 128, QT * tt:QT * (tt + 1)],
                           src_f32)
                else:
                    em.wait("tensor", "DVE", dep)
                    em.wait("tensor", "ACT", em.cnt["ACT"])
                    for i in range(QT // 128):
                        em.mm(lambda e, s=src_f32, i=i: e.transpose(
                            ps_rep.ap()[:, 128 * i:128 * (i + 1)],
                            s[:, 128 * i:128 * (i + 1)], identF.ap()),
                            inc=(i == QT // 128 - 1))
                    pe_tr = em.cnt["PE"]
                    em.wait("scalar", "PE", pe_tr)
                    em.wait("scalar", "SW", em.cnt["SW"])
                    ev = em.emit("scalar", lambda e: e.copy(
                        out=sc[:, D:D + QT], in_=ps_rep.ap()))
                    em.wait("scalar", "ACT", ev)
                    hpv = oc - 2 * FC
                    em.dma("scalar",
                           v_own[hpv][:, QT * tt:QT * (tt + 1)],
                           sc[:, D:D + QT])
                sw_now = em.cnt["SW"]
                em.wait("vector", "SW", sw_now)
                em.wait("scalar", "SW", sw_now)
        em.wait("scalar", "DVE", em.cnt["DVE"])
        em.dma("scalar", kmax_own, kmaxs.ap()[:, 0:HP])

    def collective_phase():
        em.wait("gpsimd", "SW", em.cnt["SW"])
        rg = [[0, 1], [2, 3], [4, 5], [6, 7]]
        for i_, o_ in ((kt_own, kt_all), (v_own, v_all), (kmax_own, kmax_all)):
            cc = em.emit("gpsimd", lambda e, i_=i_, o_=o_: e.collective_compute(
                "AllGather", ALU.bypass, replica_groups=rg,
                ins=[i_.tensor.ap().opt()], outs=[o_.tensor.ap().opt()]),
                inc="CC")
        em.wait("sync", "CC", cc)
        l1 = em.dma("sync", kmtmp.ap()[:, 0:HP], kmax_all[0])
        l1 = em.dma("sync", kmtmp.ap()[:, HP:2 * HP], kmax_all[1])
        em.wait("vector", "LD", l1)
        em.wait("vector", "CC", cc)
        c = em.emit("vector", lambda e: e.tensor_max(
            kmaxs.ap(), kmtmp.ap()[:, 0:HP], kmtmp.ap()[:, HP:2 * HP]))
        em.wait("vector", "DVE", c)
        c = em.emit("vector", lambda e: e.tensor_mul(
            qmax.ap(), qmax.ap(), qmax.ap()))
        em.wait("vector", "DVE", c)
        c = em.emit("vector", lambda e: e.tensor_mul(
            kmaxs.ap(), kmaxs.ap(), kmaxs.ap()))
        em.wait("vector", "DVE", c)

    def attn_phase(bk):
        causal = bk == "a"
        c = em.emit("vector", lambda e: e.tensor_copy(
            out=dmsqA.ap()[:, 0:HP], in_=qmax.ap()))
        c = em.emit("vector", lambda e: e.tensor_copy(
            out=dmsqA.ap()[:, HP:2 * HP], in_=kmaxs.ap()))
        em.wait("tensor", "DVE", c)
        bank0 = ps_s[0][0]
        for hp in range(HP):
            for hh in range(2):
                stt = otop if hh == 0 else obot
                col = 4 * hp + 2 * hh
                em.mm(lambda e, stt=stt, col=col, hp=hp: e.matmul(
                    bank0.ap()[:, col:col + 1], stt.ap(),
                    dmsqA.ap()[:, hp:hp + 1], start=True, stop=True))
                em.mm(lambda e, stt=stt, col=col, hp=hp: e.matmul(
                    bank0.ap()[:, col + 1:col + 2], stt.ap(),
                    dmsqA.ap()[:, HP + hp:HP + hp + 1],
                    start=True, stop=True),
                    inc=(hp == HP - 1 and hh == 1))
        pe_b = em.cnt["PE"]
        em.wait("scalar", "PE", pe_b)
        aq = em.emit("scalar", lambda e: e.activation(
            out=sq.ap(), in_=bank0.ap()[:, 0:4 * HP], func=AF.Sqrt))
        em.wait("vector", "ACT", aq)
        sqv = sq.ap().rearrange("p (i two) -> p i two", two=2)
        nb = em.emit("vector", lambda e: e.tensor_tensor(
            out=negbA.ap(), in0=sqv[:, :, 0], in1=sqv[:, :, 1], op=ALU.mult))
        em.wait("vector", "DVE", nb)
        nb = em.emit("vector", lambda e: e.tensor_scalar_mul(
            negbA.ap(), negbA.ap(), -BSAFE / 8.0))
        if causal:
            em.wait("vector", "DVE", nb)
            for i in range(2 * HP):
                nb = em.emit("vector", lambda e, i=i: e.tensor_scalar_add(
                    nbmA.ap()[:, i * NB:(i + 1) * NB], btab.ap(),
                    negbA.ap()[:, i:i + 1]))
        em.wait("scalar", "DVE", nb)
        em.wait("tensor", "ACT", em.cnt["ACT"])

        prev_last = 0
        qld = {}
        avfree = {}
        exp_mark = {0: 0, 1: 0}
        av_mark = {0: 0, 1: 0}
        ds_mark = {0: 0, 1: 0}
        for hp in range(HP):
            em.wait("sync", "PE", prev_last)
            kw = 0
            for s in range(2):
                kw = em.dma("sync", vb(s), v_all[s, hp], sem="LK")
            for s in range(2):
                kw = em.dma("sync", kb(s),
                            kt_all[s, 128 * hp:128 * (hp + 1), :], sem="LK")
            em.wait("tensor", "LK", kw)
            for qt in range(NQT):
                parq = qt % 2
                if (hp, qt) not in qld:
                    lq = em.dma("sync", qti(parq),
                                q_sc[128 * hp:128 * (hp + 1),
                                     QT * qt:QT * (qt + 1)], sem="LQ")
                    qld[(hp, qt)] = lq
                em.wait("tensor", "LQ", qld[(hp, qt)])
                klist = kc_lists[qt] if causal else list(range(NKC))
                for idx, g in enumerate(klist):
                    par = idx % 2
                    s, gg = g // KCS, g % KCS
                    em.wait("tensor", "ACT", exp_mark[par])
                    for hh in range(2):
                        base = 64 * hh
                        o = ps_s[hh][par]
                        stt = kb(s)[base:base + 64, 128 * gg:128 * (gg + 1)]
                        mv = qti(parq)[base:base + 64, 0:QT]
                        em.mm(lambda e, o=o, stt=stt, mv=mv, bb=base: e.matmul(
                            o.ap(), stt, mv, start=True, stop=True,
                            tile_position=(bb, 0)), inc=(hh == 1))
                    qk_pe = em.cnt["PE"]
                    mi = mask_idx.get((qt, g)) if causal else None
                    mask_dve = None
                    if mi is not None:
                        em.wait("vector", "PE", qk_pe)
                        for hh in range(2):
                            mask_dve = em.emit(
                                "vector", lambda e, hh=hh, par=par, mi=mi:
                                e.tensor_add(
                                    ps_s[hh][par].ap(), ps_s[hh][par].ap(),
                                    msk.ap()[:, QT * mi:QT * (mi + 1)]))
                    em.wait("scalar", "PE", qk_pe)
                    if mask_dve is not None:
                        em.wait("scalar", "DVE", mask_dve)
                    em.wait("scalar", "PE", av_mark[par])
                    em.wait("scalar", "DVE", ds_mark[par])
                    for hh in range(2):
                        if causal:
                            bi = bias_idx[(qt, g)]
                            bias_ap = nbmA.ap()[:, (2 * hp + hh) * NB + bi:
                                                (2 * hp + hh) * NB + bi + 1]
                        else:
                            bias_ap = negbA.ap()[:, 2 * hp + hh:
                                                 2 * hp + hh + 1]
                        em.emit("scalar",
                                lambda e, hh=hh, par=par, bi_=bias_ap:
                                e.activation(
                                    out=pex(hh, par),
                                    in_=ps_s[hh][par].ap(), func=AF.Exp,
                                    bias=bi_, scale=0.125),
                                inc="ACT" if hh == 1 else None)
                    exp_mark[par] = em.cnt["ACT"]
                    em.wait("vector", "ACT", exp_mark[par])
                    for hh in range(2):
                        if idx == 0:
                            dd = em.emit("vector", lambda e, hh=hh, par=par:
                                         e.tensor_copy(out=dsum(hh),
                                                       in_=pex(hh, par)))
                        else:
                            dd = em.emit("vector", lambda e, hh=hh, par=par:
                                         e.tensor_add(dsum(hh), dsum(hh),
                                                      pex(hh, par)))
                    ds_mark[par] = dd
                    em.wait("tensor", "ACT", exp_mark[par])
                    if idx == 0:
                        em.wait("tensor", "DVE", avfree.get(parq, 0))
                    first, last = idx == 0, idx == len(klist) - 1
                    for hh in range(2):
                        vs = vb(s)[:, 128 * gg + 64 * hh:
                                   128 * gg + 64 * hh + 64]
                        em.mm(lambda e, vs=vs, hh=hh, par=par, parq=parq,
                              fi=first, la=last: e.matmul(
                                  ps_av[parq].ap()[64 * hh:64 * (hh + 1), :],
                                  vs, pex(hh, par), start=fi, stop=la,
                                  tile_position=(0, 64 * hh)),
                              inc=(hh == 1))
                    av_mark[par] = em.cnt["PE"]
                last_pe = em.cnt["PE"]
                prev_last = last_pe
                nhp, nqt = (hp, qt + 1) if qt + 1 < NQT else (hp + 1, 0)
                if nhp < HP:
                    em.wait("sync", "PE", last_pe)
                    qld[(nhp, nqt)] = em.dma(
                        "sync", qti(nqt % 2),
                        q_sc[128 * nhp:128 * (nhp + 1),
                             QT * nqt:QT * (nqt + 1)], sem="LQ")
                em.wait("tensor", "DVE", ds_mark[(len(klist) - 1) % 2])
                em.wait("tensor", "ACT", em.cnt["ACT"])
                for hh in range(2):
                    em.mm(lambda e, hh=hh: e.matmul(
                        ps_rep.ap()[32 * hh:32 * hh + 1, :], ones128.ap(),
                        dsum(hh), start=True, stop=True,
                        tile_position=(0, 32 * hh)), inc=(hh == 1))
                dn_pe = em.cnt["PE"]
                em.wait("vector", "PE", dn_pe)
                rc = em.emit("vector", lambda e: e.reciprocal(
                    out=recip.ap()[:, 0:QT], in_=ps_rep.ap()[0:1, :]))
                rc = em.emit("vector", lambda e: e.reciprocal(
                    out=recip.ap()[:, QT:2 * QT], in_=ps_rep.ap()[32:33, :]))
                em.wait("tensor", "DVE", rc)
                for hh in range(2):
                    em.mm(lambda e, hh=hh: e.matmul(
                        ps_rep.ap()[64 * hh:64 * (hh + 1), :], ones64.ap(),
                        recip.ap()[:, QT * hh:QT * (hh + 1)],
                        start=True, stop=True,
                        tile_position=(0, 64 * hh)), inc=(hh == 1))
                rp_pe = em.cnt["PE"]
                em.wait("scalar", "PE", rp_pe)
                gv = em.emit("scalar", lambda e: e.copy(
                    out=repS.ap(), in_=ps_rep.ap()))
                em.wait("vector", "ACT", gv)
                dst = ar[:, NT * hp + QT * qt:NT * hp + QT * (qt + 1)]
                mu = em.emit("vector", lambda e, dst=dst, parq=parq:
                             e.tensor_tensor(out=dst, in0=ps_av[parq].ap(),
                                             in1=repS.ap(), op=ALU.mult))
                avfree[parq] = mu

    def matout_phase(bk):
        lw = 0
        for dc in range(FC):
            lw = em.dma("sync", wp(dc), w_o[bk][dc], sem="LA1")
        ldb = em.dma("sync", pf.ap(),
                     b_o[bk].unsqueeze(0).broadcast_to([128, D]))
        em.wait("tensor", "LA1", lw)
        evac_at = {}
        for tc in range(NTT):
            for og in range(NOG):
                bank = ps_s[0][og]
                em.wait("tensor", "DVE", evac_at.get(og, 0))
                for dc in range(FC):
                    a_st = ar[:, NT * dc + 128 * tc:NT * dc + 128 * (tc + 1)]
                    wv = wp(dc)[:, QT * og:QT * (og + 1)]
                    em.mm(lambda e, b=bank, a=a_st, w=wv, dc=dc: e.matmul(
                        b.ap(), a, w, start=(dc == 0), stop=(dc == FC - 1)),
                        inc=(dc == FC - 1))
                pe_mm = em.cnt["PE"]
                em.wait("vector", "PE", pe_mm)
                em.wait("vector", "LD", ldb)
                c1 = em.emit("vector", lambda e, b=bank, og=og: e.tensor_add(
                    xd.ap()[:, 0:QT], b.ap(),
                    pf.ap()[:, QT * og:QT * (og + 1)]))
                em.wait("vector", "DVE", c1)
                ra = rsb.ap()[:, D * tc + QT * og:D * tc + QT * (og + 1)]
                ev = em.emit("vector", lambda e, ra=ra: e.tensor_add(
                    ra, ra, xd.ap()[:, 0:QT]))
                evac_at[og] = ev

    def ffn_phase():
        ldb = em.dma("sync", pf.ap(), b_ffn.unsqueeze(0).broadcast_to([128, D]))
        wcast = 0
        for ic in range(FC):
            em.wait("sync", "DVE", wcast)
            lw = em.dma("sync", sc[:, 0:D],
                        w_ffn_t[128 * ic:128 * (ic + 1), :], sem="LA2")
            em.wait("vector", "LA2", lw)
            wcast = em.emit("vector", lambda e, ic=ic: e.tensor_copy(
                out=wf8(ic), in_=sc[:, 0:D]))
        em.wait("tensor", "DVE", wcast)
        evac_at = {}
        for tc in range(NTT):
            for og in range(NOG):
                bank = ps_s[0][og]
                em.wait("tensor", "DVE", evac_at.get(og, 0))
                for ic in range(FC):
                    xa = xq8c(ic)[:, 128 * tc:128 * (tc + 1)]
                    wv = wf8(ic)[:, QT * og:QT * (og + 1)]
                    em.mm(lambda e, b=bank, x=xa, w=wv, ic=ic: e.matmul(
                        b.ap(), x, w, start=(ic == 0), stop=(ic == FC - 1)),
                        inc=(ic == FC - 1))
                pe_mm = em.cnt["PE"]
                em.wait("vector", "PE", pe_mm)
                em.wait("vector", "LD", ldb)
                c1 = em.emit("vector", lambda e, b=bank, og=og:
                             e.scalar_tensor_tensor(
                                 out=xd.ap()[:, 0:QT], in0=b.ap(),
                                 scalar=scvec.ap()[:, 3:4],
                                 in1=pf.ap()[:, QT * og:QT * (og + 1)],
                                 op0=ALU.mult, op1=ALU.add))
                em.wait("vector", "DVE", c1)
                ra = rsb.ap()[:, D * tc + QT * og:D * tc + QT * (og + 1)]
                ev = em.emit("vector", lambda e, ra=ra: e.tensor_add(
                    ra, ra, xd.ap()[:, 0:QT]))
                evac_at[og] = ev

    def attn_sublayer(bk):
        ln_phase("attn")
        em.barrier()
        qkv_phase(bk)
        collective_phase()
        em.barrier()
        attn_phase(bk)
        em.barrier()
        matout_phase(bk)
        em.barrier()

    def ffn_sublayer():
        ln_phase("quant")
        em.barrier()
        ffn_phase()
        em.barrier()

    if prefix is not None:
        ln_phase("attn")
        em.barrier()
        if prefix == "ln":
            pass
        if prefix in ("qkv", "coll", "attn", "matout"):
            qkv_phase("a")
            em.barrier()
        if prefix in ("coll", "attn", "matout"):
            collective_phase()
            em.barrier()
        if prefix in ("attn", "matout"):
            attn_phase("a")
            em.barrier()
        if prefix == "matout":
            matout_phase("a")
            em.barrier()
    else:
        subs = [lambda: attn_sublayer("a"), ffn_sublayer,
                lambda: attn_sublayer("b"), ffn_sublayer]
        for i in range(n_sub):
            subs[i]()
    for t in range(NTT):
        em.dma("scalar", y_out[128 * t:128 * (t + 1), :],
               rsb.ap()[:, D * t:D * (t + 1)])
    em.wait("scalar", "SW", em.cnt["SW"])
    em.wait("gpsimd", "SW", em.cnt["SW"])

    with nc.Block() as block:
        em.flush(block)
    nc._es.close()
    return nc


def _prep_inputs(T_, L0_attn_a_x, L0_attn_a_qkv_w, L0_attn_a_qkv_b,
                 L0_attn_a_proj_w, L0_attn_a_proj_b, L0_attn_b_qkv_w,
                 L0_attn_b_qkv_b, L0_attn_b_out_w, L0_attn_b_out_b,
                 L0_ffn_weight_fp8, L0_ffn_bias, L0_ffn_input_scale,
                 L0_ffn_weight_scale):
    NT, NTT, NQT, QT, NKC, KCS, HP, FC = shapes(T_)
    _, _, _, NB, NM, bias_vals, mask_keys = causal_tables(T_)
    x = np.ascontiguousarray(np.asarray(L0_attn_a_x, dtype=np.float32))
    wb = np.asarray(L0_attn_b_qkv_w, dtype=np.float32)
    bb = np.asarray(L0_attn_b_qkv_b, dtype=np.float32)
    perm = np.zeros(3 * D, dtype=np.int64)
    for h in range(NH):
        for j in range(DH):
            perm[h * DH + j] = h * 3 * DH + j
            perm[D + h * DH + j] = h * 3 * DH + DH + j
            perm[2 * D + h * DH + j] = h * 3 * DH + 2 * DH + j
    wb_g = np.ascontiguousarray(wb[:, perm])
    bb_g = np.ascontiguousarray(bb[perm])

    def lay_qkv(w):
        return np.ascontiguousarray(
            w.reshape(FC, 128, 3 * FC, 128).transpose(2, 1, 0, 3)
            .reshape(3 * FC, 128, D))

    def lay_o(w):
        return np.ascontiguousarray(
            np.asarray(w, dtype=np.float32).reshape(FC, 128, D))

    w_ffn_t = np.ascontiguousarray(
        np.asarray(L0_ffn_weight_fp8, dtype=np.float32).T)
    identF = np.eye(128, dtype=np.float32)
    identB = identF.astype(BF16NP)
    otop = np.zeros((128, 128), np.float32)
    otop[0:64, :] = 1.0
    obot = np.zeros((128, 128), np.float32)
    obot[64:128, :] = 1.0
    rcos, rsin = host_rope()
    permsw = np.zeros((128, 128), np.float32)
    for p in range(128):
        partner = p + 32 if (p % 64) < 32 else p - 32
        permsw[partner, p] = 1.0
    nb = x.shape[0]
    wq_a = lay_qkv(np.asarray(L0_attn_a_qkv_w, dtype=np.float32))
    wq_b = lay_qkv(wb_g)
    in_maps = []
    for c in range(2 * nb):
        b, h = c // 2, c % 2
        btabv = np.tile(np.asarray(bias_vals[h], np.float32)[None, :],
                        (128, 1))
        in_maps.append(dict(
            x_in=np.ascontiguousarray(x[b, NT * h:NT * (h + 1), :]),
            w_qkv_a=wq_a,
            b_qkv_a=np.ascontiguousarray(np.asarray(
                L0_attn_a_qkv_b, dtype=np.float32).reshape(-1, 128).T),
            w_proj_a=lay_o(L0_attn_a_proj_w),
            b_proj_a=np.asarray(L0_attn_a_proj_b, dtype=np.float32),
            w_qkv_b=wq_b,
            b_qkv_b=np.ascontiguousarray(bb_g.reshape(-1, 128).T),
            w_out_b=lay_o(L0_attn_b_out_w),
            b_out_b=np.asarray(L0_attn_b_out_b, dtype=np.float32),
            w_ffn_t=w_ffn_t,
            b_ffn=np.asarray(L0_ffn_bias, dtype=np.float32),
            s_in=np.asarray(L0_ffn_input_scale, dtype=np.float32),
            s_w=np.asarray(L0_ffn_weight_scale, dtype=np.float32),
            identf=identF, identb=identB,
            otop=otop.astype(BF16NP), obot=obot.astype(BF16NP),
            btab=btabv, mtab=host_masktab(h, mask_keys, QT),
            rcos=rcos, rsin=rsin, permsw=permsw,
        ))
    return in_maps


def kernel(**inputs):
    NT = shapes(T)[0]
    in_maps = _prep_inputs(T, **inputs)
    nc = build_program(T)
    res = run_bass_kernel_spmd(nc, in_maps, core_ids=list(range(NCORES)))
    out = np.empty((B, T, D), dtype=np.float32)
    for c in range(NCORES):
        b, h = c // 2, c % 2
        out[b, NT * h:NT * (h + 1), :] = res.results[c]["y"]
    return out


# version 13
# speedup vs baseline: 2.5997x; 1.2198x over previous
import numpy as np
import ml_dtypes
import concourse.bass as bass
import concourse.mybir as mybir
from concourse.bass_utils import run_bass_kernel_spmd

F32 = mybir.dt.float32
BF16 = mybir.dt.bfloat16
FP8 = mybir.dt.float8e4
AF = mybir.ActivationFunctionType
ALU = mybir.AluOpType
BF16NP = ml_dtypes.bfloat16

B, T, D, NH, DH = 4, 4096, 1024, 16, 64
DBG = set()
NCORES = 8
MASKVAL = -30000.0
BSAFE = 1.03


def shapes(T_):
    NT = T_ // 2
    NTT = NT // 128
    NQT = max(1, NT // 512)
    QT = NT // NQT
    NKC = T_ // 128
    KCS = NKC // 2
    HP = NH // 2
    FC = D // 128
    return NT, NTT, NQT, QT, NKC, KCS, HP, FC


def causal_tables(T_):
    NT, NTT, NQT, QT, NKC, KCS, HP, FC = shapes(T_)

    def classify(h, j, g):
        pos_k = NT * (g // KCS) + 128 * (g % KCS)
        d = (NT * h + QT * j) - pos_k
        if d >= 127:
            return ("none", 0)
        if d + (QT - 1) < 0:
            return ("all", 0)
        return ("part", d)

    kc_lists, bias_idx, mask_idx = [], {}, {}
    bias_cls, mask_cls = {}, {}
    for j in range(NQT):
        lst = []
        for g in range(NKC):
            cl = [classify(h, j, g) for h in range(2)]
            if cl[0][0] == "all" and cl[1][0] == "all":
                continue
            lst.append(g)
            bkey = (cl[0][0] == "all", cl[1][0] == "all")
            bias_idx[(j, g)] = bias_cls.setdefault(bkey, len(bias_cls))
            if "part" in (cl[0][0], cl[1][0]):
                mkey = (cl[0][1] if cl[0][0] == "part" else None,
                        cl[1][1] if cl[1][0] == "part" else None)
                mask_idx[(j, g)] = mask_cls.setdefault(mkey, len(mask_cls))
            else:
                mask_idx[(j, g)] = None
        kc_lists.append(lst)
    n_bias = max(1, len(bias_cls))
    n_mask = max(1, len(mask_cls))
    bias_vals = [[0.0] * n_bias for _ in range(2)]
    for (a0, a1), i in bias_cls.items():
        bias_vals[0][i] = MASKVAL if a0 else 0.0
        bias_vals[1][i] = MASKVAL if a1 else 0.0
    mask_keys = [None] * n_mask
    for k, i in mask_cls.items():
        mask_keys[i] = k
    return kc_lists, bias_idx, mask_idx, n_bias, n_mask, bias_vals, mask_keys


def host_masktab(h, mask_keys, QT):
    n = len(mask_keys)
    out = np.zeros((128, n * QT), dtype=np.float32)
    for i, key in enumerate(mask_keys):
        if key is None:
            continue
        d = key[h]
        if d is None:
            continue
        p = np.arange(128)[:, None]
        c = np.arange(QT)[None, :]
        out[:, i * QT:(i + 1) * QT] = np.where(p <= d + c, 0.0, MASKVAL)
    return np.ascontiguousarray(out.astype(BF16NP))


def host_rope():
    HPn = NH // 2
    inv = (np.float32(1.0) / (np.float32(10000.0) **
           (np.arange(0, DH, 2, dtype=np.float32) / np.float32(DH))))
    t = np.arange(2048, dtype=np.float32)
    fr = (t[:, None] * inv[None, :]).astype(np.float32)
    cos = np.cos(fr).astype(np.float32)[:NH]
    sin = np.sin(fr).astype(np.float32)[:NH]
    cosv = np.zeros((128, HPn), np.float32)
    sinv2 = np.zeros((128, HPn), np.float32)
    for hp in range(HPn):
        for p in range(128):
            head = 2 * hp + p // 64
            r = p % 64
            cosv[p, hp] = cos[head][r % 32]
            sinv2[p, hp] = -sin[head][r] if r < 32 else sin[head][r - 32]
    return cosv, sinv2


ENGS = ("sync", "tensor", "scalar", "vector", "gpsimd")
ESEM = {"tensor": "PE", "scalar": "ACT", "vector": "DVE", "gpsimd": "GP"}


class Em:
    def __init__(self, nc):
        self.nc = nc
        self.ops = {e: [] for e in ENGS}
        self.sems = {}
        self.cnt = {}
        for name in ("LD", "SW", "PE", "ACT", "DVE", "GP", "CC",
                     "LW0", "LW1", "LQ", "LK", "LA0", "LA1", "LA2", "LA3"):
            self.sems[name] = nc._es.enter_context(nc.semaphore(name))
            self.cnt[name] = 0

    def wait(self, eng, sem, val):
        if val is None or val <= 0:
            return
        s = self.sems[sem]
        self.ops[eng].append(lambda e: e.wait_ge(s, val))

    def emit(self, eng, fn, inc="auto", amt=1):
        if inc == "auto":
            inc = ESEM.get(eng)
        if inc is None:
            self.ops[eng].append(fn)
            return self.cnt.get(ESEM.get(eng, "PE"), 0)
        s = self.sems[inc]
        self.cnt[inc] += amt

        def run(e, fn=fn, s=s, amt=amt):
            fn(e).then_inc(s, amt)
        self.ops[eng].append(run)
        return self.cnt[inc]

    def mm(self, fn, inc=False):
        return self.emit("tensor", fn, inc="PE" if inc else None)

    def dma(self, eng, out, in_, sem=None):
        sem = sem or ("SW" if eng in ("gpsimd", "scalar") else "LD")
        return self.emit(eng, lambda e: e.dma_start(out=out, in_=in_),
                         inc=sem, amt=16)

    def barrier(self):
        marks = dict(self.cnt)
        for eng in ENGS:
            for sem in marks:
                self.wait(eng, sem, marks[sem])

    def flush(self, block):
        for eng in ENGS:
            ops = self.ops[eng]
            getattr(block, eng)(lambda e, ops=ops: [f(e) for f in ops])


def build_program(T_, n_sub=4, prefix=None):
    NT, NTT, NQT, QT, NKC, KCS, HP, FC = shapes(T_)
    kc_lists, bias_idx, mask_idx, NB, NM, _, _ = causal_tables(T_)
    NOC = 3 * FC
    NOG = D // QT

    import contextlib
    nc = bass.Bass("TRN2", target_bir_lowering=False, debug=False)
    nc._es = contextlib.ExitStack()

    def din(name, shp, dt=F32):
        return nc.dram_tensor(name, shp, dt, kind="ExternalInput").ap()

    x_in = din("x_in", [NT, D])
    w_qkv = {"a": din("w_qkv_a", [NOC, 128, D]),
             "b": din("w_qkv_b", [NOC, 128, D])}
    b_qkv = {"a": din("b_qkv_a", [128, NOC]), "b": din("b_qkv_b", [128, NOC])}
    w_o = {"a": din("w_proj_a", [FC, 128, D]), "b": din("w_out_b", [FC, 128, D])}
    b_o = {"a": din("b_proj_a", [D]), "b": din("b_out_b", [D])}
    w_ffn_t = din("w_ffn_t", [D, D])
    b_ffn = din("b_ffn", [D])
    s_in = din("s_in", [1])
    s_w = din("s_w", [1])
    identF_in = din("identf", [128, 128])
    identB_in = din("identb", [128, 128], BF16)
    otop_in = din("otop", [128, 128], BF16)
    obot_in = din("obot", [128, 128], BF16)
    btab_in = din("btab", [128, NB])
    mtab_in = din("mtab", [128, NM * QT], BF16)
    perm_in = din("permsw", [128, 128])
    rcos_in = din("rcos", [128, HP])
    rsin_in = din("rsin", [128, HP])
    y_out = nc.dram_tensor("y", [NT, D], F32, kind="ExternalOutput").ap()

    def dsc(name, shp, dt=F32):
        return nc.dram_tensor(name, shp, dt, kind="Internal").ap()

    q_sc = dsc("q_sc", [D, NT])
    kt_own = dsc("kt_own", [D, NT])
    v_own = dsc("v_own", [HP, 128, KCS * 128])
    kmax_own = dsc("kmax_own", [128, HP])
    kt_all = dsc("kt_all", [2, D, NT])
    v_all = dsc("v_all", [2, HP, 128, KCS * 128])
    kmax_all = dsc("kmax_all", [2, 128, HP])

    sbuf = nc.alloc_sbuf_tensor
    rsb = sbuf("rsb", [128, NTT * D], F32)
    arena = sbuf("arena", [128, FC * NT], F32)
    ar = arena.ap()
    stream = sbuf("stream", [128, 8192], F32)
    st_ = stream.ap()
    st8 = stream.ap().bitcast(FP8)

    def wring(r):
        return st_[:, r * D:(r + 1) * D]

    def kb(s):
        return st_[:, s * NT:(s + 1) * NT]

    def vb(s):
        return st_[:, 2 * NT + s * NT:2 * NT + (s + 1) * NT]

    def wp(dc):
        return st_[:, dc * D:(dc + 1) * D]

    def xq8c(ic):
        return st8[:, ic * NT:(ic + 1) * NT]

    def wf8(ic):
        return st8[:, FC * NT + ic * D:FC * NT + (ic + 1) * D]

    xd = sbuf("xd", [128, D], F32)
    xn = sbuf("xn", [128, D], F32)
    scr = sbuf("scr", [128, 3 * D // 2], F32)
    sc = scr.ap()
    scr8 = scr.ap().bitcast(FP8)
    scrB = scr.ap().bitcast(BF16)
    pexp = sbuf("pexp", [128, 4 * QT], F32)

    def pex(hh, par):
        return pexp.ap()[:, (2 * hh + par) * QT:(2 * hh + par + 1) * QT]

    dsumb = sbuf("dsumb", [128, 2 * QT], F32)

    def dsum(hh):
        return dsumb.ap()[:, hh * QT:(hh + 1) * QT]

    pf = sbuf("pf", [128, D], F32)

    def qti(par):
        return pf.ap()[:, par * QT:(par + 1) * QT]
    stat = sbuf("stat", [128, 16], F32)
    sq = sbuf("sq", [128, 4 * HP], F32)
    recip = sbuf("recips", [1, 2 * QT], F32)
    repS = sbuf("repS", [128, QT], F32)
    msk = sbuf("msk", [128, NM * QT], BF16)
    btab = sbuf("btabs", [128, NB], F32)
    nbmA = sbuf("nbmA", [128, 2 * HP * NB], F32)
    negbA = sbuf("negbA", [128, 2 * HP], F32)
    qmax = sbuf("qmax", [128, HP], F32)
    kmaxs = sbuf("kmaxs", [128, HP], F32)
    kmtmp = sbuf("kmtmp", [128, 2 * HP], F32)
    dmsqA = sbuf("dmsqA", [128, 2 * HP], BF16)
    otop = sbuf("otops", [128, 128], BF16)
    obot = sbuf("obots", [128, 128], BF16)
    identF = sbuf("identFs", [128, 128], F32)
    identB = sbuf("identBs", [128, 128], BF16)
    permS = sbuf("permS", [128, 128], F32)
    rcos = sbuf("rcoss", [128, HP], F32)
    rsin = sbuf("rsins", [128, HP], F32)
    ones128 = sbuf("ones128", [128, 1], F32)
    ones64 = sbuf("ones64", [1, 64], F32)
    scvec = sbuf("scvec", [128, 4], F32)
    bqv = sbuf("bqv", [128, NOC], F32)

    pt = nc.alloc_psum_tensor
    ps_s = [[pt(f"ps_s{hh}{p}", [128, QT], F32) for p in range(2)]
            for hh in range(2)]
    ps_av = [pt(f"ps_av{p}", [128, QT], F32) for p in range(2)]
    ps_rep = pt("ps_rep", [128, QT], F32)
    ps_tr = pt("ps_tr", [128, 1024], BF16)

    em = Em(nc)

    for t in range(NTT):
        em.dma("sync", rsb.ap()[:, D * t:D * (t + 1)],
               x_in[128 * t:128 * (t + 1), :])
    for dst, src in ((identF, identF_in), (identB, identB_in),
                     (otop, otop_in), (obot, obot_in), (btab, btab_in),
                     (msk, mtab_in), (rcos, rcos_in), (rsin, rsin_in),
                     (permS, perm_in)):
        em.dma("sync", dst.ap(), src)
    em.dma("sync", scvec.ap()[:, 0:1],
           s_in.unsqueeze(0).broadcast_to([128, 1]))
    em.dma("sync", scvec.ap()[:, 1:2],
           s_w.unsqueeze(0).broadcast_to([128, 1]))
    ld0 = em.cnt["LD"]
    em.wait("vector", "LD", ld0)
    c = em.emit("vector", lambda e: e.reciprocal(
        out=scvec.ap()[:, 2:3], in_=scvec.ap()[:, 0:1]))
    em.wait("vector", "DVE", c)
    em.emit("vector", lambda e: e.tensor_mul(
        scvec.ap()[:, 3:4], scvec.ap()[:, 0:1], scvec.ap()[:, 1:2]))
    em.emit("vector", lambda e: e.memset(ones128.ap(), 1.0))
    em.emit("vector", lambda e: e.memset(ones64.ap(), 1.0))
    em.barrier()

    def ln_phase(mode):
        tr_mark = {0: 0, 1: 0}
        gp_mark = {0: 0, 1: 0}
        sq_mark = {0: 0, 1: 0}
        st_mark = {0: 0, 1: 0}
        evac_mark = 0
        for t in range(NTT):
            p = t % 2
            rt = rsb.ap()[:, D * t:D * (t + 1)]
            xdp = xd.ap() if p == 0 else pexp.ap()[:, 0:D]
            xnp = xn.ap() if p == 0 else pexp.ap()[:, D:2 * D]
            stp = stat.ap()[:, 8 * p:8 * (p + 1)]
            sqt = sc[:, 0:D] if p == 0 else dsumb.ap()[:, 0:D]
            c0 = em.emit("vector", lambda e, rt=rt, stp=stp: e.reduce_sum(
                stp[:, 0:1], rt, axis=mybir.AxisListType.X))
            em.wait("vector", "DVE", c0)
            c0b = em.emit("vector", lambda e, stp=stp: e.tensor_scalar_mul(
                stp[:, 1:2], stp[:, 0:1], -1.0 / D))
            em.wait("vector", "DVE", c0b)
            em.wait("vector", "PE", tr_mark[p])
            em.wait("vector", "GP", gp_mark[p])
            c1 = em.emit("vector", lambda e, rt=rt, xdp=xdp, stp=stp:
                         e.tensor_scalar_add(xdp, rt, stp[:, 1:2]))
            em.wait("scalar", "DVE", c1)
            em.wait("scalar", "DVE", sq_mark[p])
            c2 = em.emit("scalar", lambda e, xdp=xdp, sqt=sqt: e.activation(
                out=sqt, in_=xdp, func=AF.Square))
            em.wait("vector", "ACT", c2)
            c3 = em.emit("vector", lambda e, sqt=sqt, stp=stp: e.reduce_sum(
                stp[:, 2:3], sqt, axis=mybir.AxisListType.X))
            sq_mark[p] = c3
            em.wait("vector", "DVE", c3)
            em.wait("vector", "ACT", st_mark[p])
            c3b = em.emit("vector", lambda e, stp=stp: e.tensor_scalar(
                out=stp[:, 5:6], in0=stp[:, 2:3],
                scalar1=1.0 / D, scalar2=1e-5, op0=ALU.mult, op1=ALU.add))
            em.wait("scalar", "DVE", c3b)
            c4 = em.emit("scalar", lambda e, stp=stp: e.activation(
                out=stp[:, 3:4], in_=stp[:, 5:6], func=AF.Sqrt))
            st_mark[p] = c4
            em.wait("vector", "ACT", c4)
            c4b = em.emit("vector", lambda e, stp=stp: e.reciprocal(
                out=stp[:, 6:7], in_=stp[:, 3:4]))
            em.wait("vector", "DVE", c4b)
            c5 = em.emit("vector", lambda e, xdp=xdp, xnp=xnp, stp=stp:
                         e.tensor_scalar_mul(xnp, xdp, stp[:, 6:7]))
            if mode == "attn":
                em.wait("tensor", "DVE", c5)
                for half in range(2):
                    em.wait("tensor", "ACT", evac_mark)
                    for i in range(4):
                        f = 4 * half + i
                        em.mm(lambda e, f=f, i=i, xnp=xnp: e.transpose(
                            ps_rep.ap()[:, 128 * i:128 * (i + 1)],
                            xnp[:, 128 * f:128 * (f + 1)], identF.ap()),
                            inc=(i == 3))
                    pe_c = em.cnt["PE"]
                    em.wait("scalar", "PE", pe_c)
                    dst = ar.rearrange("p (f n) -> p f n", f=FC)[
                        :, 4 * half:4 * half + 4, 128 * t:128 * (t + 1)]
                    evac_mark = em.emit("scalar", lambda e, dst=dst: e.copy(
                        out=dst, in_=ps_rep.ap().rearrange(
                            "p (f n) -> p f n", f=4)))
                tr_mark[p] = em.cnt["PE"]
            else:
                em.wait("vector", "DVE", c5)
                c6a = em.emit("vector", lambda e, xdp=xdp, xnp=xnp:
                              e.tensor_scalar(
                                  out=xdp, in0=xnp, scalar1=scvec.ap()[:, 2:3],
                                  scalar2=448.0, op0=ALU.mult, op1=ALU.min))
                em.wait("vector", "DVE", c6a)
                q8 = scr8[:, (4 + p) * D:(5 + p) * D]
                c6 = em.emit("vector", lambda e, q8=q8, xdp=xdp:
                             e.tensor_single_scalar(out=q8, in_=xdp,
                                                    scalar=-448.0, op=ALU.max))
                em.wait("gpsimd", "DVE", c6)
                em.wait("gpsimd", "PE", tr_mark[p])
                ub = (xd.ap() if p == 0 else pexp.ap()[:, 0:D]).bitcast(
                    BF16)[:, 0:D]
                c7 = em.emit("gpsimd", lambda e, q8=q8, ub=ub:
                             e.tensor_copy(out=ub, in_=q8))
                gp_mark[p] = c7
                em.wait("tensor", "GP", c7)
                em.wait("tensor", "ACT", evac_mark)
                for f in range(FC):
                    em.mm(lambda e, ub=ub, f=f: e.transpose(
                        ps_tr.ap()[:, 128 * f:128 * (f + 1)],
                        ub[:, 128 * f:128 * (f + 1)], identB.ap()),
                        inc=(f == FC - 1))
                tr_mark[p] = em.cnt["PE"]
                em.wait("scalar", "PE", tr_mark[p])
                dst = st8[:, 0:FC * NT].rearrange("p (f n) -> p f n", f=FC)[
                    :, :, 128 * t:128 * (t + 1)]
                evac_mark = em.emit("scalar", lambda e, dst=dst: e.copy(
                    out=dst, in_=ps_tr.ap().rearrange(
                        "p (f n) -> p f n", f=FC)))

    def qkv_phase(bk, oc_sel=None):
        rope = bk == "b"
        ldb = em.dma("sync", bqv.ap()[:, 0:NOC], b_qkv[bk])
        em.wait("vector", "LD", ldb)
        zz = em.emit("vector", lambda e: e.memset(qmax.ap(), 0.0))
        zz = em.emit("vector", lambda e: e.memset(kmaxs.ap(), 0.0))
        em.wait("vector", "DVE", zz)
        prev_mm = {}
        evac_at = {}
        rope_done = {}
        oc_list = (list(range(FC, 3 * FC)) + list(range(FC))
                   if oc_sel is None else list(oc_sel))
        for oci, oc in enumerate(oc_list):
            ring = oci % 2
            if oci >= 2:
                em.wait("sync", "PE", prev_mm[oc_list[oci - 2]])
            lw = em.dma("sync", wring(ring), w_qkv[bk][oc], sem=f"LW{ring}")
            em.wait("tensor", f"LW{ring}", lw)
            is_v = oc >= 2 * FC
            hp = oc % FC
            for tt in range(NT // QT):
                par = tt % 2
                bank = ps_s[0][par]
                em.wait("tensor", "DVE", evac_at.get(("m", par), 0))
                for f in range(FC):
                    mv = ar[:, NT * f + QT * tt:NT * f + QT * (tt + 1)]
                    em.mm(lambda e, b=bank, f=f, mv=mv, ring=ring: e.matmul(
                        b.ap(), wring(ring)[:, 128 * f:128 * (f + 1)], mv,
                        start=(f == 0), stop=(f == FC - 1)),
                        inc=(f == FC - 1))
                pe_mm = em.cnt["PE"]
                prev_mm[oc] = pe_mm
                em.wait("vector", "PE", pe_mm)
                em.wait("vector", "SW", em.cnt["SW"])
                a1 = em.emit("vector", lambda e, b=bank, oc=oc, par=par:
                             e.tensor_scalar_add(sc[:, QT * par:QT * (par + 1)],
                                                 b.ap(),
                                                 bqv.ap()[:, oc:oc + 1]))
                evac_at[("m", par)] = a1
                src_f32 = sc[:, QT * par:QT * (par + 1)]
                dep = a1
                if rope and not is_v:
                    rb = ps_s[1][par]
                    em.wait("tensor", "DVE", dep)
                    em.wait("tensor", "DVE", rope_done.get(par, 0))
                    em.mm(lambda e, rb=rb, s=src_f32: e.matmul(
                        rb.ap(), permS.ap(), s, start=True, stop=True),
                        inc=True)
                    rmm = em.cnt["PE"]
                    em.wait("vector", "PE", rmm)
                    r0 = em.emit("vector", lambda e, rb=rb, hp=hp:
                                 e.tensor_scalar_mul(
                                     xd.ap()[:, 0:QT], rb.ap(),
                                     rsin.ap()[:, hp:hp + 1]))
                    rope_done[par] = r0
                    em.wait("vector", "DVE", r0)
                    r1 = em.emit("vector", lambda e, s=src_f32, hp=hp:
                                 e.scalar_tensor_tensor(
                                     out=xn.ap()[:, 0:QT], in0=s,
                                     scalar=rcos.ap()[:, hp:hp + 1],
                                     in1=xd.ap()[:, 0:QT],
                                     op0=ALU.mult, op1=ALU.add))
                    src_f32 = xn.ap()[:, 0:QT]
                    dep = r1
                if not is_v:
                    em.wait("vector", "DVE", dep)
                    m1 = em.emit("vector", lambda e, s=src_f32: e.reduce_max(
                        stat.ap()[:, 8:9], s, axis=mybir.AxisListType.X,
                        apply_absolute_value=True))
                    tgt = qmax if oc < FC else kmaxs
                    em.wait("vector", "DVE", m1)
                    m2 = em.emit("vector", lambda e, tgt=tgt, hp=hp:
                                 e.tensor_tensor(
                                     out=tgt.ap()[:, hp:hp + 1],
                                     in0=tgt.ap()[:, hp:hp + 1],
                                     in1=stat.ap()[:, 8:9], op=ALU.max))
                    dst_t = q_sc if oc < FC else kt_own
                    r0c = 128 * hp
                    em.wait("scalar", "DVE", dep)
                    em.dma("scalar",
                           dst_t[r0c:r0c + 128, QT * tt:QT * (tt + 1)],
                           src_f32)
                else:
                    em.wait("tensor", "DVE", dep)
                    em.wait("tensor", "ACT", em.cnt["ACT"])
                    for i in range(QT // 128):
                        em.mm(lambda e, s=src_f32, i=i: e.transpose(
                            ps_rep.ap()[:, 128 * i:128 * (i + 1)],
                            s[:, 128 * i:128 * (i + 1)], identF.ap()),
                            inc=(i == QT // 128 - 1))
                    pe_tr = em.cnt["PE"]
                    em.wait("scalar", "PE", pe_tr)
                    em.wait("scalar", "SW", em.cnt["SW"])
                    ev = em.emit("scalar", lambda e: e.copy(
                        out=sc[:, D:D + QT], in_=ps_rep.ap()))
                    em.wait("scalar", "ACT", ev)
                    hpv = oc - 2 * FC
                    em.dma("scalar",
                           v_own[hpv][:, QT * tt:QT * (tt + 1)],
                           sc[:, D:D + QT])
                sw_now = em.cnt["SW"]
                em.wait("vector", "SW", sw_now)
                em.wait("scalar", "SW", sw_now)
            if oc_sel is None and oc == 2 * FC - 1 and "latecc" not in DBG:
                em.wait("scalar", "DVE", em.cnt["DVE"])
                em.dma("scalar", kmax_own, kmaxs.ap()[:, 0:HP])
                cc_start(((kt_own, kt_all), (kmax_own, kmax_all)))
            if oc_sel is None and oc == 3 * FC - 1 and "latecc" not in DBG:
                cc_start(((v_own, v_all),))
        if oc_sel is None and "latecc" in DBG:
            em.wait("scalar", "DVE", em.cnt["DVE"])
            em.dma("scalar", kmax_own, kmaxs.ap()[:, 0:HP])
            cc_start(((kt_own, kt_all), (kmax_own, kmax_all),
                      (v_own, v_all)))

    def cc_start(pairs):
        em.wait("gpsimd", "SW", em.cnt["SW"])
        rg = [[0, 1], [2, 3], [4, 5], [6, 7]]
        for i_, o_ in pairs:
            em.emit("gpsimd", lambda e, i_=i_, o_=o_: e.collective_compute(
                "AllGather", ALU.bypass, replica_groups=rg,
                ins=[i_.tensor.ap().opt()], outs=[o_.tensor.ap().opt()]),
                inc="CC")

    def collective_phase():
        cc = em.cnt["CC"]
        em.wait("sync", "CC", cc)
        l1 = em.dma("sync", kmtmp.ap()[:, 0:HP], kmax_all[0])
        l1 = em.dma("sync", kmtmp.ap()[:, HP:2 * HP], kmax_all[1])
        em.wait("vector", "LD", l1)
        em.wait("vector", "CC", cc)
        c = em.emit("vector", lambda e: e.tensor_max(
            kmaxs.ap(), kmtmp.ap()[:, 0:HP], kmtmp.ap()[:, HP:2 * HP]))
        em.wait("vector", "DVE", c)
        c = em.emit("vector", lambda e: e.tensor_mul(
            qmax.ap(), qmax.ap(), qmax.ap()))
        em.wait("vector", "DVE", c)
        c = em.emit("vector", lambda e: e.tensor_mul(
            kmaxs.ap(), kmaxs.ap(), kmaxs.ap()))
        em.wait("vector", "DVE", c)

    def attn_phase(bk):
        causal = bk == "a"
        c = em.emit("vector", lambda e: e.tensor_copy(
            out=dmsqA.ap()[:, 0:HP], in_=qmax.ap()))
        c = em.emit("vector", lambda e: e.tensor_copy(
            out=dmsqA.ap()[:, HP:2 * HP], in_=kmaxs.ap()))
        em.wait("tensor", "DVE", c)
        bank0 = ps_s[0][0]
        for hp in range(HP):
            for hh in range(2):
                stt = otop if hh == 0 else obot
                col = 4 * hp + 2 * hh
                em.mm(lambda e, stt=stt, col=col, hp=hp: e.matmul(
                    bank0.ap()[:, col:col + 1], stt.ap(),
                    dmsqA.ap()[:, hp:hp + 1], start=True, stop=True))
                em.mm(lambda e, stt=stt, col=col, hp=hp: e.matmul(
                    bank0.ap()[:, col + 1:col + 2], stt.ap(),
                    dmsqA.ap()[:, HP + hp:HP + hp + 1],
                    start=True, stop=True),
                    inc=(hp == HP - 1 and hh == 1))
        pe_b = em.cnt["PE"]
        em.wait("scalar", "PE", pe_b)
        aq = em.emit("scalar", lambda e: e.activation(
            out=sq.ap(), in_=bank0.ap()[:, 0:4 * HP], func=AF.Sqrt))
        em.wait("vector", "ACT", aq)
        sqv = sq.ap().rearrange("p (i two) -> p i two", two=2)
        nb = em.emit("vector", lambda e: e.tensor_tensor(
            out=negbA.ap(), in0=sqv[:, :, 0], in1=sqv[:, :, 1], op=ALU.mult))
        em.wait("vector", "DVE", nb)
        nb = em.emit("vector", lambda e: e.tensor_scalar_mul(
            negbA.ap(), negbA.ap(), -BSAFE / 8.0))
        if causal:
            em.wait("vector", "DVE", nb)
            for i in range(2 * HP):
                nb = em.emit("vector", lambda e, i=i: e.tensor_scalar_add(
                    nbmA.ap()[:, i * NB:(i + 1) * NB], btab.ap(),
                    negbA.ap()[:, i:i + 1]))
        em.wait("scalar", "DVE", nb)
        em.wait("tensor", "ACT", em.cnt["ACT"])

        prev_last = 0
        qld = {}
        avfree = {}
        exp_mark = {0: 0, 1: 0}
        av_mark = {0: 0, 1: 0}
        ds_mark = {0: 0, 1: 0}
        for hp in range(HP):
            em.wait("sync", "PE", prev_last)
            kw = 0
            for s in range(2):
                kw = em.dma("sync", vb(s), v_all[s, hp], sem="LK")
            for s in range(2):
                kw = em.dma("sync", kb(s),
                            kt_all[s, 128 * hp:128 * (hp + 1), :], sem="LK")
            em.wait("tensor", "LK", kw)
            for qt in range(NQT):
                parq = qt % 2
                if (hp, qt) not in qld:
                    lq = em.dma("sync", qti(parq),
                                q_sc[128 * hp:128 * (hp + 1),
                                     QT * qt:QT * (qt + 1)], sem="LQ")
                    qld[(hp, qt)] = lq
                em.wait("tensor", "LQ", qld[(hp, qt)])
                klist = kc_lists[qt] if causal else list(range(NKC))
                nkl = len(klist)

                def emit_av(idx, s, gg, par):
                    em.wait("tensor", "ACT", exp_mark[par])
                    if idx == 0:
                        em.wait("tensor", "DVE", avfree.get(parq, 0))
                    first, last = idx == 0, idx == nkl - 1
                    for hh in range(2):
                        vs = vb(s)[:, 128 * gg + 64 * hh:
                                   128 * gg + 64 * hh + 64]
                        em.mm(lambda e, vs=vs, hh=hh, par=par, pq=parq,
                              fi=first, la=last: e.matmul(
                                  ps_av[pq].ap()[64 * hh:64 * (hh + 1), :],
                                  vs, pex(hh, par), start=fi, stop=la,
                                  tile_position=(0, 64 * hh)),
                              inc=(hh == 1))
                    av_mark[par] = em.cnt["PE"]

                pend = None
                for idx, g in enumerate(klist):
                    par = idx % 2
                    s, gg = g // KCS, g % KCS
                    em.wait("tensor", "ACT", exp_mark[par])
                    for hh in range(2):
                        base = 64 * hh
                        o = ps_s[hh][par]
                        stt = kb(s)[base:base + 64, 128 * gg:128 * (gg + 1)]
                        mv = qti(parq)[base:base + 64, 0:QT]
                        em.mm(lambda e, o=o, stt=stt, mv=mv, bb=base: e.matmul(
                            o.ap(), stt, mv, start=True, stop=True,
                            tile_position=(bb, 0)), inc=(hh == 1))
                    qk_pe = em.cnt["PE"]
                    if pend is not None and "noshift" not in DBG:
                        emit_av(*pend)
                    if "noshift" not in DBG:
                        pend = (idx, s, gg, par)
                    mi = mask_idx.get((qt, g)) if causal else None
                    mask_dve = None
                    if mi is not None:
                        em.wait("vector", "PE", qk_pe)
                        for hh in range(2):
                            mask_dve = em.emit(
                                "vector", lambda e, hh=hh, par=par, mi=mi:
                                e.tensor_add(
                                    ps_s[hh][par].ap(), ps_s[hh][par].ap(),
                                    msk.ap()[:, QT * mi:QT * (mi + 1)]))
                    em.wait("scalar", "PE", qk_pe)
                    if mask_dve is not None:
                        em.wait("scalar", "DVE", mask_dve)
                    em.wait("scalar", "PE", av_mark[par])
                    em.wait("scalar", "DVE", ds_mark[par])
                    for hh in range(2):
                        if causal:
                            bi = bias_idx[(qt, g)]
                            bias_ap = nbmA.ap()[:, (2 * hp + hh) * NB + bi:
                                                (2 * hp + hh) * NB + bi + 1]
                        else:
                            bias_ap = negbA.ap()[:, 2 * hp + hh:
                                                 2 * hp + hh + 1]
                        em.emit("scalar",
                                lambda e, hh=hh, par=par, bi_=bias_ap:
                                e.activation(
                                    out=pex(hh, par),
                                    in_=ps_s[hh][par].ap(), func=AF.Exp,
                                    bias=bi_, scale=0.125),
                                inc="ACT" if hh == 1 else None)
                    exp_mark[par] = em.cnt["ACT"]
                    em.wait("vector", "ACT", exp_mark[par])
                    for hh in range(2):
                        if idx == 0:
                            dd = em.emit("vector", lambda e, hh=hh, par=par:
                                         e.tensor_copy(out=dsum(hh),
                                                       in_=pex(hh, par)))
                        else:
                            dd = em.emit("vector", lambda e, hh=hh, par=par:
                                         e.tensor_add(dsum(hh), dsum(hh),
                                                      pex(hh, par)))
                    ds_mark[par] = dd
                    if "noshift" in DBG:
                        emit_av(idx, s, gg, par)
                if "noshift" not in DBG:
                    emit_av(*pend)
                last_pe = em.cnt["PE"]
                prev_last = last_pe
                nhp, nqt = (hp, qt + 1) if qt + 1 < NQT else (hp + 1, 0)
                if nhp < HP:
                    em.wait("sync", "PE", last_pe)
                    qld[(nhp, nqt)] = em.dma(
                        "sync", qti(nqt % 2),
                        q_sc[128 * nhp:128 * (nhp + 1),
                             QT * nqt:QT * (nqt + 1)], sem="LQ")
                em.wait("tensor", "DVE", ds_mark[(len(klist) - 1) % 2])
                em.wait("tensor", "ACT", em.cnt["ACT"])
                for hh in range(2):
                    em.mm(lambda e, hh=hh: e.matmul(
                        ps_rep.ap()[32 * hh:32 * hh + 1, :], ones128.ap(),
                        dsum(hh), start=True, stop=True,
                        tile_position=(0, 32 * hh)), inc=(hh == 1))
                dn_pe = em.cnt["PE"]
                em.wait("vector", "PE", dn_pe)
                rc = em.emit("vector", lambda e: e.reciprocal(
                    out=recip.ap()[:, 0:QT], in_=ps_rep.ap()[0:1, :]))
                rc = em.emit("vector", lambda e: e.reciprocal(
                    out=recip.ap()[:, QT:2 * QT], in_=ps_rep.ap()[32:33, :]))
                em.wait("tensor", "DVE", rc)
                for hh in range(2):
                    em.mm(lambda e, hh=hh: e.matmul(
                        ps_rep.ap()[64 * hh:64 * (hh + 1), :], ones64.ap(),
                        recip.ap()[:, QT * hh:QT * (hh + 1)],
                        start=True, stop=True,
                        tile_position=(0, 64 * hh)), inc=(hh == 1))
                rp_pe = em.cnt["PE"]
                em.wait("scalar", "PE", rp_pe)
                gv = em.emit("scalar", lambda e: e.copy(
                    out=repS.ap(), in_=ps_rep.ap()))
                em.wait("vector", "ACT", gv)
                dst = ar[:, NT * hp + QT * qt:NT * hp + QT * (qt + 1)]
                mu = em.emit("vector", lambda e, dst=dst, parq=parq:
                             e.tensor_tensor(out=dst, in0=ps_av[parq].ap(),
                                             in1=repS.ap(), op=ALU.mult))
                avfree[parq] = mu

    def matout_phase(bk):
        lw = 0
        for dc in range(FC):
            lw = em.dma("sync", wp(dc), w_o[bk][dc], sem="LA1")
        ldb = em.dma("sync", pf.ap(),
                     b_o[bk].unsqueeze(0).broadcast_to([128, D]))
        em.wait("tensor", "LA1", lw)
        evac_at = {}
        for tc in range(NTT):
            for og in range(NOG):
                bank = ps_s[0][og]
                em.wait("tensor", "DVE", evac_at.get(og, 0))
                for dc in range(FC):
                    a_st = ar[:, NT * dc + 128 * tc:NT * dc + 128 * (tc + 1)]
                    wv = wp(dc)[:, QT * og:QT * (og + 1)]
                    em.mm(lambda e, b=bank, a=a_st, w=wv, dc=dc: e.matmul(
                        b.ap(), a, w, start=(dc == 0), stop=(dc == FC - 1)),
                        inc=(dc == FC - 1))
                pe_mm = em.cnt["PE"]
                em.wait("vector", "PE", pe_mm)
                em.wait("vector", "LD", ldb)
                c1 = em.emit("vector", lambda e, b=bank, og=og: e.tensor_add(
                    xd.ap()[:, 0:QT], b.ap(),
                    pf.ap()[:, QT * og:QT * (og + 1)]))
                em.wait("vector", "DVE", c1)
                ra = rsb.ap()[:, D * tc + QT * og:D * tc + QT * (og + 1)]
                ev = em.emit("vector", lambda e, ra=ra: e.tensor_add(
                    ra, ra, xd.ap()[:, 0:QT]))
                evac_at[og] = ev

    def ffn_stage():
        lw = 0
        for ic in range(FC):
            lw = em.dma("sync", ar[:, D * ic:D * (ic + 1)],
                        w_ffn_t[128 * ic:128 * (ic + 1), :], sem="LA2")
        em.wait("vector", "LA2", lw)
        for ic in range(FC):
            em.emit("vector", lambda e, ic=ic: e.tensor_copy(
                out=wf8(ic), in_=ar[:, D * ic:D * (ic + 1)]))

    def ffn_phase():
        ldb = em.dma("sync", pf.ap(), b_ffn.unsqueeze(0).broadcast_to([128, D]))
        evac_at = {}
        for tc in range(NTT):
            for og in range(NOG):
                bank = ps_s[0][og]
                em.wait("tensor", "DVE", evac_at.get(og, 0))
                for ic in range(FC):
                    xa = xq8c(ic)[:, 128 * tc:128 * (tc + 1)]
                    wv = wf8(ic)[:, QT * og:QT * (og + 1)]
                    em.mm(lambda e, b=bank, x=xa, w=wv, ic=ic: e.matmul(
                        b.ap(), x, w, start=(ic == 0), stop=(ic == FC - 1)),
                        inc=(ic == FC - 1))
                pe_mm = em.cnt["PE"]
                em.wait("vector", "PE", pe_mm)
                em.wait("vector", "LD", ldb)
                c1 = em.emit("vector", lambda e, b=bank, og=og:
                             e.scalar_tensor_tensor(
                                 out=xd.ap()[:, 0:QT], in0=b.ap(),
                                 scalar=scvec.ap()[:, 3:4],
                                 in1=pf.ap()[:, QT * og:QT * (og + 1)],
                                 op0=ALU.mult, op1=ALU.add))
                em.wait("vector", "DVE", c1)
                ra = rsb.ap()[:, D * tc + QT * og:D * tc + QT * (og + 1)]
                ev = em.emit("vector", lambda e, ra=ra: e.tensor_add(
                    ra, ra, xd.ap()[:, 0:QT]))
                evac_at[og] = ev

    def attn_sublayer(bk):
        ln_phase("attn")
        em.barrier()
        qkv_phase(bk)
        collective_phase()
        em.barrier()
        attn_phase(bk)
        em.barrier()
        matout_phase(bk)
        em.barrier()

    def ffn_sublayer():
        ffn_stage()
        ln_phase("quant")
        em.barrier()
        ffn_phase()
        em.barrier()

    if prefix in ("lnq8", "ffnq8"):
        if prefix == "ffnq8":
            attn_sublayer("a")
        ffn_stage()
        ln_phase("quant")
        em.barrier()
        for j in range(6):
            em.dma("scalar", y_out[128 * j:128 * (j + 1), :],
                   st_[:, 1024 * j:1024 * (j + 1)])
        em.wait("scalar", "SW", em.cnt["SW"])
        em.wait("gpsimd", "SW", em.cnt["SW"])
        with nc.Block() as block:
            em.flush(block)
        nc._es.close()
        return nc
    if prefix is not None:
        ln_phase("attn")
        em.barrier()
        if prefix == "lnxT":
            for j in range(NTT):
                em.dma("scalar", y_out[128 * j:128 * (j + 1), :],
                       ar[:, 1024 * j:1024 * (j + 1)])
        if prefix == "ln":
            pass
        if prefix in ("qkv", "coll", "attn", "matout"):
            qkv_phase("a")
            em.barrier()
        if prefix in ("coll", "attn", "matout"):
            collective_phase()
            em.barrier()
        if prefix in ("attn", "matout", "attnA"):
            attn_phase("a")
            em.barrier()
        if prefix == "attnA":
            for j in range(NTT):
                em.dma("scalar", y_out[128 * j:128 * (j + 1), :],
                       ar[:, 1024 * j:1024 * (j + 1)])
        if prefix == "matout":
            matout_phase("a")
            em.barrier()
    else:
        subs = [lambda: attn_sublayer("a"), ffn_sublayer,
                lambda: attn_sublayer("b"), ffn_sublayer]
        for i in range(n_sub):
            subs[i]()
    if prefix not in ("lnxT", "attnA"):
        for t in range(NTT):
            em.dma("scalar", y_out[128 * t:128 * (t + 1), :],
                   rsb.ap()[:, D * t:D * (t + 1)])
    em.wait("scalar", "SW", em.cnt["SW"])
    em.wait("gpsimd", "SW", em.cnt["SW"])

    with nc.Block() as block:
        em.flush(block)
    nc._es.close()
    return nc


def _prep_inputs(T_, L0_attn_a_x, L0_attn_a_qkv_w, L0_attn_a_qkv_b,
                 L0_attn_a_proj_w, L0_attn_a_proj_b, L0_attn_b_qkv_w,
                 L0_attn_b_qkv_b, L0_attn_b_out_w, L0_attn_b_out_b,
                 L0_ffn_weight_fp8, L0_ffn_bias, L0_ffn_input_scale,
                 L0_ffn_weight_scale):
    NT, NTT, NQT, QT, NKC, KCS, HP, FC = shapes(T_)
    _, _, _, NB, NM, bias_vals, mask_keys = causal_tables(T_)
    x = np.ascontiguousarray(np.asarray(L0_attn_a_x, dtype=np.float32))
    wb = np.asarray(L0_attn_b_qkv_w, dtype=np.float32)
    bb = np.asarray(L0_attn_b_qkv_b, dtype=np.float32)
    perm = np.zeros(3 * D, dtype=np.int64)
    for h in range(NH):
        for j in range(DH):
            perm[h * DH + j] = h * 3 * DH + j
            perm[D + h * DH + j] = h * 3 * DH + DH + j
            perm[2 * D + h * DH + j] = h * 3 * DH + 2 * DH + j
    wb_g = np.ascontiguousarray(wb[:, perm])
    bb_g = np.ascontiguousarray(bb[perm])

    def lay_qkv(w):
        return np.ascontiguousarray(
            w.reshape(FC, 128, 3 * FC, 128).transpose(2, 1, 0, 3)
            .reshape(3 * FC, 128, D))

    def lay_o(w):
        return np.ascontiguousarray(
            np.asarray(w, dtype=np.float32).reshape(FC, 128, D))

    w_ffn_t = np.ascontiguousarray(
        np.asarray(L0_ffn_weight_fp8, dtype=np.float32).T)
    identF = np.eye(128, dtype=np.float32)
    identB = identF.astype(BF16NP)
    otop = np.zeros((128, 128), np.float32)
    otop[0:64, :] = 1.0
    obot = np.zeros((128, 128), np.float32)
    obot[64:128, :] = 1.0
    rcos, rsin = host_rope()
    permsw = np.zeros((128, 128), np.float32)
    for p in range(128):
        partner = p + 32 if (p % 64) < 32 else p - 32
        permsw[partner, p] = 1.0
    nb = x.shape[0]
    wq_a = lay_qkv(np.asarray(L0_attn_a_qkv_w, dtype=np.float32))
    wq_b = lay_qkv(wb_g)
    in_maps = []
    for c in range(2 * nb):
        b, h = c // 2, c % 2
        btabv = np.tile(np.asarray(bias_vals[h], np.float32)[None, :],
                        (128, 1))
        in_maps.append(dict(
            x_in=np.ascontiguousarray(x[b, NT * h:NT * (h + 1), :]),
            w_qkv_a=wq_a,
            b_qkv_a=np.ascontiguousarray(np.asarray(
                L0_attn_a_qkv_b, dtype=np.float32).reshape(-1, 128).T),
            w_proj_a=lay_o(L0_attn_a_proj_w),
            b_proj_a=np.asarray(L0_attn_a_proj_b, dtype=np.float32),
            w_qkv_b=wq_b,
            b_qkv_b=np.ascontiguousarray(bb_g.reshape(-1, 128).T),
            w_out_b=lay_o(L0_attn_b_out_w),
            b_out_b=np.asarray(L0_attn_b_out_b, dtype=np.float32),
            w_ffn_t=w_ffn_t,
            b_ffn=np.asarray(L0_ffn_bias, dtype=np.float32),
            s_in=np.asarray(L0_ffn_input_scale, dtype=np.float32),
            s_w=np.asarray(L0_ffn_weight_scale, dtype=np.float32),
            identf=identF, identb=identB,
            otop=otop.astype(BF16NP), obot=obot.astype(BF16NP),
            btab=btabv, mtab=host_masktab(h, mask_keys, QT),
            rcos=rcos, rsin=rsin, permsw=permsw,
        ))
    return in_maps


def kernel(**inputs):
    NT = shapes(T)[0]
    in_maps = _prep_inputs(T, **inputs)
    nc = build_program(T)
    res = run_bass_kernel_spmd(nc, in_maps, core_ids=list(range(NCORES)))
    out = np.empty((B, T, D), dtype=np.float32)
    for c in range(NCORES):
        b, h = c // 2, c % 2
        out[b, NT * h:NT * (h + 1), :] = res.results[c]["y"]
    return out


# version 15
# speedup vs baseline: 3.2292x; 1.2422x over previous
import numpy as np
import ml_dtypes
import concourse.bass as bass
import concourse.mybir as mybir
from concourse.bass_utils import run_bass_kernel_spmd

F32 = mybir.dt.float32
BF16 = mybir.dt.bfloat16
FP8 = mybir.dt.float8e4
AF = mybir.ActivationFunctionType
ALU = mybir.AluOpType
BF16NP = ml_dtypes.bfloat16

B, T, D, NH, DH = 4, 4096, 1024, 16, 64
DBG = set()
NCORES = 8
MASKVAL = -30000.0
BSAFE = 1.03


def shapes(T_):
    NT = T_ // 2
    NTT = NT // 128
    NQT = max(1, NT // 512)
    QT = NT // NQT
    NKC = T_ // 128
    KCS = NKC // 2
    HP = NH // 2
    FC = D // 128
    return NT, NTT, NQT, QT, NKC, KCS, HP, FC


def causal_tables(T_):
    NT, NTT, NQT, QT, NKC, KCS, HP, FC = shapes(T_)
    KPB = QT // 128

    def qpos(h, j):
        return QT * (2 * j + h)

    def kpos(s, gg):
        return QT * (2 * (gg // KPB) + s) + 128 * (gg % KPB)

    def classify(h, j, g):
        d = qpos(h, j) - kpos(g // KCS, g % KCS)
        if d >= 127:
            return ("none", 0)
        if d + (QT - 1) < 0:
            return ("all", 0)
        return ("part", d)

    kc_lists, bias_idx, mask_idx = [], {}, {}
    bias_cls, mask_cls = {}, {}
    for j in range(NQT):
        lst = []
        for g in range(NKC):
            cl = [classify(h, j, g) for h in range(2)]
            if cl[0][0] == "all" and cl[1][0] == "all":
                continue
            lst.append(g)
            bkey = (cl[0][0] == "all", cl[1][0] == "all")
            bias_idx[(j, g)] = bias_cls.setdefault(bkey, len(bias_cls))
            if "part" in (cl[0][0], cl[1][0]):
                mkey = (cl[0][1] if cl[0][0] == "part" else None,
                        cl[1][1] if cl[1][0] == "part" else None)
                mask_idx[(j, g)] = mask_cls.setdefault(mkey, len(mask_cls))
            else:
                mask_idx[(j, g)] = None
        kc_lists.append(lst)
    n_bias = max(1, len(bias_cls))
    n_mask = max(1, len(mask_cls))
    bias_vals = [[0.0] * n_bias for _ in range(2)]
    for (a0, a1), i in bias_cls.items():
        bias_vals[0][i] = MASKVAL if a0 else 0.0
        bias_vals[1][i] = MASKVAL if a1 else 0.0
    mask_keys = [None] * n_mask
    for k, i in mask_cls.items():
        mask_keys[i] = k
    return kc_lists, bias_idx, mask_idx, n_bias, n_mask, bias_vals, mask_keys


def host_masktab(h, mask_keys, QT):
    n = len(mask_keys)
    out = np.zeros((128, n * QT), dtype=np.float32)
    for i, key in enumerate(mask_keys):
        if key is None:
            continue
        d = key[h]
        if d is None:
            continue
        p = np.arange(128)[:, None]
        c = np.arange(QT)[None, :]
        out[:, i * QT:(i + 1) * QT] = np.where(p <= d + c, 0.0, MASKVAL)
    return np.ascontiguousarray(out.astype(BF16NP))


def host_rope():
    HPn = NH // 2
    inv = (np.float32(1.0) / (np.float32(10000.0) **
           (np.arange(0, DH, 2, dtype=np.float32) / np.float32(DH))))
    t = np.arange(2048, dtype=np.float32)
    fr = (t[:, None] * inv[None, :]).astype(np.float32)
    cos = np.cos(fr).astype(np.float32)[:NH]
    sin = np.sin(fr).astype(np.float32)[:NH]
    cosv = np.zeros((128, HPn), np.float32)
    sinv2 = np.zeros((128, HPn), np.float32)
    for hp in range(HPn):
        for p in range(128):
            head = 2 * hp + p // 64
            r = p % 64
            cosv[p, hp] = cos[head][r % 32]
            sinv2[p, hp] = -sin[head][r] if r < 32 else sin[head][r - 32]
    return cosv, sinv2


ENGS = ("sync", "tensor", "scalar", "vector", "gpsimd")
ESEM = {"tensor": "PE", "scalar": "ACT", "vector": "DVE", "gpsimd": "GP"}


class Em:
    def __init__(self, nc):
        self.nc = nc
        self.ops = {e: [] for e in ENGS}
        self.sems = {}
        self.cnt = {}
        for name in ("LD", "SW", "PE", "ACT", "DVE", "GP", "CC",
                     "LW0", "LW1", "LQ", "LK", "LA0", "LA1", "LA2", "LA3"):
            self.sems[name] = nc._es.enter_context(nc.semaphore(name))
            self.cnt[name] = 0

    def wait(self, eng, sem, val):
        if val is None or val <= 0:
            return
        s = self.sems[sem]
        self.ops[eng].append(lambda e: e.wait_ge(s, val))

    def emit(self, eng, fn, inc="auto", amt=1):
        if inc == "auto":
            inc = ESEM.get(eng)
        if inc is None:
            self.ops[eng].append(fn)
            return self.cnt.get(ESEM.get(eng, "PE"), 0)
        s = self.sems[inc]
        self.cnt[inc] += amt

        def run(e, fn=fn, s=s, amt=amt):
            fn(e).then_inc(s, amt)
        self.ops[eng].append(run)
        return self.cnt[inc]

    def mm(self, fn, inc=False):
        return self.emit("tensor", fn, inc="PE" if inc else None)

    def dma(self, eng, out, in_, sem=None):
        sem = sem or ("SW" if eng in ("gpsimd", "scalar") else "LD")
        return self.emit(eng, lambda e: e.dma_start(out=out, in_=in_),
                         inc=sem, amt=16)

    def barrier(self):
        marks = dict(self.cnt)
        for eng in ENGS:
            for sem in marks:
                self.wait(eng, sem, marks[sem])

    def flush(self, block):
        for eng in ENGS:
            ops = self.ops[eng]
            getattr(block, eng)(lambda e, ops=ops: [f(e) for f in ops])


def build_program(T_, n_sub=4, prefix=None):
    NT, NTT, NQT, QT, NKC, KCS, HP, FC = shapes(T_)
    kc_lists, bias_idx, mask_idx, NB, NM, _, _ = causal_tables(T_)
    NOC = 3 * FC
    NOG = D // QT

    import contextlib
    nc = bass.Bass("TRN2", target_bir_lowering=False, debug=False)
    nc._es = contextlib.ExitStack()

    def din(name, shp, dt=F32):
        return nc.dram_tensor(name, shp, dt, kind="ExternalInput").ap()

    x_in = din("x_in", [NT, D])
    w_qkv = {"a": din("w_qkv_a", [NOC, 128, D]),
             "b": din("w_qkv_b", [NOC, 128, D])}
    b_qkv = {"a": din("b_qkv_a", [128, NOC]), "b": din("b_qkv_b", [128, NOC])}
    w_o = {"a": din("w_proj_a", [FC, 128, D]), "b": din("w_out_b", [FC, 128, D])}
    b_o = {"a": din("b_proj_a", [D]), "b": din("b_out_b", [D])}
    w_ffn_t = din("w_ffn_t", [D, D])
    b_ffn = din("b_ffn", [D])
    s_in = din("s_in", [1])
    s_w = din("s_w", [1])
    identF_in = din("identf", [128, 128])
    identB_in = din("identb", [128, 128], BF16)
    otop_in = din("otop", [128, 128], BF16)
    obot_in = din("obot", [128, 128], BF16)
    btab_in = din("btab", [128, NB])
    mtab_in = din("mtab", [128, NM * QT], BF16)
    perm_in = din("permsw", [128, 128])
    rcos_in = din("rcos", [128, HP])
    rsin_in = din("rsin", [128, HP])
    y_out = nc.dram_tensor("y", [NT, D], F32, kind="ExternalOutput").ap()

    def dsc(name, shp, dt=F32):
        return nc.dram_tensor(name, shp, dt, kind="Internal").ap()

    q_sc = dsc("q_sc", [D, NT])
    kt_own = dsc("kt_own", [D, NT])
    v_own = dsc("v_own", [HP, 128, KCS * 128])
    kmax_own = dsc("kmax_own", [128, HP])
    kt_all = dsc("kt_all", [2, D, NT])
    v_all = dsc("v_all", [2, HP, 128, KCS * 128])
    kmax_all = dsc("kmax_all", [2, 128, HP])

    sbuf = nc.alloc_sbuf_tensor
    rsb = sbuf("rsb", [128, NTT * D], F32)
    arena = sbuf("arena", [128, FC * NT], F32)
    ar = arena.ap()
    stream = sbuf("stream", [128, 8192], F32)
    st_ = stream.ap()
    st8 = stream.ap().bitcast(FP8)

    def wring(r):
        return st_[:, r * D:(r + 1) * D]

    def kb(s):
        return st_[:, s * NT:(s + 1) * NT]

    def vb(s):
        return st_[:, 2 * NT + s * NT:2 * NT + (s + 1) * NT]

    def wp(dc):
        return st_[:, dc * D:(dc + 1) * D]

    def xq8c(ic):
        return st8[:, ic * NT:(ic + 1) * NT]

    def wf8(ic):
        return st8[:, FC * NT + ic * D:FC * NT + (ic + 1) * D]

    xd = sbuf("xd", [128, D], F32)
    xn = sbuf("xn", [128, D], F32)
    scr = sbuf("scr", [128, 3 * D // 2], F32)
    sc = scr.ap()
    scr8 = scr.ap().bitcast(FP8)
    scrB = scr.ap().bitcast(BF16)
    pexp = sbuf("pexp", [128, 4 * QT], F32)

    def pex(hh, par):
        return pexp.ap()[:, (2 * hh + par) * QT:(2 * hh + par + 1) * QT]

    dsumb = sbuf("dsumb", [128, 2 * QT], F32)

    def dsum(hh):
        return dsumb.ap()[:, hh * QT:(hh + 1) * QT]

    pf = sbuf("pf", [128, D], F32)

    def qti(par):
        return pf.ap()[:, par * QT:(par + 1) * QT]
    stat = sbuf("stat", [128, 16], F32)
    sq = sbuf("sq", [128, 4 * HP], F32)
    recip = sbuf("recips", [1, 2 * QT], F32)
    repS = sbuf("repS", [128, QT], F32)
    msk = sbuf("msk", [128, NM * QT], BF16)
    btab = sbuf("btabs", [128, NB], F32)
    nbmA = sbuf("nbmA", [128, 2 * HP * NB], F32)
    negbA = sbuf("negbA", [128, 2 * HP], F32)
    qmax = sbuf("qmax", [128, HP], F32)
    kmaxs = sbuf("kmaxs", [128, HP], F32)
    kmtmp = sbuf("kmtmp", [128, 2 * HP], F32)
    dmsqA = sbuf("dmsqA", [128, 2 * HP], BF16)
    otop = sbuf("otops", [128, 128], BF16)
    obot = sbuf("obots", [128, 128], BF16)
    identF = sbuf("identFs", [128, 128], F32)
    identB = sbuf("identBs", [128, 128], BF16)
    permS = sbuf("permS", [128, 128], F32)
    rcos = sbuf("rcoss", [128, HP], F32)
    rsin = sbuf("rsins", [128, HP], F32)
    ones128 = sbuf("ones128", [128, 1], F32)
    ones64 = sbuf("ones64", [1, 64], F32)
    scvec = sbuf("scvec", [128, 4], F32)
    bqv = sbuf("bqv", [128, NOC], F32)

    pt = nc.alloc_psum_tensor
    ps_s = [[pt(f"ps_s{hh}{p}", [128, QT], F32) for p in range(2)]
            for hh in range(2)]
    ps_av = [pt(f"ps_av{p}", [128, QT], F32) for p in range(2)]
    ps_rep = pt("ps_rep", [128, QT], F32)
    ps_tr = pt("ps_tr", [128, 1024], BF16)

    em = Em(nc)

    for t in range(NTT):
        em.dma("sync", rsb.ap()[:, D * t:D * (t + 1)],
               x_in[128 * t:128 * (t + 1), :])
    for dst, src in ((identF, identF_in), (identB, identB_in),
                     (otop, otop_in), (obot, obot_in), (btab, btab_in),
                     (msk, mtab_in), (rcos, rcos_in), (rsin, rsin_in),
                     (permS, perm_in)):
        em.dma("sync", dst.ap(), src)
    em.dma("sync", scvec.ap()[:, 0:1],
           s_in.unsqueeze(0).broadcast_to([128, 1]))
    em.dma("sync", scvec.ap()[:, 1:2],
           s_w.unsqueeze(0).broadcast_to([128, 1]))
    ld0 = em.cnt["LD"]
    em.wait("vector", "LD", ld0)
    c = em.emit("vector", lambda e: e.reciprocal(
        out=scvec.ap()[:, 2:3], in_=scvec.ap()[:, 0:1]))
    em.wait("vector", "DVE", c)
    em.emit("vector", lambda e: e.tensor_mul(
        scvec.ap()[:, 3:4], scvec.ap()[:, 0:1], scvec.ap()[:, 1:2]))
    em.emit("vector", lambda e: e.memset(ones128.ap(), 1.0))
    em.emit("vector", lambda e: e.memset(ones64.ap(), 1.0))
    em.barrier()

    def ln_phase(mode, pe_gate=0, dve_gate=0):
        tr_mark = {0: 0, 1: 0}
        gp_mark = {0: 0, 1: 0}
        sq_mark = {0: 0, 1: 0}
        st_mark = {0: 0, 1: 0}
        evac_mark = em.cnt["ACT"]
        tile_marks = []
        if pe_gate or dve_gate:
            em.wait("scalar", "PE", pe_gate)
            em.wait("scalar", "DVE", dve_gate)
        for t in range(NTT):
            p = t % 2
            rt = rsb.ap()[:, D * t:D * (t + 1)]
            xdp = xd.ap() if p == 0 else pexp.ap()[:, 0:D]
            xnp = xn.ap() if p == 0 else pexp.ap()[:, D:2 * D]
            stp = stat.ap()[:, 8 * p:8 * (p + 1)]
            sqt = sc[:, 0:D] if p == 0 else dsumb.ap()[:, 0:D]
            c0 = em.emit("vector", lambda e, rt=rt, stp=stp: e.reduce_sum(
                stp[:, 0:1], rt, axis=mybir.AxisListType.X))
            em.wait("vector", "DVE", c0)
            c0b = em.emit("vector", lambda e, stp=stp: e.tensor_scalar_mul(
                stp[:, 1:2], stp[:, 0:1], -1.0 / D))
            em.wait("vector", "DVE", c0b)
            em.wait("vector", "PE", tr_mark[p])
            em.wait("vector", "GP", gp_mark[p])
            c1 = em.emit("vector", lambda e, rt=rt, xdp=xdp, stp=stp:
                         e.tensor_scalar_add(xdp, rt, stp[:, 1:2]))
            em.wait("scalar", "DVE", c1)
            em.wait("scalar", "DVE", sq_mark[p])
            c2 = em.emit("scalar", lambda e, xdp=xdp, sqt=sqt: e.activation(
                out=sqt, in_=xdp, func=AF.Square))
            em.wait("vector", "ACT", c2)
            c3 = em.emit("vector", lambda e, sqt=sqt, stp=stp: e.reduce_sum(
                stp[:, 2:3], sqt, axis=mybir.AxisListType.X))
            sq_mark[p] = c3
            em.wait("vector", "DVE", c3)
            em.wait("vector", "ACT", st_mark[p])
            c3b = em.emit("vector", lambda e, stp=stp: e.tensor_scalar(
                out=stp[:, 5:6], in0=stp[:, 2:3],
                scalar1=1.0 / D, scalar2=1e-5, op0=ALU.mult, op1=ALU.add))
            em.wait("scalar", "DVE", c3b)
            c4 = em.emit("scalar", lambda e, stp=stp: e.activation(
                out=stp[:, 3:4], in_=stp[:, 5:6], func=AF.Sqrt))
            st_mark[p] = c4
            em.wait("vector", "ACT", c4)
            c4b = em.emit("vector", lambda e, stp=stp: e.reciprocal(
                out=stp[:, 6:7], in_=stp[:, 3:4]))
            em.wait("vector", "DVE", c4b)
            c5 = em.emit("vector", lambda e, xdp=xdp, xnp=xnp, stp=stp:
                         e.tensor_scalar_mul(xnp, xdp, stp[:, 6:7]))
            if mode == "attn":
                em.wait("tensor", "DVE", c5)
                for half in range(2):
                    em.wait("tensor", "ACT", evac_mark)
                    for i in range(4):
                        f = 4 * half + i
                        em.mm(lambda e, f=f, i=i, xnp=xnp: e.transpose(
                            ps_rep.ap()[:, 128 * i:128 * (i + 1)],
                            xnp[:, 128 * f:128 * (f + 1)], identF.ap()),
                            inc=(i == 3))
                    pe_c = em.cnt["PE"]
                    em.wait("scalar", "PE", pe_c)
                    dst = ar.rearrange("p (f n) -> p f n", f=FC)[
                        :, 4 * half:4 * half + 4, 128 * t:128 * (t + 1)]
                    evac_mark = em.emit("scalar", lambda e, dst=dst: e.copy(
                        out=dst, in_=ps_rep.ap().rearrange(
                            "p (f n) -> p f n", f=4)))
                tr_mark[p] = em.cnt["PE"]
                tile_marks.append(evac_mark)
            else:
                em.wait("vector", "DVE", c5)
                c6a = em.emit("vector", lambda e, xdp=xdp, xnp=xnp:
                              e.tensor_scalar(
                                  out=xdp, in0=xnp, scalar1=scvec.ap()[:, 2:3],
                                  scalar2=448.0, op0=ALU.mult, op1=ALU.min))
                em.wait("vector", "DVE", c6a)
                q8 = scr8[:, (4 + p) * D:(5 + p) * D]
                c6 = em.emit("vector", lambda e, q8=q8, xdp=xdp:
                             e.tensor_single_scalar(out=q8, in_=xdp,
                                                    scalar=-448.0, op=ALU.max))
                em.wait("gpsimd", "DVE", c6)
                em.wait("gpsimd", "PE", tr_mark[p])
                ub = (xd.ap() if p == 0 else pexp.ap()[:, 0:D]).bitcast(
                    BF16)[:, 0:D]
                c7 = em.emit("gpsimd", lambda e, q8=q8, ub=ub:
                             e.tensor_copy(out=ub, in_=q8))
                gp_mark[p] = c7
                em.wait("tensor", "GP", c7)
                em.wait("tensor", "ACT", evac_mark)
                for f in range(FC):
                    em.mm(lambda e, ub=ub, f=f: e.transpose(
                        ps_tr.ap()[:, 128 * f:128 * (f + 1)],
                        ub[:, 128 * f:128 * (f + 1)], identB.ap()),
                        inc=(f == FC - 1))
                tr_mark[p] = em.cnt["PE"]
                em.wait("scalar", "PE", tr_mark[p])
                dst = st8[:, 0:FC * NT].rearrange("p (f n) -> p f n", f=FC)[
                    :, :, 128 * t:128 * (t + 1)]
                evac_mark = em.emit("scalar", lambda e, dst=dst: e.copy(
                    out=dst, in_=ps_tr.ap().rearrange(
                        "p (f n) -> p f n", f=FC)))
                tile_marks.append(evac_mark)

        return tile_marks

    def qkv_phase(bk, oc_sel=None, ln_marks=None, bank_dve0=0):
        rope = bk == "b"
        ldb = em.dma("sync", bqv.ap()[:, 0:NOC], b_qkv[bk])
        em.wait("vector", "LD", ldb)
        zz = em.emit("vector", lambda e: e.memset(qmax.ap(), 0.0))
        zz = em.emit("vector", lambda e: e.memset(kmaxs.ap(), 0.0))
        em.wait("vector", "DVE", zz)
        prev_mm = {}
        evac_at = {}
        rope_done = {}
        oc_list = (list(range(FC, 3 * FC)) + list(range(FC))
                   if oc_sel is None else list(oc_sel))
        for oci, oc in enumerate(oc_list):
            ring = oci % 2
            if oci >= 2:
                em.wait("sync", "PE", prev_mm[oc_list[oci - 2]])
            lw = em.dma("sync", wring(ring), w_qkv[bk][oc], sem=f"LW{ring}")
            em.wait("tensor", f"LW{ring}", lw)
            is_v = oc >= 2 * FC
            hp = oc % FC
            for tt in range(NT // QT):
                par = tt % 2
                bank = ps_s[0][par]
                if ln_marks is not None:
                    em.wait("tensor", "ACT", ln_marks[4 * tt + 3])
                em.wait("tensor", "DVE", evac_at.get(("m", par), bank_dve0))
                for f in range(FC):
                    mv = ar[:, NT * f + QT * tt:NT * f + QT * (tt + 1)]
                    em.mm(lambda e, b=bank, f=f, mv=mv, ring=ring: e.matmul(
                        b.ap(), wring(ring)[:, 128 * f:128 * (f + 1)], mv,
                        start=(f == 0), stop=(f == FC - 1)),
                        inc=(f == FC - 1))
                pe_mm = em.cnt["PE"]
                prev_mm[oc] = pe_mm
                em.wait("vector", "PE", pe_mm)
                em.wait("vector", "SW", em.cnt["SW"])
                a1 = em.emit("vector", lambda e, b=bank, oc=oc, par=par:
                             e.tensor_scalar_add(sc[:, QT * par:QT * (par + 1)],
                                                 b.ap(),
                                                 bqv.ap()[:, oc:oc + 1]))
                evac_at[("m", par)] = a1
                src_f32 = sc[:, QT * par:QT * (par + 1)]
                dep = a1
                if rope and not is_v:
                    rb = ps_s[1][par]
                    em.wait("tensor", "DVE", dep)
                    em.wait("tensor", "DVE", rope_done.get(par, 0))
                    em.mm(lambda e, rb=rb, s=src_f32: e.matmul(
                        rb.ap(), permS.ap(), s, start=True, stop=True),
                        inc=True)
                    rmm = em.cnt["PE"]
                    em.wait("vector", "PE", rmm)
                    r0 = em.emit("vector", lambda e, rb=rb, hp=hp:
                                 e.tensor_scalar_mul(
                                     xd.ap()[:, 0:QT], rb.ap(),
                                     rsin.ap()[:, hp:hp + 1]))
                    rope_done[par] = r0
                    em.wait("vector", "DVE", r0)
                    r1 = em.emit("vector", lambda e, s=src_f32, hp=hp:
                                 e.scalar_tensor_tensor(
                                     out=xn.ap()[:, 0:QT], in0=s,
                                     scalar=rcos.ap()[:, hp:hp + 1],
                                     in1=xd.ap()[:, 0:QT],
                                     op0=ALU.mult, op1=ALU.add))
                    src_f32 = xn.ap()[:, 0:QT]
                    dep = r1
                if not is_v:
                    em.wait("vector", "DVE", dep)
                    m1 = em.emit("vector", lambda e, s=src_f32: e.reduce_max(
                        stat.ap()[:, 8:9], s, axis=mybir.AxisListType.X,
                        apply_absolute_value=True))
                    tgt = qmax if oc < FC else kmaxs
                    em.wait("vector", "DVE", m1)
                    m2 = em.emit("vector", lambda e, tgt=tgt, hp=hp:
                                 e.tensor_tensor(
                                     out=tgt.ap()[:, hp:hp + 1],
                                     in0=tgt.ap()[:, hp:hp + 1],
                                     in1=stat.ap()[:, 8:9], op=ALU.max))
                    dst_t = q_sc if oc < FC else kt_own
                    r0c = 128 * hp
                    em.wait("scalar", "DVE", dep)
                    em.dma("scalar",
                           dst_t[r0c:r0c + 128, QT * tt:QT * (tt + 1)],
                           src_f32)
                else:
                    em.wait("tensor", "DVE", dep)
                    em.wait("tensor", "ACT", em.cnt["ACT"])
                    for i in range(QT // 128):
                        em.mm(lambda e, s=src_f32, i=i: e.transpose(
                            ps_rep.ap()[:, 128 * i:128 * (i + 1)],
                            s[:, 128 * i:128 * (i + 1)], identF.ap()),
                            inc=(i == QT // 128 - 1))
                    pe_tr = em.cnt["PE"]
                    em.wait("scalar", "PE", pe_tr)
                    em.wait("scalar", "SW", em.cnt["SW"])
                    ev = em.emit("scalar", lambda e: e.copy(
                        out=sc[:, D:D + QT], in_=ps_rep.ap()))
                    em.wait("scalar", "ACT", ev)
                    hpv = oc - 2 * FC
                    em.dma("scalar",
                           v_own[hpv][:, QT * tt:QT * (tt + 1)],
                           sc[:, D:D + QT])
                sw_now = em.cnt["SW"]
                em.wait("vector", "SW", sw_now)
                em.wait("scalar", "SW", sw_now)
            if oc_sel is None and oc == 2 * FC - 1 and "latecc" not in DBG:
                em.wait("scalar", "DVE", em.cnt["DVE"])
                em.dma("scalar", kmax_own, kmaxs.ap()[:, 0:HP])
                cc_start(((kt_own, kt_all), (kmax_own, kmax_all)))
            if oc_sel is None and oc == 3 * FC - 1 and "latecc" not in DBG:
                cc_start(((v_own, v_all),), sem="LA3")
        if oc_sel is None and "latecc" in DBG:
            em.wait("scalar", "DVE", em.cnt["DVE"])
            em.dma("scalar", kmax_own, kmaxs.ap()[:, 0:HP])
            cc_start(((kt_own, kt_all), (kmax_own, kmax_all)))
            cc_start(((v_own, v_all),), sem="LA3")

    ccm = {"k": 0, "v": 0}

    def cc_start(pairs, sem="CC"):
        em.wait("gpsimd", "SW", em.cnt["SW"])
        rg = [[0, 1], [2, 3], [4, 5], [6, 7]]
        for i_, o_ in pairs:
            em.emit("gpsimd", lambda e, i_=i_, o_=o_: e.collective_compute(
                "AllGather", ALU.bypass, replica_groups=rg,
                ins=[i_.tensor.ap().opt()], outs=[o_.tensor.ap().opt()]),
                inc=sem)
        ccm["k" if sem == "CC" else "v"] = em.cnt[sem]

    def collective_phase():
        cc = ccm["k"]
        em.wait("sync", "CC", cc)
        l1 = em.dma("sync", kmtmp.ap()[:, 0:HP], kmax_all[0])
        l1 = em.dma("sync", kmtmp.ap()[:, HP:2 * HP], kmax_all[1])
        em.wait("vector", "LD", l1)
        em.wait("vector", "CC", cc)
        c = em.emit("vector", lambda e: e.tensor_max(
            kmaxs.ap(), kmtmp.ap()[:, 0:HP], kmtmp.ap()[:, HP:2 * HP]))
        em.wait("vector", "DVE", c)
        c = em.emit("vector", lambda e: e.tensor_mul(
            qmax.ap(), qmax.ap(), qmax.ap()))
        em.wait("vector", "DVE", c)
        c = em.emit("vector", lambda e: e.tensor_mul(
            kmaxs.ap(), kmaxs.ap(), kmaxs.ap()))
        em.wait("vector", "DVE", c)

    def attn_phase(bk):
        causal = bk == "a"
        em.wait("sync", "SW", em.cnt["SW"])
        em.wait("sync", "DVE", em.cnt["DVE"])
        c = em.emit("vector", lambda e: e.tensor_copy(
            out=dmsqA.ap()[:, 0:HP], in_=qmax.ap()))
        c = em.emit("vector", lambda e: e.tensor_copy(
            out=dmsqA.ap()[:, HP:2 * HP], in_=kmaxs.ap()))
        em.wait("tensor", "DVE", c)
        bank0 = ps_s[0][0]
        for hp in range(HP):
            for hh in range(2):
                stt = otop if hh == 0 else obot
                col = 4 * hp + 2 * hh
                em.mm(lambda e, stt=stt, col=col, hp=hp: e.matmul(
                    bank0.ap()[:, col:col + 1], stt.ap(),
                    dmsqA.ap()[:, hp:hp + 1], start=True, stop=True))
                em.mm(lambda e, stt=stt, col=col, hp=hp: e.matmul(
                    bank0.ap()[:, col + 1:col + 2], stt.ap(),
                    dmsqA.ap()[:, HP + hp:HP + hp + 1],
                    start=True, stop=True),
                    inc=(hp == HP - 1 and hh == 1))
        pe_b = em.cnt["PE"]
        em.wait("scalar", "PE", pe_b)
        aq = em.emit("scalar", lambda e: e.activation(
            out=sq.ap(), in_=bank0.ap()[:, 0:4 * HP], func=AF.Sqrt))
        em.wait("vector", "ACT", aq)
        sqv = sq.ap().rearrange("p (i two) -> p i two", two=2)
        nb = em.emit("vector", lambda e: e.tensor_tensor(
            out=negbA.ap(), in0=sqv[:, :, 0], in1=sqv[:, :, 1], op=ALU.mult))
        em.wait("vector", "DVE", nb)
        nb = em.emit("vector", lambda e: e.tensor_scalar_mul(
            negbA.ap(), negbA.ap(), -BSAFE / 8.0))
        if causal:
            em.wait("vector", "DVE", nb)
            for i in range(2 * HP):
                nb = em.emit("vector", lambda e, i=i: e.tensor_scalar_add(
                    nbmA.ap()[:, i * NB:(i + 1) * NB], btab.ap(),
                    negbA.ap()[:, i:i + 1]))
        em.wait("scalar", "DVE", nb)
        em.wait("tensor", "ACT", em.cnt["ACT"])

        prev_last = 0
        qld = {}
        avfree = {}
        exp_mark = {0: 0, 1: 0}
        av_mark = {0: 0, 1: 0}
        ds_mark = {0: 0, 1: 0}
        kv_sem = ("LK", "LA0")
        for hp in range(HP):
            em.wait("sync", "PE", prev_last)
            if hp == 0:
                em.wait("sync", "CC", ccm["k"])
                em.wait("sync", "LA3", ccm["v"])
            kvm = {}
            for s in range(2):
                em.dma("sync", vb(s), v_all[s, hp], sem=kv_sem[s])
                kvm[s] = em.dma("sync", kb(s),
                                kt_all[s, 128 * hp:128 * (hp + 1), :],
                                sem=kv_sem[s])
            for qt in range(NQT):
                parq = qt % 2
                if (hp, qt) not in qld:
                    lq = em.dma("sync", qti(parq),
                                q_sc[128 * hp:128 * (hp + 1),
                                     QT * qt:QT * (qt + 1)], sem="LQ")
                    qld[(hp, qt)] = lq
                em.wait("tensor", "LQ", qld[(hp, qt)])
                klist = kc_lists[qt] if causal else list(range(NKC))
                nkl = len(klist)

                def emit_av(idx, s, gg, par):
                    em.wait("tensor", "ACT", exp_mark[par])
                    if idx == 0:
                        em.wait("tensor", "DVE", avfree.get(parq, 0))
                    first, last = idx == 0, idx == nkl - 1
                    for hh in range(2):
                        vs = vb(s)[:, 128 * gg + 64 * hh:
                                   128 * gg + 64 * hh + 64]
                        em.mm(lambda e, vs=vs, hh=hh, par=par, pq=parq,
                              fi=first, la=last: e.matmul(
                                  ps_av[pq].ap()[64 * hh:64 * (hh + 1), :],
                                  vs, pex(hh, par), start=fi, stop=la,
                                  tile_position=(0, 64 * hh)),
                              inc=(hh == 1))
                    av_mark[par] = em.cnt["PE"]

                pend = None
                src_waited = set()
                for idx, g in enumerate(klist):
                    par = idx % 2
                    s, gg = g // KCS, g % KCS
                    if s not in src_waited:
                        em.wait("tensor", kv_sem[s], kvm[s])
                        src_waited.add(s)
                    em.wait("tensor", "ACT", exp_mark[par])
                    for hh in range(2):
                        base = 64 * hh
                        o = ps_s[hh][par]
                        stt = kb(s)[base:base + 64, 128 * gg:128 * (gg + 1)]
                        mv = qti(parq)[base:base + 64, 0:QT]
                        em.mm(lambda e, o=o, stt=stt, mv=mv, bb=base: e.matmul(
                            o.ap(), stt, mv, start=True, stop=True,
                            tile_position=(bb, 0)), inc=(hh == 1))
                    qk_pe = em.cnt["PE"]
                    if pend is not None and "noshift" not in DBG:
                        emit_av(*pend)
                    if "noshift" not in DBG:
                        pend = (idx, s, gg, par)
                    mi = mask_idx.get((qt, g)) if causal else None
                    mask_dve = None
                    if mi is not None:
                        em.wait("vector", "PE", qk_pe)
                        for hh in range(2):
                            mask_dve = em.emit(
                                "vector", lambda e, hh=hh, par=par, mi=mi:
                                e.tensor_add(
                                    ps_s[hh][par].ap(), ps_s[hh][par].ap(),
                                    msk.ap()[:, QT * mi:QT * (mi + 1)]))
                    em.wait("scalar", "PE", qk_pe)
                    if mask_dve is not None:
                        em.wait("scalar", "DVE", mask_dve)
                    em.wait("scalar", "PE", av_mark[par])
                    em.wait("scalar", "DVE", ds_mark[par])
                    for hh in range(2):
                        if causal:
                            bi = bias_idx[(qt, g)]
                            bias_ap = nbmA.ap()[:, (2 * hp + hh) * NB + bi:
                                                (2 * hp + hh) * NB + bi + 1]
                        else:
                            bias_ap = negbA.ap()[:, 2 * hp + hh:
                                                 2 * hp + hh + 1]
                        em.emit("scalar",
                                lambda e, hh=hh, par=par, bi_=bias_ap:
                                e.activation(
                                    out=pex(hh, par),
                                    in_=ps_s[hh][par].ap(), func=AF.Exp,
                                    bias=bi_, scale=0.125),
                                inc="ACT" if hh == 1 else None)
                    exp_mark[par] = em.cnt["ACT"]
                    em.wait("vector", "ACT", exp_mark[par])
                    for hh in range(2):
                        if idx == 0:
                            dd = em.emit("vector", lambda e, hh=hh, par=par:
                                         e.tensor_copy(out=dsum(hh),
                                                       in_=pex(hh, par)))
                        else:
                            dd = em.emit("vector", lambda e, hh=hh, par=par:
                                         e.tensor_add(dsum(hh), dsum(hh),
                                                      pex(hh, par)))
                    ds_mark[par] = dd
                    if "noshift" in DBG:
                        emit_av(idx, s, gg, par)
                if "noshift" not in DBG:
                    emit_av(*pend)
                last_pe = em.cnt["PE"]
                prev_last = last_pe
                nhp, nqt = (hp, qt + 1) if qt + 1 < NQT else (hp + 1, 0)
                if nhp < HP:
                    em.wait("sync", "PE", last_pe)
                    qld[(nhp, nqt)] = em.dma(
                        "sync", qti(nqt % 2),
                        q_sc[128 * nhp:128 * (nhp + 1),
                             QT * nqt:QT * (nqt + 1)], sem="LQ")
                em.wait("tensor", "DVE", ds_mark[(len(klist) - 1) % 2])
                em.wait("tensor", "ACT", em.cnt["ACT"])
                for hh in range(2):
                    em.mm(lambda e, hh=hh: e.matmul(
                        ps_rep.ap()[32 * hh:32 * hh + 1, :], ones128.ap(),
                        dsum(hh), start=True, stop=True,
                        tile_position=(0, 32 * hh)), inc=(hh == 1))
                dn_pe = em.cnt["PE"]
                em.wait("vector", "PE", dn_pe)
                rc = em.emit("vector", lambda e: e.reciprocal(
                    out=recip.ap()[:, 0:QT], in_=ps_rep.ap()[0:1, :]))
                rc = em.emit("vector", lambda e: e.reciprocal(
                    out=recip.ap()[:, QT:2 * QT], in_=ps_rep.ap()[32:33, :]))
                em.wait("tensor", "DVE", rc)
                for hh in range(2):
                    em.mm(lambda e, hh=hh: e.matmul(
                        ps_rep.ap()[64 * hh:64 * (hh + 1), :], ones64.ap(),
                        recip.ap()[:, QT * hh:QT * (hh + 1)],
                        start=True, stop=True,
                        tile_position=(0, 64 * hh)), inc=(hh == 1))
                rp_pe = em.cnt["PE"]
                em.wait("scalar", "PE", rp_pe)
                gv = em.emit("scalar", lambda e: e.copy(
                    out=repS.ap(), in_=ps_rep.ap()))
                em.wait("vector", "ACT", gv)
                dst = ar[:, NT * hp + QT * qt:NT * hp + QT * (qt + 1)]
                mu = em.emit("vector", lambda e, dst=dst, parq=parq:
                             e.tensor_tensor(out=dst, in0=ps_av[parq].ap(),
                                             in1=repS.ap(), op=ALU.mult))
                avfree[parq] = mu

    def matout_phase(bk):
        lw = 0
        for dc in range(FC):
            lw = em.dma("sync", wp(dc), w_o[bk][dc], sem="LA1")
        ldb = em.dma("sync", pf.ap(),
                     b_o[bk].unsqueeze(0).broadcast_to([128, D]))
        em.wait("tensor", "LA1", lw)
        evac_at = {}
        for tc in range(NTT):
            for og in range(NOG):
                bank = ps_s[0][og]
                em.wait("tensor", "DVE", evac_at.get(og, 0))
                for dc in range(FC):
                    a_st = ar[:, NT * dc + 128 * tc:NT * dc + 128 * (tc + 1)]
                    wv = wp(dc)[:, QT * og:QT * (og + 1)]
                    em.mm(lambda e, b=bank, a=a_st, w=wv, dc=dc: e.matmul(
                        b.ap(), a, w, start=(dc == 0), stop=(dc == FC - 1)),
                        inc=(dc == FC - 1))
                pe_mm = em.cnt["PE"]
                em.wait("vector", "PE", pe_mm)
                em.wait("vector", "LD", ldb)
                c1 = em.emit("vector", lambda e, b=bank, og=og: e.tensor_add(
                    xd.ap()[:, 0:QT], b.ap(),
                    pf.ap()[:, QT * og:QT * (og + 1)]))
                em.wait("vector", "DVE", c1)
                ra = rsb.ap()[:, D * tc + QT * og:D * tc + QT * (og + 1)]
                ev = em.emit("vector", lambda e, ra=ra: e.tensor_add(
                    ra, ra, xd.ap()[:, 0:QT]))
                evac_at[og] = ev

    def ffn_stage():
        lw = 0
        for ic in range(FC):
            lw = em.dma("sync", ar[:, D * ic:D * (ic + 1)],
                        w_ffn_t[128 * ic:128 * (ic + 1), :], sem="LA2")
        em.wait("vector", "LA2", lw)
        em.wait("vector", "PE", em.cnt["PE"])
        wc = 0
        for ic in range(FC):
            wc = em.emit("vector", lambda e, ic=ic: e.tensor_copy(
                out=wf8(ic), in_=ar[:, D * ic:D * (ic + 1)]))
        return wc

    def ffn_phase(qmarks=None, wc=0):
        ldb = em.dma("sync", pf.ap(), b_ffn.unsqueeze(0).broadcast_to([128, D]))
        evac_at = {}
        for tc in range(NTT):
            for og in range(NOG):
                bank = ps_s[0][og]
                if qmarks is not None and og == 0:
                    em.wait("tensor", "ACT", qmarks[tc])
                em.wait("tensor", "DVE", evac_at.get(og, wc))
                for ic in range(FC):
                    xa = xq8c(ic)[:, 128 * tc:128 * (tc + 1)]
                    wv = wf8(ic)[:, QT * og:QT * (og + 1)]
                    em.mm(lambda e, b=bank, x=xa, w=wv, ic=ic: e.matmul(
                        b.ap(), x, w, start=(ic == 0), stop=(ic == FC - 1)),
                        inc=(ic == FC - 1))
                pe_mm = em.cnt["PE"]
                em.wait("vector", "PE", pe_mm)
                em.wait("vector", "LD", ldb)
                c1 = em.emit("vector", lambda e, b=bank, og=og:
                             e.scalar_tensor_tensor(
                                 out=xd.ap()[:, 0:QT], in0=b.ap(),
                                 scalar=scvec.ap()[:, 3:4],
                                 in1=pf.ap()[:, QT * og:QT * (og + 1)],
                                 op0=ALU.mult, op1=ALU.add))
                em.wait("vector", "DVE", c1)
                ra = rsb.ap()[:, D * tc + QT * og:D * tc + QT * (og + 1)]
                ev = em.emit("vector", lambda e, ra=ra: e.tensor_add(
                    ra, ra, xd.ap()[:, 0:QT]))
                evac_at[og] = ev

    def attn_sublayer(bk):
        dve0 = em.cnt["DVE"]
        act0 = em.cnt["ACT"]
        marks = ln_phase("attn", dve_gate=dve0)
        em.wait("tensor", "ACT", act0)
        qkv_phase(bk, ln_marks=marks, bank_dve0=dve0)
        collective_phase()
        attn_phase(bk)
        em.barrier()
        matout_phase(bk)

    def ffn_sublayer():
        pe0 = em.cnt["PE"]
        em.wait("sync", "PE", pe0)
        wc = ffn_stage()
        qmarks = ln_phase("quant", pe_gate=pe0)
        ffn_phase(qmarks, wc)
        em.barrier()

    if prefix in ("lnq8", "ffnq8"):
        if prefix == "ffnq8":
            attn_sublayer("a")
        ffn_stage()
        ln_phase("quant")
        em.barrier()
        for j in range(6):
            em.dma("scalar", y_out[128 * j:128 * (j + 1), :],
                   st_[:, 1024 * j:1024 * (j + 1)])
        em.wait("scalar", "SW", em.cnt["SW"])
        em.wait("gpsimd", "SW", em.cnt["SW"])
        with nc.Block() as block:
            em.flush(block)
        nc._es.close()
        return nc
    if prefix is not None:
        ln_phase("attn")
        em.barrier()
        if prefix == "lnxT":
            for j in range(NTT):
                em.dma("scalar", y_out[128 * j:128 * (j + 1), :],
                       ar[:, 1024 * j:1024 * (j + 1)])
        if prefix == "ln":
            pass
        if prefix in ("qkv", "coll", "attn", "matout", "attnA"):
            qkv_phase("a")
            em.barrier()
        if prefix in ("coll", "attn", "matout", "attnA"):
            collective_phase()
            em.barrier()
        if prefix in ("attn", "matout", "attnA"):
            attn_phase("a")
            em.barrier()
        if prefix == "attnA":
            for j in range(NTT):
                em.dma("scalar", y_out[128 * j:128 * (j + 1), :],
                       ar[:, 1024 * j:1024 * (j + 1)])
        if prefix == "matout":
            matout_phase("a")
            em.barrier()
    else:
        subs = [lambda: attn_sublayer("a"), ffn_sublayer,
                lambda: attn_sublayer("b"), ffn_sublayer]
        for i in range(n_sub):
            subs[i]()
    if prefix not in ("lnxT", "attnA"):
        em.barrier()
        for t in range(NTT):
            em.dma("scalar", y_out[128 * t:128 * (t + 1), :],
                   rsb.ap()[:, D * t:D * (t + 1)])
    em.wait("scalar", "SW", em.cnt["SW"])
    em.wait("gpsimd", "SW", em.cnt["SW"])

    with nc.Block() as block:
        em.flush(block)
    nc._es.close()
    return nc


def _prep_inputs(T_, L0_attn_a_x, L0_attn_a_qkv_w, L0_attn_a_qkv_b,
                 L0_attn_a_proj_w, L0_attn_a_proj_b, L0_attn_b_qkv_w,
                 L0_attn_b_qkv_b, L0_attn_b_out_w, L0_attn_b_out_b,
                 L0_ffn_weight_fp8, L0_ffn_bias, L0_ffn_input_scale,
                 L0_ffn_weight_scale):
    NT, NTT, NQT, QT, NKC, KCS, HP, FC = shapes(T_)
    _, _, _, NB, NM, bias_vals, mask_keys = causal_tables(T_)
    x = np.ascontiguousarray(np.asarray(L0_attn_a_x, dtype=np.float32))
    wb = np.asarray(L0_attn_b_qkv_w, dtype=np.float32)
    bb = np.asarray(L0_attn_b_qkv_b, dtype=np.float32)
    perm = np.zeros(3 * D, dtype=np.int64)
    for h in range(NH):
        for j in range(DH):
            perm[h * DH + j] = h * 3 * DH + j
            perm[D + h * DH + j] = h * 3 * DH + DH + j
            perm[2 * D + h * DH + j] = h * 3 * DH + 2 * DH + j
    wb_g = np.ascontiguousarray(wb[:, perm])
    bb_g = np.ascontiguousarray(bb[perm])

    def lay_qkv(w):
        return np.ascontiguousarray(
            w.reshape(FC, 128, 3 * FC, 128).transpose(2, 1, 0, 3)
            .reshape(3 * FC, 128, D))

    def lay_o(w):
        return np.ascontiguousarray(
            np.asarray(w, dtype=np.float32).reshape(FC, 128, D))

    w_ffn_t = np.ascontiguousarray(
        np.asarray(L0_ffn_weight_fp8, dtype=np.float32).T)
    identF = np.eye(128, dtype=np.float32)
    identB = identF.astype(BF16NP)
    otop = np.zeros((128, 128), np.float32)
    otop[0:64, :] = 1.0
    obot = np.zeros((128, 128), np.float32)
    obot[64:128, :] = 1.0
    rcos, rsin = host_rope()
    permsw = np.zeros((128, 128), np.float32)
    for p in range(128):
        partner = p + 32 if (p % 64) < 32 else p - 32
        permsw[partner, p] = 1.0
    nb = x.shape[0]
    wq_a = lay_qkv(np.asarray(L0_attn_a_qkv_w, dtype=np.float32))
    wq_b = lay_qkv(wb_g)
    in_maps = []
    for c in range(2 * nb):
        b, h = c // 2, c % 2
        btabv = np.tile(np.asarray(bias_vals[h], np.float32)[None, :],
                        (128, 1))
        in_maps.append(dict(
            x_in=np.ascontiguousarray(np.concatenate(
                [x[b, QT * (2 * j + h):QT * (2 * j + h) + QT, :]
                 for j in range(NQT)], axis=0)),
            w_qkv_a=wq_a,
            b_qkv_a=np.ascontiguousarray(np.asarray(
                L0_attn_a_qkv_b, dtype=np.float32).reshape(-1, 128).T),
            w_proj_a=lay_o(L0_attn_a_proj_w),
            b_proj_a=np.asarray(L0_attn_a_proj_b, dtype=np.float32),
            w_qkv_b=wq_b,
            b_qkv_b=np.ascontiguousarray(bb_g.reshape(-1, 128).T),
            w_out_b=lay_o(L0_attn_b_out_w),
            b_out_b=np.asarray(L0_attn_b_out_b, dtype=np.float32),
            w_ffn_t=w_ffn_t,
            b_ffn=np.asarray(L0_ffn_bias, dtype=np.float32),
            s_in=np.asarray(L0_ffn_input_scale, dtype=np.float32),
            s_w=np.asarray(L0_ffn_weight_scale, dtype=np.float32),
            identf=identF, identb=identB,
            otop=otop.astype(BF16NP), obot=obot.astype(BF16NP),
            btab=btabv, mtab=host_masktab(h, mask_keys, QT),
            rcos=rcos, rsin=rsin, permsw=permsw,
        ))
    return in_maps


def kernel(**inputs):
    NT, NTT, NQT, QT = shapes(T)[:4]
    in_maps = _prep_inputs(T, **inputs)
    nc = build_program(T)
    res = run_bass_kernel_spmd(nc, in_maps, core_ids=list(range(NCORES)))
    out = np.empty((B, T, D), dtype=np.float32)
    for c in range(NCORES):
        b, h = c // 2, c % 2
        for j in range(NQT):
            out[b, QT * (2 * j + h):QT * (2 * j + h) + QT, :] = \
                res.results[c]["y"][QT * j:QT * (j + 1)]
    return out


# version 17 (final)
# speedup vs baseline: 3.3414x; 1.0348x over previous
import numpy as np
import ml_dtypes
import concourse.bass as bass
import concourse.mybir as mybir
from concourse.bass_utils import run_bass_kernel_spmd

F32 = mybir.dt.float32
BF16 = mybir.dt.bfloat16
FP8 = mybir.dt.float8e4
AF = mybir.ActivationFunctionType
ALU = mybir.AluOpType
BF16NP = ml_dtypes.bfloat16

B, T, D, NH, DH = 4, 4096, 1024, 16, 64
DBG = set()
NCORES = 8
MASKVAL = -30000.0
BSAFE = 1.03


def shapes(T_):
    NT = T_ // 2
    NTT = NT // 128
    NQT = max(1, NT // 512)
    QT = NT // NQT
    NKC = T_ // 128
    KCS = NKC // 2
    HP = NH // 2
    FC = D // 128
    return NT, NTT, NQT, QT, NKC, KCS, HP, FC


def causal_tables(T_):
    NT, NTT, NQT, QT, NKC, KCS, HP, FC = shapes(T_)
    KPB = QT // 128

    def qpos(h, j):
        return QT * (2 * j + h)

    def kpos(s, gg):
        return QT * (2 * (gg // KPB) + s) + 128 * (gg % KPB)

    def classify(h, j, g):
        d = qpos(h, j) - kpos(g // KCS, g % KCS)
        if d >= 127:
            return ("none", 0)
        if d + (QT - 1) < 0:
            return ("all", 0)
        return ("part", d)

    kc_lists, bias_idx, mask_idx = [], {}, {}
    bias_cls, mask_cls = {}, {}
    for j in range(NQT):
        lst = []
        for g in range(NKC):
            cl = [classify(h, j, g) for h in range(2)]
            if cl[0][0] == "all" and cl[1][0] == "all":
                continue
            lst.append(g)
            bkey = (cl[0][0] == "all", cl[1][0] == "all")
            bias_idx[(j, g)] = bias_cls.setdefault(bkey, len(bias_cls))
            if "part" in (cl[0][0], cl[1][0]):
                mkey = (cl[0][1] if cl[0][0] == "part" else None,
                        cl[1][1] if cl[1][0] == "part" else None)
                mask_idx[(j, g)] = mask_cls.setdefault(mkey, len(mask_cls))
            else:
                mask_idx[(j, g)] = None
        kc_lists.append(lst)
    n_bias = max(1, len(bias_cls))
    n_mask = max(1, len(mask_cls))
    bias_vals = [[0.0] * n_bias for _ in range(2)]
    for (a0, a1), i in bias_cls.items():
        bias_vals[0][i] = MASKVAL if a0 else 0.0
        bias_vals[1][i] = MASKVAL if a1 else 0.0
    mask_keys = [None] * n_mask
    for k, i in mask_cls.items():
        mask_keys[i] = k
    return kc_lists, bias_idx, mask_idx, n_bias, n_mask, bias_vals, mask_keys


def host_masktab(h, mask_keys, QT):
    n = len(mask_keys)
    out = np.zeros((128, n * QT), dtype=np.float32)
    for i, key in enumerate(mask_keys):
        if key is None:
            continue
        d = key[h]
        if d is None:
            continue
        p = np.arange(128)[:, None]
        c = np.arange(QT)[None, :]
        out[:, i * QT:(i + 1) * QT] = np.where(p <= d + c, 0.0, MASKVAL)
    return np.ascontiguousarray(out.astype(BF16NP))


def host_rope():
    HPn = NH // 2
    inv = (np.float32(1.0) / (np.float32(10000.0) **
           (np.arange(0, DH, 2, dtype=np.float32) / np.float32(DH))))
    t = np.arange(2048, dtype=np.float32)
    fr = (t[:, None] * inv[None, :]).astype(np.float32)
    cos = np.cos(fr).astype(np.float32)[:NH]
    sin = np.sin(fr).astype(np.float32)[:NH]
    cosv = np.zeros((128, HPn), np.float32)
    sinv2 = np.zeros((128, HPn), np.float32)
    for hp in range(HPn):
        for p in range(128):
            head = 2 * hp + p // 64
            r = p % 64
            cosv[p, hp] = cos[head][r % 32]
            sinv2[p, hp] = -sin[head][r] if r < 32 else sin[head][r - 32]
    return cosv, sinv2


ENGS = ("sync", "tensor", "scalar", "vector", "gpsimd")
ESEM = {"tensor": "PE", "scalar": "ACT", "vector": "DVE", "gpsimd": "GP"}


class Em:
    def __init__(self, nc):
        self.nc = nc
        self.ops = {e: [] for e in ENGS}
        self.sems = {}
        self.cnt = {}
        for name in ("LD", "SW", "PE", "ACT", "DVE", "GP", "CC",
                     "LW0", "LW1", "LQ", "LK", "LA0", "LA1", "LA2", "LA3"):
            self.sems[name] = nc._es.enter_context(nc.semaphore(name))
            self.cnt[name] = 0

    def wait(self, eng, sem, val):
        if val is None or val <= 0:
            return
        s = self.sems[sem]
        self.ops[eng].append(lambda e: e.wait_ge(s, val))

    def emit(self, eng, fn, inc="auto", amt=1):
        if inc == "auto":
            inc = ESEM.get(eng)
        if inc is None:
            self.ops[eng].append(fn)
            return self.cnt.get(ESEM.get(eng, "PE"), 0)
        s = self.sems[inc]
        self.cnt[inc] += amt

        def run(e, fn=fn, s=s, amt=amt):
            fn(e).then_inc(s, amt)
        self.ops[eng].append(run)
        return self.cnt[inc]

    def mm(self, fn, inc=False):
        return self.emit("tensor", fn, inc="PE" if inc else None)

    def dma(self, eng, out, in_, sem=None):
        sem = sem or ("SW" if eng in ("gpsimd", "scalar") else "LD")
        return self.emit(eng, lambda e: e.dma_start(out=out, in_=in_),
                         inc=sem, amt=16)

    def barrier(self):
        marks = dict(self.cnt)
        for eng in ENGS:
            for sem in marks:
                self.wait(eng, sem, marks[sem])

    def flush(self, block):
        for eng in ENGS:
            ops = self.ops[eng]
            getattr(block, eng)(lambda e, ops=ops: [f(e) for f in ops])


def build_program(T_, n_sub=4, prefix=None):
    NT, NTT, NQT, QT, NKC, KCS, HP, FC = shapes(T_)
    kc_lists, bias_idx, mask_idx, NB, NM, _, _ = causal_tables(T_)
    NOC = 3 * FC
    NOG = D // QT

    import contextlib
    nc = bass.Bass("TRN2", target_bir_lowering=False, debug=False)
    nc._es = contextlib.ExitStack()

    def din(name, shp, dt=F32):
        return nc.dram_tensor(name, shp, dt, kind="ExternalInput").ap()

    x_in = din("x_in", [NT, D])
    w_qkv = {"a": din("w_qkv_a", [NOC, 128, D]),
             "b": din("w_qkv_b", [NOC, 128, D])}
    b_qkv = {"a": din("b_qkv_a", [128, NOC]), "b": din("b_qkv_b", [128, NOC])}
    w_o = {"a": din("w_proj_a", [FC, 128, D]), "b": din("w_out_b", [FC, 128, D])}
    b_o = {"a": din("b_proj_a", [D]), "b": din("b_out_b", [D])}
    w_ffn_t = din("w_ffn_t", [D, D])
    b_ffn = din("b_ffn", [D])
    s_in = din("s_in", [1])
    s_w = din("s_w", [1])
    identF_in = din("identf", [128, 128])
    identB_in = din("identb", [128, 128], BF16)
    otop_in = din("otop", [128, 128], BF16)
    obot_in = din("obot", [128, 128], BF16)
    btab_in = din("btab", [128, NB])
    mtab_in = din("mtab", [128, NM * QT], BF16)
    perm_in = din("permsw", [128, 128])
    rcos_in = din("rcos", [128, HP])
    rsin_in = din("rsin", [128, HP])
    y_out = nc.dram_tensor("y", [NT, D], F32, kind="ExternalOutput").ap()

    def dsc(name, shp, dt=F32):
        return nc.dram_tensor(name, shp, dt, kind="Internal").ap()

    q_sc = dsc("q_sc", [D, NT])
    kt_own = dsc("kt_own", [D, NT])
    v_own = dsc("v_own", [HP, 128, KCS * 128])
    kmax_own = dsc("kmax_own", [128, HP])
    kt_all = dsc("kt_all", [2, D, NT])
    v_all = dsc("v_all", [2, HP, 128, KCS * 128])
    kmax_all = dsc("kmax_all", [2, 128, HP])

    sbuf = nc.alloc_sbuf_tensor
    rsb = sbuf("rsb", [128, NTT * D], F32)
    arena = sbuf("arena", [128, FC * NT], F32)
    ar = arena.ap()
    stream = sbuf("stream", [128, 8192], F32)
    st_ = stream.ap()
    st8 = stream.ap().bitcast(FP8)

    def wring(r):
        return st_[:, r * D:(r + 1) * D]

    def kb(s):
        return st_[:, s * NT:(s + 1) * NT]

    def vb(s):
        return st_[:, 2 * NT + s * NT:2 * NT + (s + 1) * NT]

    def wp(dc):
        return st_[:, dc * D:(dc + 1) * D]

    def xq8c(ic):
        return st8[:, ic * NT:(ic + 1) * NT]

    def wf8(ic):
        return st8[:, FC * NT + ic * D:FC * NT + (ic + 1) * D]

    xd = sbuf("xd", [128, D], F32)
    xn = sbuf("xn", [128, D], F32)
    scr = sbuf("scr", [128, 3 * D // 2], F32)
    sc = scr.ap()
    scr8 = scr.ap().bitcast(FP8)
    scrB = scr.ap().bitcast(BF16)
    pexp = sbuf("pexp", [128, 4 * QT], F32)

    def pex(hh, par):
        return pexp.ap()[:, (2 * hh + par) * QT:(2 * hh + par + 1) * QT]

    dsumb = sbuf("dsumb", [128, 2 * QT], F32)

    def dsum(hh):
        return dsumb.ap()[:, hh * QT:(hh + 1) * QT]

    pf = sbuf("pf", [128, D], F32)

    def qti(par):
        return pf.ap()[:, par * QT:(par + 1) * QT]
    stat = sbuf("stat", [128, 16], F32)
    sq = sbuf("sq", [128, 4 * HP], F32)
    recip = sbuf("recips", [1, 2 * QT], F32)
    repS = sbuf("repS", [128, QT], F32)
    msk = sbuf("msk", [128, NM * QT], BF16)
    btab = sbuf("btabs", [128, NB], F32)
    nbmA = sbuf("nbmA", [128, 2 * HP * NB], F32)
    negbA = sbuf("negbA", [128, 2 * HP], F32)
    qmax = sbuf("qmax", [128, HP], F32)
    kmaxs = sbuf("kmaxs", [128, HP], F32)
    kmtmp = sbuf("kmtmp", [128, 2 * HP], F32)
    dmsqA = sbuf("dmsqA", [128, 2 * HP], BF16)
    otop = sbuf("otops", [128, 128], BF16)
    obot = sbuf("obots", [128, 128], BF16)
    identF = sbuf("identFs", [128, 128], F32)
    identB = sbuf("identBs", [128, 128], BF16)
    permS = sbuf("permS", [128, 128], F32)
    rcos = sbuf("rcoss", [128, HP], F32)
    rsin = sbuf("rsins", [128, HP], F32)
    ones128 = sbuf("ones128", [128, 1], F32)
    ones64 = sbuf("ones64", [1, 64], F32)
    scvec = sbuf("scvec", [128, 4], F32)
    bqv = sbuf("bqv", [128, NOC], F32)

    pt = nc.alloc_psum_tensor
    ps_s = [[pt(f"ps_s{hh}{p}", [128, QT], F32) for p in range(2)]
            for hh in range(2)]
    ps_av = [pt(f"ps_av{p}", [128, QT], F32) for p in range(2)]
    ps_rep = pt("ps_rep", [128, QT], F32)
    ps_tr = pt("ps_tr", [128, 1024], BF16)

    em = Em(nc)

    for t in range(NTT):
        em.dma("sync", rsb.ap()[:, D * t:D * (t + 1)],
               x_in[128 * t:128 * (t + 1), :])
    for dst, src in ((identF, identF_in), (identB, identB_in),
                     (otop, otop_in), (obot, obot_in), (btab, btab_in),
                     (msk, mtab_in), (rcos, rcos_in), (rsin, rsin_in),
                     (permS, perm_in)):
        em.dma("sync", dst.ap(), src)
    em.dma("sync", scvec.ap()[:, 0:1],
           s_in.unsqueeze(0).broadcast_to([128, 1]))
    em.dma("sync", scvec.ap()[:, 1:2],
           s_w.unsqueeze(0).broadcast_to([128, 1]))
    ld0 = em.cnt["LD"]
    em.wait("vector", "LD", ld0)
    c = em.emit("vector", lambda e: e.reciprocal(
        out=scvec.ap()[:, 2:3], in_=scvec.ap()[:, 0:1]))
    em.wait("vector", "DVE", c)
    em.emit("vector", lambda e: e.tensor_mul(
        scvec.ap()[:, 3:4], scvec.ap()[:, 0:1], scvec.ap()[:, 1:2]))
    em.emit("vector", lambda e: e.memset(ones128.ap(), 1.0))
    em.emit("vector", lambda e: e.memset(ones64.ap(), 1.0))
    em.barrier()

    def ln_phase(mode, pe_gate=0, dve_gate=0):
        tr_mark = {0: 0, 1: 0}
        gp_mark = {0: 0, 1: 0}
        sq_mark = {0: 0, 1: 0}
        st_mark = {0: 0, 1: 0}
        evac_mark = em.cnt["ACT"]
        tile_marks = []
        if pe_gate or dve_gate:
            em.wait("scalar", "PE", pe_gate)
            em.wait("scalar", "DVE", dve_gate)
        for t in range(NTT):
            p = t % 2
            rt = rsb.ap()[:, D * t:D * (t + 1)]
            xdp = xd.ap() if p == 0 else pexp.ap()[:, 0:D]
            xnp = xn.ap() if p == 0 else pexp.ap()[:, D:2 * D]
            stp = stat.ap()[:, 8 * p:8 * (p + 1)]
            sqt = sc[:, 0:D] if p == 0 else dsumb.ap()[:, 0:D]
            c0 = em.emit("vector", lambda e, rt=rt, stp=stp: e.reduce_sum(
                stp[:, 0:1], rt, axis=mybir.AxisListType.X))
            em.wait("vector", "DVE", c0)
            c0b = em.emit("vector", lambda e, stp=stp: e.tensor_scalar_mul(
                stp[:, 1:2], stp[:, 0:1], -1.0 / D))
            em.wait("vector", "DVE", c0b)
            em.wait("vector", "PE", tr_mark[p])
            em.wait("vector", "GP", gp_mark[p])
            c1 = em.emit("vector", lambda e, rt=rt, xdp=xdp, stp=stp:
                         e.tensor_scalar_add(xdp, rt, stp[:, 1:2]))
            em.wait("scalar", "DVE", c1)
            em.wait("scalar", "DVE", sq_mark[p])
            c2 = em.emit("scalar", lambda e, xdp=xdp, sqt=sqt: e.activation(
                out=sqt, in_=xdp, func=AF.Square))
            em.wait("vector", "ACT", c2)
            c3 = em.emit("vector", lambda e, sqt=sqt, stp=stp: e.reduce_sum(
                stp[:, 2:3], sqt, axis=mybir.AxisListType.X))
            sq_mark[p] = c3
            em.wait("vector", "DVE", c3)
            em.wait("vector", "ACT", st_mark[p])
            c3b = em.emit("vector", lambda e, stp=stp: e.tensor_scalar(
                out=stp[:, 5:6], in0=stp[:, 2:3],
                scalar1=1.0 / D, scalar2=1e-5, op0=ALU.mult, op1=ALU.add))
            em.wait("scalar", "DVE", c3b)
            c4 = em.emit("scalar", lambda e, stp=stp: e.activation(
                out=stp[:, 3:4], in_=stp[:, 5:6], func=AF.Sqrt))
            st_mark[p] = c4
            em.wait("vector", "ACT", c4)
            c4b = em.emit("vector", lambda e, stp=stp: e.reciprocal(
                out=stp[:, 6:7], in_=stp[:, 3:4]))
            em.wait("vector", "DVE", c4b)
            c5 = em.emit("vector", lambda e, xdp=xdp, xnp=xnp, stp=stp:
                         e.tensor_scalar_mul(xnp, xdp, stp[:, 6:7]))
            if mode == "attn":
                em.wait("tensor", "DVE", c5)
                for half in range(2):
                    em.wait("tensor", "ACT", evac_mark)
                    for i in range(4):
                        f = 4 * half + i
                        em.mm(lambda e, f=f, i=i, xnp=xnp: e.transpose(
                            ps_rep.ap()[:, 128 * i:128 * (i + 1)],
                            xnp[:, 128 * f:128 * (f + 1)], identF.ap()),
                            inc=(i == 3))
                    pe_c = em.cnt["PE"]
                    em.wait("scalar", "PE", pe_c)
                    dst = ar.rearrange("p (f n) -> p f n", f=FC)[
                        :, 4 * half:4 * half + 4, 128 * t:128 * (t + 1)]
                    evac_mark = em.emit("scalar", lambda e, dst=dst: e.copy(
                        out=dst, in_=ps_rep.ap().rearrange(
                            "p (f n) -> p f n", f=4)))
                tr_mark[p] = em.cnt["PE"]
                tile_marks.append(evac_mark)
            else:
                em.wait("vector", "DVE", c5)
                c6a = em.emit("vector", lambda e, xdp=xdp, xnp=xnp:
                              e.tensor_scalar(
                                  out=xdp, in0=xnp, scalar1=scvec.ap()[:, 2:3],
                                  scalar2=448.0, op0=ALU.mult, op1=ALU.min))
                em.wait("vector", "DVE", c6a)
                q8 = scr8[:, (4 + p) * D:(5 + p) * D]
                c6 = em.emit("vector", lambda e, q8=q8, xdp=xdp:
                             e.tensor_single_scalar(out=q8, in_=xdp,
                                                    scalar=-448.0, op=ALU.max))
                em.wait("gpsimd", "DVE", c6)
                em.wait("gpsimd", "PE", tr_mark[p])
                ub = (xd.ap() if p == 0 else pexp.ap()[:, 0:D]).bitcast(
                    BF16)[:, 0:D]
                c7 = em.emit("gpsimd", lambda e, q8=q8, ub=ub:
                             e.tensor_copy(out=ub, in_=q8))
                gp_mark[p] = c7
                em.wait("tensor", "GP", c7)
                em.wait("tensor", "ACT", evac_mark)
                for f in range(FC):
                    em.mm(lambda e, ub=ub, f=f: e.transpose(
                        ps_tr.ap()[:, 128 * f:128 * (f + 1)],
                        ub[:, 128 * f:128 * (f + 1)], identB.ap()),
                        inc=(f == FC - 1))
                tr_mark[p] = em.cnt["PE"]
                em.wait("scalar", "PE", tr_mark[p])
                dst = st8[:, 0:FC * NT].rearrange("p (f n) -> p f n", f=FC)[
                    :, :, 128 * t:128 * (t + 1)]
                evac_mark = em.emit("scalar", lambda e, dst=dst: e.copy(
                    out=dst, in_=ps_tr.ap().rearrange(
                        "p (f n) -> p f n", f=FC)))
                tile_marks.append(evac_mark)

        return tile_marks

    def qkv_phase(bk, oc_sel=None, ln_marks=None, bank_dve0=0):
        rope = bk == "b"
        ldb = em.dma("sync", bqv.ap()[:, 0:NOC], b_qkv[bk])
        em.wait("vector", "LD", ldb)
        zz = em.emit("vector", lambda e: e.memset(qmax.ap(), 0.0))
        zz = em.emit("vector", lambda e: e.memset(kmaxs.ap(), 0.0))
        em.wait("vector", "DVE", zz)
        prev_mm = {}
        evac_at = {}
        rope_done = {}
        oc_list = (list(range(FC, 3 * FC)) + list(range(FC))
                   if oc_sel is None else list(oc_sel))
        WSEM = ("LW0", "LW1", "LA1", "LA2")
        for oci, oc in enumerate(oc_list):
            ring = oci % 4
            if oci >= 4:
                em.wait("sync", "PE", prev_mm[oc_list[oci - 4]])
            lw = em.dma("sync", wring(ring), w_qkv[bk][oc], sem=WSEM[ring])
            em.wait("tensor", WSEM[ring], lw)
            is_v = oc >= 2 * FC
            hp = oc % FC
            for tt in range(NT // QT):
                par = tt % 2
                bank = ps_s[0][par]
                if ln_marks is not None:
                    em.wait("tensor", "ACT", ln_marks[4 * tt + 3])
                em.wait("tensor", "DVE", evac_at.get(("m", par), bank_dve0))
                for f in range(FC):
                    mv = ar[:, NT * f + QT * tt:NT * f + QT * (tt + 1)]
                    em.mm(lambda e, b=bank, f=f, mv=mv, ring=ring: e.matmul(
                        b.ap(), wring(ring)[:, 128 * f:128 * (f + 1)], mv,
                        start=(f == 0), stop=(f == FC - 1)),
                        inc=(f == FC - 1))
                pe_mm = em.cnt["PE"]
                prev_mm[oc] = pe_mm
                em.wait("vector", "PE", pe_mm)
                em.wait("vector", "SW", em.cnt["SW"])
                a1 = em.emit("vector", lambda e, b=bank, oc=oc, par=par:
                             e.tensor_scalar_add(sc[:, QT * par:QT * (par + 1)],
                                                 b.ap(),
                                                 bqv.ap()[:, oc:oc + 1]))
                evac_at[("m", par)] = a1
                src_f32 = sc[:, QT * par:QT * (par + 1)]
                dep = a1
                if rope and not is_v:
                    rb = ps_s[1][par]
                    em.wait("tensor", "DVE", dep)
                    em.wait("tensor", "DVE", rope_done.get(par, 0))
                    em.mm(lambda e, rb=rb, s=src_f32: e.matmul(
                        rb.ap(), permS.ap(), s, start=True, stop=True),
                        inc=True)
                    rmm = em.cnt["PE"]
                    em.wait("vector", "PE", rmm)
                    r0 = em.emit("vector", lambda e, rb=rb, hp=hp:
                                 e.tensor_scalar_mul(
                                     xd.ap()[:, 0:QT], rb.ap(),
                                     rsin.ap()[:, hp:hp + 1]))
                    rope_done[par] = r0
                    em.wait("vector", "DVE", r0)
                    r1 = em.emit("vector", lambda e, s=src_f32, hp=hp:
                                 e.scalar_tensor_tensor(
                                     out=xn.ap()[:, 0:QT], in0=s,
                                     scalar=rcos.ap()[:, hp:hp + 1],
                                     in1=xd.ap()[:, 0:QT],
                                     op0=ALU.mult, op1=ALU.add))
                    src_f32 = xn.ap()[:, 0:QT]
                    dep = r1
                if not is_v:
                    em.wait("vector", "DVE", dep)
                    m1 = em.emit("vector", lambda e, s=src_f32: e.reduce_max(
                        stat.ap()[:, 8:9], s, axis=mybir.AxisListType.X,
                        apply_absolute_value=True))
                    tgt = qmax if oc < FC else kmaxs
                    em.wait("vector", "DVE", m1)
                    m2 = em.emit("vector", lambda e, tgt=tgt, hp=hp:
                                 e.tensor_tensor(
                                     out=tgt.ap()[:, hp:hp + 1],
                                     in0=tgt.ap()[:, hp:hp + 1],
                                     in1=stat.ap()[:, 8:9], op=ALU.max))
                    dst_t = q_sc if oc < FC else kt_own
                    r0c = 128 * hp
                    em.wait("scalar", "DVE", dep)
                    em.dma("scalar",
                           dst_t[r0c:r0c + 128, QT * tt:QT * (tt + 1)],
                           src_f32)
                else:
                    em.wait("tensor", "DVE", dep)
                    em.wait("tensor", "ACT", em.cnt["ACT"])
                    for i in range(QT // 128):
                        em.mm(lambda e, s=src_f32, i=i: e.transpose(
                            ps_rep.ap()[:, 128 * i:128 * (i + 1)],
                            s[:, 128 * i:128 * (i + 1)], identF.ap()),
                            inc=(i == QT // 128 - 1))
                    pe_tr = em.cnt["PE"]
                    em.wait("scalar", "PE", pe_tr)
                    em.wait("scalar", "SW", em.cnt["SW"])
                    ev = em.emit("scalar", lambda e: e.copy(
                        out=sc[:, D:D + QT], in_=ps_rep.ap()))
                    em.wait("scalar", "ACT", ev)
                    hpv = oc - 2 * FC
                    em.dma("scalar",
                           v_own[hpv][:, QT * tt:QT * (tt + 1)],
                           sc[:, D:D + QT])
                sw_now = em.cnt["SW"]
                em.wait("vector", "SW", sw_now)
                em.wait("scalar", "SW", sw_now)
            if oc_sel is None and oc == 2 * FC - 1 and "latecc" not in DBG:
                em.wait("scalar", "DVE", em.cnt["DVE"])
                em.dma("scalar", kmax_own, kmaxs.ap()[:, 0:HP])
                cc_start(((kt_own, kt_all), (kmax_own, kmax_all)))
            if oc_sel is None and oc == 3 * FC - 1 and "latecc" not in DBG:
                cc_start(((v_own, v_all),), sem="LA3")
        if oc_sel is None and "latecc" in DBG:
            em.wait("scalar", "DVE", em.cnt["DVE"])
            em.dma("scalar", kmax_own, kmaxs.ap()[:, 0:HP])
            cc_start(((kt_own, kt_all), (kmax_own, kmax_all)))
            cc_start(((v_own, v_all),), sem="LA3")

    ccm = {"k": 0, "v": 0}

    def cc_start(pairs, sem="CC"):
        em.wait("gpsimd", "SW", em.cnt["SW"])
        rg = [[0, 1], [2, 3], [4, 5], [6, 7]]
        for i_, o_ in pairs:
            em.emit("gpsimd", lambda e, i_=i_, o_=o_: e.collective_compute(
                "AllGather", ALU.bypass, replica_groups=rg,
                ins=[i_.tensor.ap().opt()], outs=[o_.tensor.ap().opt()]),
                inc=sem)
        ccm["k" if sem == "CC" else "v"] = em.cnt[sem]

    def collective_phase():
        cc = ccm["k"]
        em.wait("sync", "CC", cc)
        l1 = em.dma("sync", kmtmp.ap()[:, 0:HP], kmax_all[0])
        l1 = em.dma("sync", kmtmp.ap()[:, HP:2 * HP], kmax_all[1])
        em.wait("vector", "LD", l1)
        em.wait("vector", "CC", cc)
        c = em.emit("vector", lambda e: e.tensor_max(
            kmaxs.ap(), kmtmp.ap()[:, 0:HP], kmtmp.ap()[:, HP:2 * HP]))
        em.wait("vector", "DVE", c)
        c = em.emit("vector", lambda e: e.tensor_mul(
            qmax.ap(), qmax.ap(), qmax.ap()))
        em.wait("vector", "DVE", c)
        c = em.emit("vector", lambda e: e.tensor_mul(
            kmaxs.ap(), kmaxs.ap(), kmaxs.ap()))
        em.wait("vector", "DVE", c)

    def attn_phase(bk):
        causal = bk == "a"
        em.wait("sync", "SW", em.cnt["SW"])
        em.wait("sync", "DVE", em.cnt["DVE"])
        c = em.emit("vector", lambda e: e.tensor_copy(
            out=dmsqA.ap()[:, 0:HP], in_=qmax.ap()))
        c = em.emit("vector", lambda e: e.tensor_copy(
            out=dmsqA.ap()[:, HP:2 * HP], in_=kmaxs.ap()))
        em.wait("tensor", "DVE", c)
        bank0 = ps_s[0][0]
        for hp in range(HP):
            for hh in range(2):
                stt = otop if hh == 0 else obot
                col = 4 * hp + 2 * hh
                em.mm(lambda e, stt=stt, col=col, hp=hp: e.matmul(
                    bank0.ap()[:, col:col + 1], stt.ap(),
                    dmsqA.ap()[:, hp:hp + 1], start=True, stop=True))
                em.mm(lambda e, stt=stt, col=col, hp=hp: e.matmul(
                    bank0.ap()[:, col + 1:col + 2], stt.ap(),
                    dmsqA.ap()[:, HP + hp:HP + hp + 1],
                    start=True, stop=True),
                    inc=(hp == HP - 1 and hh == 1))
        pe_b = em.cnt["PE"]
        em.wait("scalar", "PE", pe_b)
        aq = em.emit("scalar", lambda e: e.activation(
            out=sq.ap(), in_=bank0.ap()[:, 0:4 * HP], func=AF.Sqrt))
        em.wait("vector", "ACT", aq)
        sqv = sq.ap().rearrange("p (i two) -> p i two", two=2)
        nb = em.emit("vector", lambda e: e.tensor_tensor(
            out=negbA.ap(), in0=sqv[:, :, 0], in1=sqv[:, :, 1], op=ALU.mult))
        em.wait("vector", "DVE", nb)
        nb = em.emit("vector", lambda e: e.tensor_scalar_mul(
            negbA.ap(), negbA.ap(), -BSAFE / 8.0))
        if causal:
            em.wait("vector", "DVE", nb)
            for i in range(2 * HP):
                nb = em.emit("vector", lambda e, i=i: e.tensor_scalar_add(
                    nbmA.ap()[:, i * NB:(i + 1) * NB], btab.ap(),
                    negbA.ap()[:, i:i + 1]))
        em.wait("scalar", "DVE", nb)
        em.wait("tensor", "ACT", em.cnt["ACT"])

        prev_last = em.cnt["PE"]
        qld = {}
        avfree = {}
        exp_mark = {0: 0, 1: 0}
        av_mark = {0: 0, 1: 0}
        ds_mark = {0: 0, 1: 0}
        kv_sem = ("LK", "LA0")
        for hp in range(HP):
            em.wait("sync", "PE", prev_last)
            if hp == 0:
                em.wait("sync", "CC", ccm["k"])
                em.wait("sync", "LA3", ccm["v"])
            kvm = {}
            for s in range(2):
                em.dma("sync", vb(s), v_all[s, hp], sem=kv_sem[s])
                kvm[s] = em.dma("sync", kb(s),
                                kt_all[s, 128 * hp:128 * (hp + 1), :],
                                sem=kv_sem[s])
            for qt in range(NQT):
                parq = qt % 2
                if (hp, qt) not in qld:
                    lq = em.dma("sync", qti(parq),
                                q_sc[128 * hp:128 * (hp + 1),
                                     QT * qt:QT * (qt + 1)], sem="LQ")
                    qld[(hp, qt)] = lq
                em.wait("tensor", "LQ", qld[(hp, qt)])
                klist = kc_lists[qt] if causal else list(range(NKC))
                nkl = len(klist)

                def emit_av(idx, s, gg, par):
                    em.wait("tensor", "ACT", exp_mark[par])
                    if idx == 0:
                        em.wait("tensor", "DVE", avfree.get(parq, 0))
                    first, last = idx == 0, idx == nkl - 1
                    for hh in range(2):
                        vs = vb(s)[:, 128 * gg + 64 * hh:
                                   128 * gg + 64 * hh + 64]
                        em.mm(lambda e, vs=vs, hh=hh, par=par, pq=parq,
                              fi=first, la=last: e.matmul(
                                  ps_av[pq].ap()[64 * hh:64 * (hh + 1), :],
                                  vs, pex(hh, par), start=fi, stop=la,
                                  tile_position=(0, 64 * hh)),
                              inc=(hh == 1))
                    av_mark[par] = em.cnt["PE"]

                pend = None
                src_waited = set()
                for idx, g in enumerate(klist):
                    par = idx % 2
                    s, gg = g // KCS, g % KCS
                    if s not in src_waited:
                        em.wait("tensor", kv_sem[s], kvm[s])
                        src_waited.add(s)
                    em.wait("tensor", "ACT", exp_mark[par])
                    for hh in range(2):
                        base = 64 * hh
                        o = ps_s[hh][par]
                        stt = kb(s)[base:base + 64, 128 * gg:128 * (gg + 1)]
                        mv = qti(parq)[base:base + 64, 0:QT]
                        em.mm(lambda e, o=o, stt=stt, mv=mv, bb=base: e.matmul(
                            o.ap(), stt, mv, start=True, stop=True,
                            tile_position=(bb, 0)), inc=(hh == 1))
                    qk_pe = em.cnt["PE"]
                    if pend is not None and "noshift" not in DBG:
                        emit_av(*pend)
                    if "noshift" not in DBG:
                        pend = (idx, s, gg, par)
                    mi = mask_idx.get((qt, g)) if causal else None
                    mask_dve = None
                    if mi is not None:
                        em.wait("vector", "PE", qk_pe)
                        for hh in range(2):
                            mask_dve = em.emit(
                                "vector", lambda e, hh=hh, par=par, mi=mi:
                                e.tensor_add(
                                    ps_s[hh][par].ap(), ps_s[hh][par].ap(),
                                    msk.ap()[:, QT * mi:QT * (mi + 1)]))
                    em.wait("scalar", "PE", qk_pe)
                    if mask_dve is not None:
                        em.wait("scalar", "DVE", mask_dve)
                    em.wait("scalar", "PE", av_mark[par])
                    em.wait("scalar", "DVE", ds_mark[par])
                    for hh in range(2):
                        if causal:
                            bi = bias_idx[(qt, g)]
                            bias_ap = nbmA.ap()[:, (2 * hp + hh) * NB + bi:
                                                (2 * hp + hh) * NB + bi + 1]
                        else:
                            bias_ap = negbA.ap()[:, 2 * hp + hh:
                                                 2 * hp + hh + 1]
                        em.emit("scalar",
                                lambda e, hh=hh, par=par, bi_=bias_ap:
                                e.activation(
                                    out=pex(hh, par),
                                    in_=ps_s[hh][par].ap(), func=AF.Exp,
                                    bias=bi_, scale=0.125),
                                inc="ACT" if hh == 1 else None)
                    exp_mark[par] = em.cnt["ACT"]
                    em.wait("vector", "ACT", exp_mark[par])
                    for hh in range(2):
                        if idx == 0:
                            dd = em.emit("vector", lambda e, hh=hh, par=par:
                                         e.tensor_copy(out=dsum(hh),
                                                       in_=pex(hh, par)))
                        else:
                            dd = em.emit("vector", lambda e, hh=hh, par=par:
                                         e.tensor_add(dsum(hh), dsum(hh),
                                                      pex(hh, par)))
                    ds_mark[par] = dd
                    if "noshift" in DBG:
                        emit_av(idx, s, gg, par)
                if "noshift" not in DBG:
                    emit_av(*pend)
                last_pe = em.cnt["PE"]
                prev_last = last_pe
                nhp, nqt = (hp, qt + 1) if qt + 1 < NQT else (hp + 1, 0)
                if nhp < HP:
                    em.wait("sync", "PE", last_pe)
                    qld[(nhp, nqt)] = em.dma(
                        "sync", qti(nqt % 2),
                        q_sc[128 * nhp:128 * (nhp + 1),
                             QT * nqt:QT * (nqt + 1)], sem="LQ")
                em.wait("tensor", "DVE", ds_mark[(len(klist) - 1) % 2])
                em.wait("tensor", "ACT", em.cnt["ACT"])
                for hh in range(2):
                    em.mm(lambda e, hh=hh: e.matmul(
                        ps_rep.ap()[32 * hh:32 * hh + 1, :], ones128.ap(),
                        dsum(hh), start=True, stop=True,
                        tile_position=(0, 32 * hh)), inc=(hh == 1))
                dn_pe = em.cnt["PE"]
                em.wait("vector", "PE", dn_pe)
                rc = em.emit("vector", lambda e: e.reciprocal(
                    out=recip.ap()[:, 0:QT], in_=ps_rep.ap()[0:1, :]))
                rc = em.emit("vector", lambda e: e.reciprocal(
                    out=recip.ap()[:, QT:2 * QT], in_=ps_rep.ap()[32:33, :]))
                em.wait("tensor", "DVE", rc)
                for hh in range(2):
                    em.mm(lambda e, hh=hh: e.matmul(
                        ps_rep.ap()[64 * hh:64 * (hh + 1), :], ones64.ap(),
                        recip.ap()[:, QT * hh:QT * (hh + 1)],
                        start=True, stop=True,
                        tile_position=(0, 64 * hh)), inc=(hh == 1))
                rp_pe = em.cnt["PE"]
                em.wait("scalar", "PE", rp_pe)
                gv = em.emit("scalar", lambda e: e.copy(
                    out=repS.ap(), in_=ps_rep.ap()))
                em.wait("vector", "ACT", gv)
                dst = ar[:, NT * hp + QT * qt:NT * hp + QT * (qt + 1)]
                mu = em.emit("vector", lambda e, dst=dst, parq=parq:
                             e.tensor_tensor(out=dst, in0=ps_av[parq].ap(),
                                             in1=repS.ap(), op=ALU.mult))
                avfree[parq] = mu

    def matout_phase(bk):
        lw = 0
        for dc in range(FC):
            lw = em.dma("sync", wp(dc), w_o[bk][dc], sem="LA1")
        ldb = em.dma("sync", pf.ap(),
                     b_o[bk].unsqueeze(0).broadcast_to([128, D]))
        em.wait("tensor", "LA1", lw)
        evac_at = {}
        for tc in range(NTT):
            for og in range(NOG):
                bank = ps_s[0][og]
                em.wait("tensor", "DVE", evac_at.get(og, 0))
                for dc in range(FC):
                    a_st = ar[:, NT * dc + 128 * tc:NT * dc + 128 * (tc + 1)]
                    wv = wp(dc)[:, QT * og:QT * (og + 1)]
                    em.mm(lambda e, b=bank, a=a_st, w=wv, dc=dc: e.matmul(
                        b.ap(), a, w, start=(dc == 0), stop=(dc == FC - 1)),
                        inc=(dc == FC - 1))
                pe_mm = em.cnt["PE"]
                em.wait("vector", "PE", pe_mm)
                em.wait("vector", "LD", ldb)
                c1 = em.emit("vector", lambda e, b=bank, og=og: e.tensor_add(
                    xd.ap()[:, 0:QT], b.ap(),
                    pf.ap()[:, QT * og:QT * (og + 1)]))
                em.wait("vector", "DVE", c1)
                ra = rsb.ap()[:, D * tc + QT * og:D * tc + QT * (og + 1)]
                ev = em.emit("vector", lambda e, ra=ra: e.tensor_add(
                    ra, ra, xd.ap()[:, 0:QT]))
                evac_at[og] = ev

    def ffn_stage():
        lw = 0
        for ic in range(FC):
            lw = em.dma("sync", ar[:, D * ic:D * (ic + 1)],
                        w_ffn_t[128 * ic:128 * (ic + 1), :], sem="LA2")
        em.wait("gpsimd", "LA2", lw)
        em.wait("gpsimd", "PE", em.cnt["PE"])
        wc = 0
        for ic in range(FC):
            wc = em.emit("gpsimd", lambda e, ic=ic: e.tensor_copy(
                out=wf8(ic), in_=ar[:, D * ic:D * (ic + 1)]))
        return wc

    def ffn_phase(qmarks=None, wc=0):
        ldb = em.dma("sync", pf.ap(), b_ffn.unsqueeze(0).broadcast_to([128, D]))
        evac_at = {}
        for tc in range(NTT):
            for og in range(NOG):
                bank = ps_s[0][og]
                if qmarks is not None and og == 0:
                    em.wait("tensor", "ACT", qmarks[tc])
                    em.wait("tensor", "GP", wc)
                em.wait("tensor", "DVE", evac_at.get(og, 0))
                for ic in range(FC):
                    xa = xq8c(ic)[:, 128 * tc:128 * (tc + 1)]
                    wv = wf8(ic)[:, QT * og:QT * (og + 1)]
                    em.mm(lambda e, b=bank, x=xa, w=wv, ic=ic: e.matmul(
                        b.ap(), x, w, start=(ic == 0), stop=(ic == FC - 1)),
                        inc=(ic == FC - 1))
                pe_mm = em.cnt["PE"]
                em.wait("vector", "PE", pe_mm)
                em.wait("vector", "LD", ldb)
                c1 = em.emit("vector", lambda e, b=bank, og=og:
                             e.scalar_tensor_tensor(
                                 out=xd.ap()[:, 0:QT], in0=b.ap(),
                                 scalar=scvec.ap()[:, 3:4],
                                 in1=pf.ap()[:, QT * og:QT * (og + 1)],
                                 op0=ALU.mult, op1=ALU.add))
                em.wait("vector", "DVE", c1)
                ra = rsb.ap()[:, D * tc + QT * og:D * tc + QT * (og + 1)]
                ev = em.emit("vector", lambda e, ra=ra: e.tensor_add(
                    ra, ra, xd.ap()[:, 0:QT]))
                evac_at[og] = ev

    def attn_sublayer(bk):
        dve0 = em.cnt["DVE"]
        act0 = em.cnt["ACT"]
        marks = ln_phase("attn", dve_gate=dve0)
        em.wait("tensor", "ACT", act0)
        qkv_phase(bk, ln_marks=marks, bank_dve0=dve0)
        collective_phase()
        attn_phase(bk)
        em.barrier()
        matout_phase(bk)

    def ffn_sublayer():
        pe0 = em.cnt["PE"]
        em.wait("sync", "PE", pe0)
        wc = ffn_stage()
        qmarks = ln_phase("quant", pe_gate=pe0)
        ffn_phase(qmarks, wc)
        em.barrier()

    if prefix in ("lnq8", "ffnq8"):
        if prefix == "ffnq8":
            attn_sublayer("a")
        ffn_stage()
        ln_phase("quant")
        em.barrier()
        for j in range(6):
            em.dma("scalar", y_out[128 * j:128 * (j + 1), :],
                   st_[:, 1024 * j:1024 * (j + 1)])
        em.wait("scalar", "SW", em.cnt["SW"])
        em.wait("gpsimd", "SW", em.cnt["SW"])
        with nc.Block() as block:
            em.flush(block)
        nc._es.close()
        return nc
    if prefix is not None:
        ln_phase("attn")
        em.barrier()
        if prefix == "lnxT":
            for j in range(NTT):
                em.dma("scalar", y_out[128 * j:128 * (j + 1), :],
                       ar[:, 1024 * j:1024 * (j + 1)])
        if prefix == "ln":
            pass
        if prefix in ("qkv", "coll", "attn", "matout", "attnA"):
            qkv_phase("a")
            em.barrier()
        if prefix in ("coll", "attn", "matout", "attnA"):
            collective_phase()
            em.barrier()
        if prefix in ("attn", "matout", "attnA"):
            attn_phase("a")
            em.barrier()
        if prefix == "attnA":
            for j in range(NTT):
                em.dma("scalar", y_out[128 * j:128 * (j + 1), :],
                       ar[:, 1024 * j:1024 * (j + 1)])
        if prefix == "matout":
            matout_phase("a")
            em.barrier()
    else:
        subs = [lambda: attn_sublayer("a"), ffn_sublayer,
                lambda: attn_sublayer("b"), ffn_sublayer]
        for i in range(n_sub):
            subs[i]()
    if prefix not in ("lnxT", "attnA"):
        em.barrier()
        for t in range(NTT):
            em.dma("scalar", y_out[128 * t:128 * (t + 1), :],
                   rsb.ap()[:, D * t:D * (t + 1)])
    em.wait("scalar", "SW", em.cnt["SW"])
    em.wait("gpsimd", "SW", em.cnt["SW"])

    with nc.Block() as block:
        em.flush(block)
    nc._es.close()
    return nc


def _prep_inputs(T_, L0_attn_a_x, L0_attn_a_qkv_w, L0_attn_a_qkv_b,
                 L0_attn_a_proj_w, L0_attn_a_proj_b, L0_attn_b_qkv_w,
                 L0_attn_b_qkv_b, L0_attn_b_out_w, L0_attn_b_out_b,
                 L0_ffn_weight_fp8, L0_ffn_bias, L0_ffn_input_scale,
                 L0_ffn_weight_scale):
    NT, NTT, NQT, QT, NKC, KCS, HP, FC = shapes(T_)
    _, _, _, NB, NM, bias_vals, mask_keys = causal_tables(T_)
    x = np.ascontiguousarray(np.asarray(L0_attn_a_x, dtype=np.float32))
    wb = np.asarray(L0_attn_b_qkv_w, dtype=np.float32)
    bb = np.asarray(L0_attn_b_qkv_b, dtype=np.float32)
    perm = np.zeros(3 * D, dtype=np.int64)
    for h in range(NH):
        for j in range(DH):
            perm[h * DH + j] = h * 3 * DH + j
            perm[D + h * DH + j] = h * 3 * DH + DH + j
            perm[2 * D + h * DH + j] = h * 3 * DH + 2 * DH + j
    wb_g = np.ascontiguousarray(wb[:, perm])
    bb_g = np.ascontiguousarray(bb[perm])

    def lay_qkv(w):
        return np.ascontiguousarray(
            w.reshape(FC, 128, 3 * FC, 128).transpose(2, 1, 0, 3)
            .reshape(3 * FC, 128, D))

    def lay_o(w):
        return np.ascontiguousarray(
            np.asarray(w, dtype=np.float32).reshape(FC, 128, D))

    w_ffn_t = np.ascontiguousarray(
        np.asarray(L0_ffn_weight_fp8, dtype=np.float32).T)
    identF = np.eye(128, dtype=np.float32)
    identB = identF.astype(BF16NP)
    otop = np.zeros((128, 128), np.float32)
    otop[0:64, :] = 1.0
    obot = np.zeros((128, 128), np.float32)
    obot[64:128, :] = 1.0
    rcos, rsin = host_rope()
    permsw = np.zeros((128, 128), np.float32)
    for p in range(128):
        partner = p + 32 if (p % 64) < 32 else p - 32
        permsw[partner, p] = 1.0
    nb = x.shape[0]
    wq_a = lay_qkv(np.asarray(L0_attn_a_qkv_w, dtype=np.float32))
    wq_b = lay_qkv(wb_g)
    in_maps = []
    for c in range(2 * nb):
        b, h = c // 2, c % 2
        btabv = np.tile(np.asarray(bias_vals[h], np.float32)[None, :],
                        (128, 1))
        in_maps.append(dict(
            x_in=np.ascontiguousarray(np.concatenate(
                [x[b, QT * (2 * j + h):QT * (2 * j + h) + QT, :]
                 for j in range(NQT)], axis=0)),
            w_qkv_a=wq_a,
            b_qkv_a=np.ascontiguousarray(np.asarray(
                L0_attn_a_qkv_b, dtype=np.float32).reshape(-1, 128).T),
            w_proj_a=lay_o(L0_attn_a_proj_w),
            b_proj_a=np.asarray(L0_attn_a_proj_b, dtype=np.float32),
            w_qkv_b=wq_b,
            b_qkv_b=np.ascontiguousarray(bb_g.reshape(-1, 128).T),
            w_out_b=lay_o(L0_attn_b_out_w),
            b_out_b=np.asarray(L0_attn_b_out_b, dtype=np.float32),
            w_ffn_t=w_ffn_t,
            b_ffn=np.asarray(L0_ffn_bias, dtype=np.float32),
            s_in=np.asarray(L0_ffn_input_scale, dtype=np.float32),
            s_w=np.asarray(L0_ffn_weight_scale, dtype=np.float32),
            identf=identF, identb=identB,
            otop=otop.astype(BF16NP), obot=obot.astype(BF16NP),
            btab=btabv, mtab=host_masktab(h, mask_keys, QT),
            rcos=rcos, rsin=rsin, permsw=permsw,
        ))
    return in_maps


def kernel(**inputs):
    NT, NTT, NQT, QT = shapes(T)[:4]
    in_maps = _prep_inputs(T, **inputs)
    nc = build_program(T)
    res = run_bass_kernel_spmd(nc, in_maps, core_ids=list(range(NCORES)))
    out = np.empty((B, T, D), dtype=np.float32)
    for c in range(NCORES):
        b, h = c // 2, c % 2
        for j in range(NQT):
            out[b, QT * (2 * j + h):QT * (2 * j + h) + QT, :] = \
                res.results[c]["y"][QT * j:QT * (j + 1)]
    return out
